# version 13
# baseline (speedup 1.0000x reference)
import numpy as np
import ml_dtypes
import concourse.bass as bass
import concourse.mybir as mybir
from concourse.bass_utils import run_bass_kernel_spmd

F32 = mybir.dt.float32
BF16 = mybir.dt.bfloat16
U32 = mybir.dt.uint32
U8 = mybir.dt.uint8
AF = mybir.ActivationFunctionType
ALU = mybir.AluOpType
AX = mybir.AxisListType
IOA = bass.IndirectOffsetOnAxis

BPC = 4
S = 2048
D = 1024
NCH = 8
NTT = S // 128
NQS = S // 512
H = 16
DH = 64
NEG = -1.0e30
LN_EPS = 1e-5

P4_DEBUG = False

ENGINES = ("tensor", "vector", "scalar", "gpsimd", "sync")


class Sem:
    def __init__(self, nc, name):
        self.h = nc.alloc_semaphore(name)
        self.n = 0
        self.name = name


class Ctx:
    def __init__(self, nc):
        self.nc = nc
        self.q = {e: [] for e in ENGINES}
        self.pe = Sem(nc, "s_pe")
        self.act = Sem(nc, "s_act")
        self.dve = Sem(nc, "s_dve")
        self.pool = Sem(nc, "s_pool")
        self.ld = Sem(nc, "s_ld")
        self.ld2 = Sem(nc, "s_ld2")
        self.sw = Sem(nc, "s_sw")
        self.esem = {"tensor": self.pe, "vector": self.dve,
                     "scalar": self.act, "gpsimd": self.pool}
        self.dsem = {"sync": self.ld, "scalar": self.ld2, "gpsimd": self.sw}
        self.dma_sems = {}
        self.satisfied = {e: {} for e in ENGINES}
        self.n_ops = 0

    def _wait(self, eng_name, sem, cnt):
        if eng_name == "tensor" and sem is self.pe:
            return
        if cnt is None:
            cnt = sem.n
        if cnt <= 0:
            return
        sat = self.satisfied[eng_name]
        if sat.get(sem.name, 0) >= cnt:
            return
        sat[sem.name] = cnt
        h = sem.h
        self.q[eng_name].append(lambda e: e.wait_ge(h, cnt))
        self.n_ops += 1

    def op(self, eng_name, build, reads=(), writes=(), inc=1):
        sem = self.esem[eng_name]
        for res in reads:
            if res.w is not None:
                self._wait(eng_name, *res.w)
        for res in writes:
            if res.w is not None:
                self._wait(eng_name, *res.w)
            for s, n in res.readers:
                self._wait(eng_name, s, n)
        sem.n += inc
        cnt, k = sem.n, inc

        def f(e, build=build, k=k, h=sem.h):
            build(e).then_inc(h, k)
        self.q[eng_name].append(f)
        self.n_ops += 1
        for res in writes:
            res.w = (sem, cnt)
            res.readers = []
        for res in reads:
            res.readers.append((sem, cnt))
        return cnt

    def dma(self, queue, build, sem_name="d", reads=(), writes=(),
            nowaw=False):
        sem = self.dma_sems.get(sem_name)
        if sem is None:
            sem = Sem(self.nc, "d_" + sem_name)
            self.dma_sems[sem_name] = sem
        for res in reads:
            if res.w is not None:
                self._wait(queue, *res.w)
        for res in writes:
            if not nowaw and res.w is not None:
                self._wait(queue, *res.w)
            for s, n in res.readers:
                self._wait(queue, s, n)
        sem.n += 16
        cnt = sem.n

        def f(e, build=build, h=sem.h):
            build(e).then_inc(h, 16)
        self.q[queue].append(f)
        self.n_ops += 1
        for res in writes:
            res.w = (sem, cnt)
            res.readers = []
        for res in reads:
            res.readers.append((sem, cnt))
        return cnt

    def raw(self, eng_name, fn):
        self.q[eng_name].append(fn)

    def barrier(self):
        sems = [self.pe, self.act, self.dve, self.pool] + \
            list(self.dma_sems.values())
        for e in ENGINES:
            for s in sems:
                self._wait(e, s, s.n)

    def replay(self):
        nc = self.nc
        with nc.Block() as block:
            qs = self.q

            @block.sync
            def _(e):
                for f in qs["sync"]:
                    f(e)

            @block.tensor
            def _(e):
                for f in qs["tensor"]:
                    f(e)

            @block.vector
            def _(e):
                for f in qs["vector"]:
                    f(e)

            @block.scalar
            def _(e):
                for f in qs["scalar"]:
                    f(e)

            @block.gpsimd
            def _(e):
                for f in qs["gpsimd"]:
                    f(e)


class Res:

    __slots__ = ("w", "readers", "name")

    def __init__(self, name=""):
        self.w = None
        self.readers = []
        self.name = name


class Arena:
    def __init__(self, nc, name, dtype, free_elems):
        self.t = nc.alloc_sbuf_tensor(name, [128, free_elems], dtype)
        self.cap = free_elems
        self.off = 0
        self.marks = []

    def take(self, n):
        assert self.off + n <= self.cap, \
            f"arena overflow {self.off}+{n}>{self.cap}"
        o = self.off
        self.off += n
        return o

    def view(self, off, a, b):
        return self.t[:, off + a: off + b]

    def take_f(self, n_f32):
        o = self.take(2 * n_f32)
        assert o % 2 == 0
        return o

    def viewf(self, off, a, b):
        return self.t[:, off + 2 * a: off + 2 * b].bitcast(F32)

    def viewu(self, off, a, b):
        return self.t[:, off + 2 * a: off + 2 * b].bitcast(U32)

    def mark(self):
        self.marks.append(self.off)

    def release(self):
        self.off = self.marks.pop()



def host_rope_tables():
    inv = 1.0 / (10000.0 ** (np.arange(0, DH, 2, dtype=np.float64) / DH))
    pos = np.arange(S, dtype=np.float64)
    fr = np.outer(inv, pos)
    c = np.ones((128, S), np.float32)
    s = np.zeros((128, S), np.float32)
    for base in (0, 64):
        c[base:base + 32] = np.cos(fr)
        s[base:base + 32] = np.sin(fr)
    return c.astype(ml_dtypes.bfloat16), s.astype(ml_dtypes.bfloat16)




def bass_sb_rows(ap, base, count):
    return ap[base:base + count, :]


class Assets:
    pass


def nt(nc, name, p, f, dt):
    return nc.alloc_sbuf_tensor(name, [p, f], dt)


def build_common(C, uniq=""):
    nc = C.nc
    A = Assets()
    A.pbs = [nc.alloc_psum_tensor(f"pbs{i}{uniq}", [128, 1024], F32)
             for i in range(2)]
    A.rpbs = [Res("pbs0"), Res("pbs1")]
    _p4 = nc.alloc_psum_tensor(f"pb4{uniq}", [128, 512], F32)
    _p5 = nc.alloc_psum_tensor(f"pb5{uniq}", [128, 512], F32)
    A.pb = [A.pbs[0][:, 0:512], A.pbs[0][:, 512:1024],
            A.pbs[1][:, 0:512], A.pbs[1][:, 512:1024], _p4, _p5]
    A.rpb = [A.rpbs[0], A.rpbs[0], A.rpbs[1], A.rpbs[1], Res("pb4"),
             Res("pb5")]
    A.pbt = nc.alloc_psum_tensor(f"pbt{uniq}", [128, 1024], BF16)
    A.pb6 = nc.alloc_psum_tensor(f"pb6{uniq}", [128, 512], F32)
    A.rpbt = Res("pbt")
    A.rpb6 = Res("pb6")

    A.ident = nt(nc, "ident" + uniq, 128, 128, BF16)
    A.identf = nt(nc, "identf" + uniq, 128, 128, F32)
    A.masks2 = nt(nc, "masks2" + uniq, 128, 4 * 512, BF16)
    A.onecolA = nt(nc, "onecolA" + uniq, 1, 128, BF16)
    A.onecolB = nt(nc, "onecolB" + uniq, 1, 128, BF16)
    C.ab.mark()
    A.iosc = C.ab.viewf(C.ab.take_f(512), 0, 512)
    A.epsb = nt(nc, "epsb" + uniq, 128, 1, F32)
    A.r_ident = Res("ident")
    A.r_identf = Res("identf")
    A.r_masks2 = Res("masks2")
    A.r_onecol = Res("onecol")
    A.r_iosc = Res("iosc")
    A.r_epsb = Res("epsb")
    C.op("vector", lambda v: v.memset(A.epsb[:], LN_EPS), writes=[A.r_epsb])

    C.op("gpsimd", lambda g: g.iota(
        A.iosc[:], pattern=[[1, 512]], base=0, channel_multiplier=-1,
        allow_small_or_imprecise_dtypes=True), writes=[A.r_iosc])
    C.op("vector", lambda v: v.tensor_scalar(
        A.ident[:], A.iosc[:, 0:128], 0.0, None, ALU.is_equal),
        reads=[A.r_iosc], writes=[A.r_ident])
    C.op("vector", lambda v: v.tensor_scalar(
        A.identf[:], A.iosc[:, 0:128], 0.0, None, ALU.is_equal),
        reads=[A.r_iosc], writes=[A.r_identf])
    for j in range(4):
        C.op("vector", lambda v, j=j: v.tensor_scalar(
            A.masks2[:, j * 512:(j + 1) * 512],
            A.iosc[:], float(j * 128), None,
            ALU.is_ge), reads=[A.r_iosc], writes=[A.r_masks2])
    C.op("gpsimd", lambda g: g.iota(
        A.iosc[0:1, 0:128], pattern=[[1, 128]], base=0, channel_multiplier=0,
        allow_small_or_imprecise_dtypes=True),
        reads=[A.r_masks2, A.r_ident, A.r_identf], writes=[A.r_iosc])
    C.op("vector", lambda v: v.tensor_scalar(
        A.onecolA[:], A.iosc[0:1, 0:128], 64.0, None, ALU.is_lt),
        reads=[A.r_iosc], writes=[A.r_onecol])
    C.op("vector", lambda v: v.tensor_scalar(
        A.onecolB[:], A.iosc[0:1, 0:128], 64.0, None, ALU.is_ge),
        reads=[A.r_iosc], writes=[A.r_onecol])
    A.onecolE = nt(nc, "onecolE" + uniq, 128, 128, BF16)
    A.onecolO = nt(nc, "onecolO" + uniq, 128, 128, BF16)
    A.r_onecolEO = Res("onecolEO")
    for t_ in (A.onecolE, A.onecolO):
        C.op("vector", lambda v, t_=t_: v.memset(t_[:], 0.0),
             reads=[A.r_onecolEO], writes=[A.r_onecolEO])
    for (t_, b_) in ((A.onecolE, 0), (A.onecolO, 64)):
        C.op("vector", lambda v, t_=t_, b_=b_: v.tensor_scalar(
            t_[b_:b_ + 1, :], A.iosc[0:1, 0:128], 64.0, None, ALU.is_lt),
            reads=[A.r_iosc, A.r_onecolEO], writes=[A.r_onecolEO])
        C.op("vector", lambda v, t_=t_, b_=b_: v.tensor_scalar(
            t_[b_ + 32:b_ + 33, :], A.iosc[0:1, 0:128], 64.0, None,
            ALU.is_ge),
            reads=[A.r_iosc, A.r_onecolEO], writes=[A.r_onecolEO])
    C.barrier()
    C.ab.release()
    return A


def bcast_row_from_dram(C, dst_ap, dst_res, row_ap, parts=128, sem="bc"):
    n = row_ap.shape[-1]
    bsrc = row_ap.broadcast_to([parts, n])
    C.dma("sync", lambda e: e.dma_start(out=dst_ap, in_=bsrc),
          sem, writes=[dst_res])



def emit_cast_transpose_in(C, A, x_dram, xfm_dst3, r_xfm, o_xc, r_xc, ab,
                           sem_base):
    nb_ = len(o_xc)

    def dma_tile(t):
        par = t % nb_
        xc = ab.view(o_xc[par], 0, 1024)
        C.dma("gpsimd", lambda e, t=t, xc=xc: e.dma_start(
            out=xc, in_=x_dram[t * 128:(t + 1) * 128, :]),
            f"{sem_base}{par}", writes=[r_xc[par]])

    for t in range(nb_ - 1):
        dma_tile(t)
    for t in range(NTT):
        par = t % nb_
        if t + nb_ - 1 < NTT:
            dma_tile(t + nb_ - 1)
        xc = ab.view(o_xc[par], 0, 1024)
        for c in range(NCH):
            C.op("tensor", lambda e, c=c, xc=xc: e.transpose(
                A.pbt[:, c * 128:(c + 1) * 128], xc[:, c * 128:(c + 1) * 128],
                A.ident[:]), reads=[r_xc[par], A.r_ident], writes=[A.rpbt])
        C.op("vector", lambda v, t=t: v.tensor_copy(xfm_dst3(t), A.pbt[:]),
             reads=[A.rpbt], writes=[r_xfm])
        r_xc[par].readers.append((C.pe, C.pe.n))


def emit_load_weight_direct(C, Z, io, wname, row_off=0):
    for c in range(NCH):
        C.dma("gpsimd", lambda e, c=c: e.dma_start(
            out=Z["wb"](c, 0, 1024),
            in_=io[wname][row_off + c * 128:row_off + (c + 1) * 128, :]),
            "wb", nowaw=(c > 0), writes=[Z["r_wb"]])


def emit_load_weight_T(C, A, Z, io, wname, row_off, ab, sign=False):
    o_ws = Z["o_wscr"]
    r_ws = Z["r_wscr"]
    for rc in range(NCH):
        par = rc % 2
        ws = ab.view(o_ws[par], 0, 1024)
        if sign:
            o_wf = Z["o_wf"][par]
            wfv = ab.viewf(o_wf, 0, 1024)
            C.dma("gpsimd", lambda e, rc=rc, wfv=wfv: e.dma_start(
                out=wfv, in_=io[wname][row_off + rc * 128:
                                       row_off + (rc + 1) * 128, :]),
                f"wf{par}", writes=[Z["r_wf"][par]])
            C.op("vector", lambda v, ws=ws, wfv=wfv: v.tensor_scalar(
                ws, wfv, 0.0, None, ALU.is_ge),
                reads=[Z["r_wf"][par]], writes=[r_ws[par]])
            C.op("vector", lambda v, ws=ws: v.tensor_scalar(
                ws, ws, 2.0, -1.0, ALU.mult, ALU.add),
                reads=[r_ws[par]], writes=[r_ws[par]])
        else:
            C.dma("gpsimd", lambda e, rc=rc, ws=ws: e.dma_start(
                out=ws, in_=io[wname][row_off + rc * 128:
                                      row_off + (rc + 1) * 128, :]),
                f"wscr{par}", writes=[r_ws[par]])
        for ic in range(NCH):
            C.op("tensor", lambda e, ic=ic, ws=ws: e.transpose(
                A.pbt[:, ic * 128:(ic + 1) * 128],
                ws[:, ic * 128:(ic + 1) * 128], A.ident[:]),
                reads=[r_ws[par], A.r_ident], writes=[A.rpbt])
        dst = Z["wb_all"]().rearrange("p (c n) -> p c n", c=NCH)[
            :, :, rc * 128:(rc + 1) * 128]
        C.op("vector", lambda v, dst=dst: v.tensor_copy(dst, A.pbt[:]),
             reads=[A.rpbt], writes=[Z["r_wb"]])



def emit_attn_phase(C, A, io, cfg):
    nc = C.nc
    ab, af = C.ab, C.af
    ab.mark()
    af.mark()

    o_qfm = ab.take(NCH * S)
    o_kfm = ab.take(NCH * S)
    o_von = ab.take(NTT * 1040)
    o_wb = ab.take(NCH * 1024)
    r_qfm, r_kfm, r_von, r_wb = Res("qfm"), Res("kfm"), Res("von"), Res("wb")

    qfm = lambda c, a, b_: ab.view(o_qfm, c * S + a, c * S + b_)
    kfm = lambda c, a, b_: ab.view(o_kfm, c * S + a, c * S + b_)
    wb = lambda c, a, b_: ab.view(o_wb, c * 1024 + a, c * 1024 + b_)

    def von_ctx(kt, head):
        base = kt * 1040 + head * 65
        return ab.view(o_von, base, base + 65)

    def von_dst(kt, vs):
        base = kt * 1040 + vs * 8 * 65
        v = ab.view(o_von, base, base + 8 * 65)
        return v.rearrange("p (h d) -> p h d", d=65)[:, :, 0:64]

    biasq = nt(nc, "biasq" + cfg["uniq"], 128, 8, F32)
    biask = nt(nc, "biask" + cfg["uniq"], 128, 8, F32)
    biasd = nt(nc, "biasd" + cfg["uniq"], 128, 8, F32)
    vbias = ab.viewf(ab.take_f(1024), 0, 1024)
    d2s = ab.viewf(ab.take_f(512), 0, 512)
    d2rf = ab.viewf(ab.take_f(512), 0, 512)
    d2ball = ab.view(ab.take(512), 0, 512)
    stat2 = [af.view(af.take(8), 0, 8), af.view(af.take(8), 0, 8)]
    sq2 = [ab.view(ab.take(1024), 0, 1024), ab.view(ab.take(1024), 0, 1024)]
    r_biasq, r_biask, r_biasd = Res("bq"), Res("bk"), Res("bd")
    r_vbias = Res("vb")
    r_d2s, r_d2rf, r_d2ball = Res("d2s"), Res("d2rf"), Res("d2ball")
    r_stat2 = [Res("stat0"), Res("stat1")]
    r_sq2 = [Res("sq0"), Res("sq1")]
    C.op("vector", lambda v: v.memset(d2s[:], 1.0), writes=[r_d2s])

    C.dma("sync", lambda e: e.dma_start(out=biasq[:], in_=io[cfg["bq"]]),
          "cbq", writes=[r_biasq])
    C.dma("sync", lambda e: e.dma_start(out=biask[:], in_=io[cfg["bk"]]),
          "cbk", writes=[r_biask])
    C.dma("sync", lambda e: e.dma_start(out=biasd[:], in_=io[cfg["bd"]]),
          "cbd", writes=[r_biasd])
    bcast_row_from_dram(C, vbias, r_vbias, io[cfg["bv_row"]], sem="cvb")

    if cfg["rope"]:
        ropeC = ab.view(ab.take(S), 0, S)
        ropeS = ab.view(ab.take(S), 0, S)
        r_ropeC, r_ropeS = Res("rC"), Res("rS")
        C.dma("sync", lambda e: e.dma_start(out=ropeC, in_=io["ropec"]),
              "crc", writes=[r_ropeC])
        C.dma("sync", lambda e: e.dma_start(out=ropeS, in_=io["ropes"]),
              "crs", writes=[r_ropeS])
    else:
        ropeC = ropeS = r_ropeC = r_ropeS = None

    vall = ab.view(o_von, 0, NTT * 1040)
    ones_dst = vall.rearrange("p (t h d) -> p t h d", h=16, d=65)[:, :, :, 64:65]
    C.op("vector", lambda v: v.memset(ones_dst, 1.0), writes=[r_von])

    r_pr = [Res("pr0"), Res("pr1")]
    r_ctx = Res("ctx")
    r_att = Res("att")
    r_aout = [Res("aout0"), Res("aout1")]
    r_y02 = [Res("y0a"), Res("y0b")]
    r_yout = [Res("yout0"), Res("yout1")]
    r_xres = [Res("xres0"), Res("xres1")]
    o_y02 = [af.take(1024), af.take(1024)]
    o_yout = [af.take(1024), af.take(1024)]
    o_xres = [af.take(1024), af.take(1024)]

    Z = dict(qfm=qfm, kfm=kfm, wb=wb, von_ctx=von_ctx, von_dst=von_dst,
             o_von=o_von, r_qfm=r_qfm, r_kfm=r_kfm, r_von=r_von, r_wb=r_wb,
             wb_all=lambda: ab.view(o_wb, 0, NCH * 1024),
             biasq=biasq, biask=biask, biasd=biasd, vbias=vbias,
             r_biasq=r_biasq, r_biask=r_biask, r_biasd=r_biasd,
             r_vbias=r_vbias,
             ropeC=ropeC, ropeS=ropeS, r_ropeC=r_ropeC, r_ropeS=r_ropeS,
             r_pr=r_pr, r_ctx=r_ctx,
             r_att=r_att, r_aout=r_aout,
             d2s=d2s, d2rf=d2rf, d2ball=d2ball,
             r_d2s=r_d2s, r_d2rf=r_d2rf, r_d2ball=r_d2ball,
             stat2=stat2, r_stat2=r_stat2, sq2=sq2, r_sq2=r_sq2,
             o_y02=o_y02, r_y02=r_y02, o_yout=o_yout, r_yout=r_yout,
             o_xres=o_xres, r_xres=r_xres)

    for b in range(len(io["x"])):
        emit_attn_batch(C, A, io, cfg, b, Z)
    C.barrier()
    ab.release()
    af.release()


def emit_attn_batch(C, A, io, cfg, b, Z):
    ab, af = C.ab, C.af
    x_dram = io["x"][b]
    y_dram = io["y"][b]

    ab.mark()
    o_xfm = ab.take(NCH * S)
    o_xc = [ab.take(1024) for _ in range(3)]
    r_xfm = Res("xfm")
    r_xc = [Res(f"xc{i}") for i in range(3)]
    xfm = lambda c, a, b_: ab.view(o_xfm, c * S + a, c * S + b_)
    af.mark()
    any_t = any(cfg[k][2] for k in ("wq", "wk", "wv", "wd"))
    o_wscr = [ab.take(1024), ab.take(1024)] if any_t else None
    r_wscr = [Res("ws0"), Res("ws1")]
    Zb = dict(Z)
    Zb["o_wscr"], Zb["r_wscr"] = o_wscr, r_wscr

    def xfm_dst3(t):
        return ab.view(o_xfm, 0, NCH * S).rearrange(
            "p (c s) -> p c s", c=NCH)[:, :, t * 128:(t + 1) * 128]

    emit_cast_transpose_in(C, A, x_dram, xfm_dst3, r_xfm, o_xc, r_xc, ab, "xc")

    def load_w(key):
        wname, row_off, transposed = cfg[key]
        if transposed:
            emit_load_weight_T(C, A, Zb, io, wname, row_off, ab)
        else:
            emit_load_weight_direct(C, Zb, io, wname, row_off)

    def proj_fm(dstv, r_dst, bias, r_bias):
        for fc in range(NCH):
            for ts in range(NQS):
                pp = (4, 0, 5, 2)[(fc * NQS + ts) % 4]
                for ec in range(NCH):
                    C.op("tensor", lambda e, fc=fc, ts=ts, ec=ec, pp=pp:
                         e.matmul(A.pb[pp][:],
                                  Zb["wb"](ec, fc * 128, (fc + 1) * 128),
                                  xfm(ec, ts * 512, (ts + 1) * 512),
                                  start=(ec == 0), stop=(ec == NCH - 1)),
                         reads=[Zb["r_wb"], r_xfm], writes=[A.rpb[pp]])
                C.op("scalar", lambda e, fc=fc, ts=ts, pp=pp, dstv=dstv,
                     bias=bias: e.activation(
                         dstv(fc, ts * 512, (ts + 1) * 512), A.pb[pp][:],
                         AF.Identity, bias=bias[:, fc:fc + 1], scale=1.0),
                     reads=[A.rpb[pp], r_bias], writes=[r_dst])

    load_w("wq")
    proj_fm(Zb["qfm"], Zb["r_qfm"], Zb["biasq"], Zb["r_biasq"])
    load_w("wk")
    proj_fm(Zb["kfm"], Zb["r_kfm"], Zb["biask"], Zb["r_biask"])
    load_w("wv")
    for kt in range(NTT):
        for vs in range(2):
            pp = (4, 0, 5, 2)[(kt * 2 + vs) % 4]
            for ec in range(NCH):
                C.op("tensor", lambda e, kt=kt, vs=vs, ec=ec, pp=pp:
                     e.matmul(A.pb[pp][:],
                              xfm(ec, kt * 128, (kt + 1) * 128),
                              Zb["wb"](ec, vs * 512, (vs + 1) * 512),
                              start=(ec == 0), stop=(ec == NCH - 1)),
                     reads=[Zb["r_wb"], r_xfm], writes=[A.rpb[pp]])
            src3 = A.pb[pp][:].rearrange("p (h d) -> p h d", d=64)
            C.op("vector", lambda v, kt=kt, vs=vs, src3=src3: v.tensor_tensor(
                Zb["von_dst"](kt, vs), src3,
                Zb["vbias"][:, vs * 512:(vs + 1) * 512].rearrange(
                    "p (h d) -> p h d", d=64), ALU.add),
                reads=[A.rpb[pp], Zb["r_vbias"]], writes=[Zb["r_von"]])

    C.barrier()
    ab.release()
    af.release()
    ab.mark()
    o_sw = ab.take(S)
    o_rt = ab.take(S)
    r_sw, r_rt = Res("sw"), Res("rt")
    o_pr = [ab.take(1024), ab.take(1024)]
    o_ctx = ab.take(NCH * 512)
    o_att = ab.take(4 * 1024)
    o_aout = [ab.take(512), ab.take(512)]
    Zb["o_pr"] = o_pr
    Zb["o_ctx"], Zb["o_att"], Zb["o_aout"] = o_ctx, o_att, o_aout
    Zb["ctxv"] = lambda c: ab.view(o_ctx, c * 512, (c + 1) * 512)

    if cfg["rope"]:
        def rope(fmv, r_fm):
            for c in range(NCH):
                sw = ab.view(o_sw, 0, S)
                rt = ab.view(o_rt, 0, S)
                fv = fmv(c, 0, S)
                for (da, sa) in ((0, 16), (16, 0), (64, 80), (80, 64)):
                    C.dma("sync", lambda e, fv=fv, da=da, sa=sa, sw=sw:
                          e.dma_start(out=sw[da:da + 16, :],
                                      in_=fv[sa:sa + 16, :]), "ropesw",
                          nowaw=(da != 0), reads=[r_fm], writes=[r_sw])
                C.op("vector", lambda v, sw=sw, rt=rt: v.tensor_tensor(
                    rt, sw, Zb["ropeS"], ALU.mult),
                    reads=[r_sw, Zb["r_ropeS"]], writes=[r_rt])
                C.op("vector", lambda v, c=c, sw=sw, fv=fv: v.tensor_tensor(
                    sw, fv, Zb["ropeC"], ALU.mult),
                    reads=[r_fm, Zb["r_ropeC"]], writes=[r_sw])
                C.op("vector", lambda v, c=c, sw=sw, rt=rt, fv=fv:
                     v.tensor_tensor(fv, sw, rt, ALU.subtract),
                     reads=[r_sw, r_rt], writes=[r_fm])
        rope(Zb["qfm"], Zb["r_qfm"])
        rope(Zb["kfm"], Zb["r_kfm"])

    load_w("wd")

    for qs in range(NQS):
        emit_attn_qs(C, A, cfg, b, qs, Zb, x_dram, y_dram)
    C.barrier()
    ab.release()


def emit_attn_qs(C, A, cfg, b, qs, Z, x_dram, y_dram):
    ab, af = C.ab, C.af
    causal = cfg["causal"]
    kmax = 4 * qs + 4 if causal else NTT
    esc = cfg["escale"]

    def emit_scores(hc, kt):
        par = kt % 2
        C.op("tensor", lambda e, hc=hc, qs=qs, kt=kt, par=par:
             e.matmul(A.pbs[par][:, 0:512],
                      Z["kfm"](hc, kt * 128, (kt + 1) * 128)[0:64, :],
                      Z["qfm"](hc, qs * 512, (qs + 1) * 512)[0:64, :],
                      start=True, stop=True, tile_position=(0, 0)),
             reads=[Z["r_kfm"], Z["r_qfm"]], writes=[A.rpbs[par]])
        C.op("tensor", lambda e, hc=hc, qs=qs, kt=kt, par=par:
             e.matmul(A.pbs[par][:, 512:1024],
                      Z["kfm"](hc, kt * 128, (kt + 1) * 128)[64:128, :],
                      Z["qfm"](hc, qs * 512, (qs + 1) * 512)[64:128, :],
                      start=True, stop=True, tile_position=(64, 0)),
             reads=[Z["r_kfm"], Z["r_qfm"]], writes=[A.rpbs[par]])
        pr = ab.view(Z["o_pr"][par], 0, 1024)
        C.op("scalar", lambda e, par=par, pr=pr: e.activation(
            pr, A.pbs[par][:], AF.Exp, scale=esc),
            reads=[A.rpbs[par]], writes=[Z["r_pr"][par]])
        if causal and kt >= 4 * qs:
            j = kt - 4 * qs
            mj = A.masks2[:, j * 512:(j + 1) * 512]
            for h in range(2):
                C.op("vector", lambda v, pr=pr, mj=mj, h=h: v.tensor_tensor(
                    pr[:, h * 512:(h + 1) * 512], pr[:, h * 512:(h + 1) * 512],
                    mj, ALU.mult),
                    reads=[Z["r_pr"][par], A.r_masks2],
                    writes=[Z["r_pr"][par]])

    def emit_ctx(hc, kt):
        par = kt % 2
        pr = ab.view(Z["o_pr"][par], 0, 1024)
        C.op("tensor", lambda e, hc=hc, kt=kt, pr=pr, km=kmax:
             e.matmul(A.pb[4][0:65, :], Z["von_ctx"](kt, 2 * hc),
                      pr[:, 0:512], start=(kt == 0), stop=(kt == km - 1)),
             reads=[Z["r_von"], Z["r_pr"][par]], writes=[A.rpb[4]])
        C.op("tensor", lambda e, hc=hc, kt=kt, pr=pr, km=kmax:
             e.matmul(A.pb[5][0:65, :], Z["von_ctx"](kt, 2 * hc + 1),
                      pr[:, 512:1024], start=(kt == 0), stop=(kt == km - 1)),
             reads=[Z["r_von"], Z["r_pr"][par]], writes=[A.rpb[5]])

    def evac_ctx(hc):
        bA = (hc % 2) * 64
        ctxc = Z["ctxv"](hc)
        C.op("vector", lambda v, ctxc=ctxc: v.tensor_copy(
            ctxc[0:64, :], A.pb[4][0:64, :]),
            reads=[A.rpb[4]], writes=[Z["r_ctx"]])
        C.op("vector", lambda v, ctxc=ctxc: v.tensor_copy(
            ctxc[64:128, :], A.pb[5][0:64, :]),
            reads=[A.rpb[5]], writes=[Z["r_ctx"]])
        C.op("vector", lambda v, bA=bA: v.tensor_copy(
            Z["d2s"][bA:bA + 1, :], A.pb[4][64:65, :]),
            reads=[A.rpb[4]], writes=[Z["r_d2s"]])
        C.op("vector", lambda v, bA=bA: v.tensor_copy(
            Z["d2s"][bA + 32:bA + 33, :], A.pb[5][64:65, :]),
            reads=[A.rpb[5]], writes=[Z["r_d2s"]])

    def recip_cast():
        C.op("vector", lambda v: v.reciprocal(Z["d2rf"][:], Z["d2s"][:]),
             reads=[Z["r_d2s"]], writes=[Z["r_d2rf"]])
        C.op("vector", lambda v: v.tensor_copy(Z["d2ball"][:], Z["d2rf"][:]),
             reads=[Z["r_d2rf"]], writes=[Z["r_d2ball"]])

    def finish_norm(hc):
        sel = A.onecolE if hc % 2 == 0 else A.onecolO
        ctxc = Z["ctxv"](hc)
        C.op("tensor", lambda e, sel=sel: e.matmul(
            A.pb6[:], sel[:], Z["d2ball"][:], start=True, stop=True),
            reads=[A.r_onecolEO, Z["r_d2ball"]], writes=[A.rpb6])
        C.op("vector", lambda v, ctxc=ctxc: v.tensor_tensor(
            ctxc, ctxc, A.pb6[:], ALU.mult),
            reads=[Z["r_ctx"], A.rpb6], writes=[Z["r_ctx"]])

    for hc in range(NCH):
        emit_scores(hc, 0)
        for kt in range(1, kmax):
            emit_scores(hc, kt)
            emit_ctx(hc, kt - 1)
        emit_ctx(hc, kmax - 1)
        evac_ctx(hc)
        if hc >= 2:
            finish_norm(hc - 2)
        if hc % 2 == 1:
            recip_cast()
    finish_norm(NCH - 2)
    finish_norm(NCH - 1)

    def tp_out(dc):
        bi = dc % 2
        aout = ab.view(Z["o_aout"][bi], 0, 512)
        for j in range(4):
            C.op("tensor", lambda e, j=j, aout=aout: e.transpose(
                A.pbt[:, j * 128:(j + 1) * 128],
                aout[:, j * 128:(j + 1) * 128], A.ident[:]),
                reads=[Z["r_aout"][bi], A.r_ident], writes=[A.rpbt])
        dst = ab.view(Z["o_att"], 0, 4 * 1024).rearrange(
            "p (j d) -> p j d", j=4)[:, :, dc * 128:(dc + 1) * 128]
        src = A.pbt[:, 0:512].rearrange("p (j t) -> p j t", j=4)
        C.op("vector", lambda v, dst=dst, src=src: v.tensor_copy(dst, src),
             reads=[A.rpbt], writes=[Z["r_att"]])

    for dc in range(NCH):
        pp = (4, 0, 5, 2)[dc % 4]
        bi = dc % 2
        for ec in range(NCH):
            C.op("tensor", lambda e, dc=dc, ec=ec, pp=pp: e.matmul(
                A.pb[pp][:], Z["wb"](ec, dc * 128, (dc + 1) * 128),
                Z["ctxv"](ec), start=(ec == 0), stop=(ec == NCH - 1)),
                reads=[Z["r_wb"], Z["r_ctx"]], writes=[A.rpb[pp]])
        if dc > 0:
            tp_out(dc - 1)
        aout = ab.view(Z["o_aout"][bi], 0, 512)
        C.op("vector", lambda v, dc=dc, pp=pp, bi=bi, aout=aout:
             v.tensor_scalar(aout, A.pb[pp][:], Z["biasd"][:, dc:dc + 1],
                             None, ALU.add),
             reads=[A.rpb[pp], Z["r_biasd"]], writes=[Z["r_aout"][bi]])
    tp_out(NCH - 1)

    for j in range(4):
        t = qs * 4 + j
        par = t % 2
        xres = af.view(Z["o_xres"][par], 0, 1024)
        C.dma("sync", lambda e, t=t, xres=xres: e.dma_start(
            out=xres, in_=x_dram[t * 128:(t + 1) * 128, :]), f"xr{par}",
            writes=[Z["r_xres"][par]])
        att_t = ab.view(Z["o_att"], j * 1024, (j + 1) * 1024)
        y0 = af.view(Z["o_y02"][par], 0, 1024)
        emit_res_ln(C, A, Z, y0, Z["r_y02"][par],
                    [(att_t, Z["r_att"], 1.0)], xres, Z["r_xres"][par],
                    cfg["xscale"], Z["o_yout"][par], Z["r_yout"][par],
                    y_dram[t * 128:(t + 1) * 128, :], f"yst{par}", par=par)



def emit_ffn_phase(C, A, io):
    nc = C.nc
    ab, af = C.ab, C.af

    o_wb = ab.take(NCH * 1024)
    r_wb = Res("wb2")
    wb = lambda c, a, b_: ab.view(o_wb, c * 1024 + a, c * 1024 + b_)
    fscale = nt(nc, "fscale_sb", 128, 8, F32)
    fbias = nt(nc, "fbias_sb", 128, 8, F32)
    stat2 = [af.view(af.take(8), 0, 8), af.view(af.take(8), 0, 8)]
    sq2 = [ab.view(ab.take(1024), 0, 1024), ab.view(ab.take(1024), 0, 1024)]
    r_fscale, r_fbias = Res("fscale"), Res("fbias")
    r_stat2 = [Res("f2st0"), Res("f2st1")]
    r_sq2 = [Res("f2sq0"), Res("f2sq1")]
    C.dma("sync", lambda e: e.dma_start(out=fscale[:], in_=io["fscale"]),
          "cfs", writes=[r_fscale])
    C.dma("sync", lambda e: e.dma_start(out=fbias[:], in_=io["fbias"]),
          "cfb", writes=[r_fbias])

    ab.mark()
    af.mark()
    o_wscr = [ab.take(1024), ab.take(1024)]
    o_wf = [ab.take_f(1024), ab.take_f(1024)]
    Z = dict(o_wscr=o_wscr, r_wscr=[Res("ws0"), Res("ws1")],
             o_wf=o_wf, r_wf=[Res("wf0"), Res("wf1")],
             wb=wb, r_wb=r_wb, wb_all=lambda: ab.view(o_wb, 0, NCH * 1024),
             stat2=stat2, r_stat2=r_stat2, sq2=sq2, r_sq2=r_sq2)
    emit_load_weight_T(C, A, Z, io, "wf", 0, ab, sign=True)
    C.barrier()
    ab.release()
    af.release()

    ab.mark()
    af.mark()
    o_xfm = ab.take(NCH * S)
    o_xfl = ab.take(NCH * S)
    o_xc = [ab.take(1024), ab.take(1024)]
    o_att = ab.take_f(4 * 1024)
    o_aout = [ab.take_f(512), ab.take_f(512)]
    o_y02 = [af.take(1024), af.take(1024)]
    o_yout = [af.take(1024), af.take(1024)]
    o_xres = [af.take(1024), af.take(1024)]
    r_xfm = Res("xfm2")
    r_xfl = Res("xfl2")
    r_xc = [Res("xc0"), Res("xc1")]
    r_att = Res("att2")
    r_aout = [Res("ao0"), Res("ao1")]
    r_y02 = [Res("y02a"), Res("y02b")]
    r_yout = [Res("yo0"), Res("yo1")]
    r_xres = [Res("xr0"), Res("xr1")]
    xfm = lambda c, a, b_: ab.view(o_xfm, c * S + a, c * S + b_)
    xfl = lambda c, a, b_: ab.view(o_xfl, c * S + a, c * S + b_)
    o_xif = ab.take_f(1024)
    o_xlf = ab.take_f(1024)
    r_xif, r_xlf = Res("xif"), Res("xlf")
    o_xcl = [ab.take(1024), ab.take(1024)]
    r_xcl = [Res("xcl0"), Res("xcl1")]

    for b in range(len(io["x"])):
        x_dram = io["x"][b]
        y_dram = io["y"][b]
        for t in range(NTT):
            par = t % 2
            xc = ab.view(o_xc[par], 0, 1024)
            xcl = ab.view(o_xcl[par], 0, 1024)
            xif = ab.viewf(o_xif, 0, 1024)
            xlf = ab.viewf(o_xlf, 0, 1024)
            C.dma("sync", lambda e, t=t, x_dram=x_dram: e.dma_start(
                out=xif, in_=x_dram[t * 128:(t + 1) * 128, :]), "x2in",
                writes=[r_xif])
            C.op("scalar", lambda e, xc=xc: e.activation(
                xc, xif, AF.Identity, scale=1.0),
                reads=[r_xif], writes=[r_xc[par]])
            C.op("scalar", lambda e, xc=xc: e.activation(
                xlf, xc, AF.Identity, scale=1.0),
                reads=[r_xc[par]], writes=[r_xlf])
            C.op("vector", lambda v: v.tensor_tensor(
                xlf, xif, xlf, ALU.subtract),
                reads=[r_xif, r_xlf], writes=[r_xlf])
            C.op("scalar", lambda e, xcl=xcl: e.activation(
                xcl, xlf, AF.Identity, scale=1.0),
                reads=[r_xlf], writes=[r_xcl[par]])
            for (src_t, rs, dsto, rd) in ((xc, r_xc[par], o_xfm, r_xfm),
                                          (xcl, r_xcl[par], o_xfl, r_xfl)):
                for c in range(NCH):
                    C.op("tensor", lambda e, c=c, src_t=src_t: e.transpose(
                        A.pbt[:, c * 128:(c + 1) * 128],
                        src_t[:, c * 128:(c + 1) * 128], A.ident[:]),
                        reads=[rs, A.r_ident], writes=[A.rpbt])
                dst = ab.view(dsto, 0, NCH * S).rearrange(
                    "p (c s) -> p c s", c=NCH)[:, :, t * 128:(t + 1) * 128]
                C.op("vector", lambda v, dst=dst: v.tensor_copy(dst, A.pbt[:]),
                     reads=[A.rpbt], writes=[rd])
        for qs in range(NQS):
            for oc in range(NCH):
                pp = (4, 0, 5, 2)[oc % 4]

                def tp_oc(dc):
                    aoutp = ab.viewf(o_aout[dc % 2], 0, 512)
                    for j in range(4):
                        C.op("tensor", lambda e, j=j, aoutp=aoutp:
                             e.transpose(
                                 A.pb6[:, j * 128:(j + 1) * 128],
                                 aoutp[:, j * 128:(j + 1) * 128],
                                 A.identf[:]),
                             reads=[r_aout[dc % 2], A.r_identf],
                             writes=[A.rpb6])
                    dst = ab.viewf(o_att, 0, 4 * 1024).rearrange(
                        "p (j d) -> p j d", j=4)[:, :, dc * 128:(dc + 1) * 128]
                    src = A.pb6[:].rearrange("p (j t) -> p j t", j=4)
                    C.op("vector", lambda v, dst=dst, src=src:
                         v.tensor_copy(dst, src), reads=[A.rpb6],
                         writes=[r_att])

                for ic in range(NCH):
                    C.op("tensor", lambda e, oc=oc, ic=ic, pp=pp, qs=qs:
                         e.matmul(A.pb[pp][:],
                                  wb(ic, oc * 128, (oc + 1) * 128),
                                  xfm(ic, qs * 512, (qs + 1) * 512),
                                  start=(ic == 0), stop=False),
                         reads=[r_wb, r_xfm], writes=[A.rpb[pp]])
                for ic in range(NCH):
                    C.op("tensor", lambda e, oc=oc, ic=ic, pp=pp, qs=qs:
                         e.matmul(A.pb[pp][:],
                                  wb(ic, oc * 128, (oc + 1) * 128),
                                  xfl(ic, qs * 512, (qs + 1) * 512),
                                  start=False, stop=(ic == NCH - 1)),
                         reads=[r_wb, r_xfl], writes=[A.rpb[pp]])
                if oc > 0:
                    tp_oc(oc - 1)
                aout = ab.viewf(o_aout[oc % 2], 0, 512)
                C.op("scalar", lambda e, oc=oc, pp=pp, aout=aout:
                     e.activation(aout, A.pb[pp][:], AF.Identity,
                                  bias=fbias[:, oc:oc + 1],
                                  scale=fscale[:, oc:oc + 1]),
                     reads=[A.rpb[pp], r_fscale, r_fbias],
                     writes=[r_aout[oc % 2]])
                C.op("vector", lambda v, aout=aout: v.tensor_scalar(
                    aout, aout, 100.0, -100.0, ALU.min, ALU.max),
                    reads=[r_aout[oc % 2]], writes=[r_aout[oc % 2]])
                if oc == NCH - 1:
                    tp_oc(oc)
            for j in range(4):
                t = qs * 4 + j
                par = t % 2
                xres = af.view(o_xres[par], 0, 1024)
                C.dma("sync", lambda e, t=t, xres=xres, x_dram=x_dram:
                      e.dma_start(
                          out=xres, in_=x_dram[t * 128:(t + 1) * 128, :]),
                      f"xr{par}", writes=[r_xres[par]])
                att_t = ab.viewf(o_att, j * 1024, (j + 1) * 1024)
                y0 = af.view(o_y02[par], 0, 1024)
                emit_res_ln(C, A, Z, y0, r_y02[par], [(att_t, r_att, 1.0)],
                            xres, r_xres[par], 1.0, o_yout[par], r_yout[par],
                            y_dram[t * 128:(t + 1) * 128, :], f"yst{par}",
                            par=par)
                if P4_DEBUG and t < 2:
                    dy0 = C.nc.dram_tensor(f"dy0_{b}_{t}", [128, 1024], F32,
                                           kind="Internal").ap()
                    C.dma("sync", lambda e, dy0=dy0: e.dma_start(
                        out=dy0, in_=y0), "dbg4", nowaw=True, reads=[r_y0])
        C.barrier()
    ab.release()
    af.release()


def emit_res_ln(C, A, Z, y0, r_y0, terms, xres, r_xres, xscale, o_yout,
                r_yout, y_dram_slice, st_sem="yst", par=0):
    af = C.af
    st_, r_st = Z["stat2"][par], Z["r_stat2"][par]
    sq_, r_sq_ = Z["sq2"][par], Z["r_sq2"][par]
    (t_ap, t_res, t_scale) = terms[0]
    assert t_scale == 1.0
    if xscale == 1.0:
        C.op("vector", lambda v: v.tensor_tensor(y0, t_ap, xres, ALU.add),
             reads=[t_res, r_xres], writes=[r_y0])
    else:
        C.op("vector", lambda v: v.scalar_tensor_tensor(
            y0, xres, float(xscale), t_ap, ALU.mult, ALU.add),
            reads=[t_res, r_xres], writes=[r_y0])
    st = st_
    C.op("vector", lambda v: v.tensor_reduce(st[:, 0:1], y0, AX.X, ALU.add),
         reads=[r_y0], writes=[r_st])
    C.op("scalar", lambda e: e.activation(
        sq_, y0, AF.Square, accum_out=st[:, 1:2]),
        reads=[r_y0], writes=[r_sq_, r_st])
    C.op("vector", lambda v: v.tensor_scalar(
        st[:, 2:3], st[:, 0:1], 1.0 / D, None, ALU.mult),
        reads=[r_st], writes=[r_st])
    C.op("vector", lambda v: v.tensor_scalar(
        st[:, 3:4], st[:, 2:3], st[:, 2:3], None, ALU.mult),
        reads=[r_st], writes=[r_st])
    C.op("vector", lambda v: v.scalar_tensor_tensor(
        st[:, 4:5], st[:, 1:2], 1.0 / D, st[:, 3:4], ALU.mult, ALU.subtract),
        reads=[r_st], writes=[r_st])
    C.op("scalar", lambda e: e.activation(
        st[:, 5:6], st[:, 4:5], AF.Ln, bias=A.epsb[:], scale=1.0),
        reads=[r_st, A.r_epsb], writes=[r_st])
    C.op("scalar", lambda e: e.activation(
        st[:, 5:6], st[:, 5:6], AF.Exp, scale=-0.5),
        reads=[r_st], writes=[r_st])
    C.op("vector", lambda v: v.tensor_scalar(
        st[:, 6:7], st[:, 2:3], st[:, 5:6], -1.0, ALU.mult, ALU.mult),
        reads=[r_st], writes=[r_st])
    yout = af.view(o_yout, 0, 1024)
    C.op("scalar", lambda e: e.activation(
        yout, y0, AF.Identity, bias=st[:, 6:7], scale=st[:, 5:6]),
        reads=[r_y0, r_st], writes=[r_yout])
    C.dma("sync", lambda e: e.dma_start(out=y_dram_slice, in_=yout),
          st_sem, reads=[r_yout])



NQQ = 1024
NKK = 512
NG = 4
NHT = 8
BISECT_ITERS = 34


def emit_phase4(C, A, io, scratch):
    nc = C.nc
    ab = C.ab
    nb = len(io["x"])
    RT = nb * 8
    ab.mark()

    o_wrtf = ab.take_f(64)
    o_wrth = ab.take(64)
    o_wrtl = ab.take(64)
    r_wrtf, r_wrth, r_wrtl = Res("wrtf"), Res("wrth"), Res("wrtl")
    wrtf = ab.viewf(o_wrtf, 0, 64)
    wrth = ab.view(o_wrth, 0, 64)
    wrtl = ab.view(o_wrtl, 0, 64)
    o_tmpf = ab.take_f(64)
    tmpf = ab.viewf(o_tmpf, 0, 64)
    r_tmpf = Res("tmpf")
    C.dma("sync", lambda e: e.dma_start(out=wrtf, in_=io["wrt"]), "cwrt",
          writes=[r_wrtf])
    C.op("vector", lambda v: v.tensor_copy(wrth, wrtf),
         reads=[r_wrtf], writes=[r_wrth])
    C.op("vector", lambda v: v.tensor_copy(tmpf, wrth),
         reads=[r_wrth], writes=[r_tmpf])
    C.op("vector", lambda v: v.tensor_tensor(tmpf, wrtf, tmpf, ALU.subtract),
         reads=[r_wrtf, r_tmpf], writes=[r_tmpf])
    C.op("vector", lambda v: v.tensor_copy(wrtl, tmpf),
         reads=[r_tmpf], writes=[r_wrtl])

    nqv = nt(nc, "nqv4", 32, 1, F32)
    r_nqv = Res("nqv")
    C.dma("sync", lambda e: e.dma_start(out=nqv[0:RT, :], in_=io["nqrow"][0:RT, :]),
          "cnqv", writes=[r_nqv])
    bsel = nt(nc, "bsel4", 32, 4, BF16)
    r_bsel = Res("bsel")
    C.dma("sync", lambda e: e.dma_start(out=bsel[:], in_=io["bsel"]),
          "cbsel", writes=[r_bsel])
    nmsk = nt(nc, "nmsk4", 128, 1, F32)
    r_nmsk = Res("nmsk")
    C.op("vector", lambda v: v.memset(nmsk[0:8, :], NEG), writes=[r_nmsk])
    C.op("vector", lambda v: v.memset(nmsk[0:1, :], 0.0), writes=[r_nmsk])
    nullbc = ab.viewf(ab.take_f(1024), 0, 1024)
    r_nullbc = Res("nullbc")
    bcast_row_from_dram(C, nullbc, r_nullbc, io["ntok_row"], sem="cnbc")
    stat2 = [ab.viewf(ab.take_f(8), 0, 8), ab.viewf(ab.take_f(8), 0, 8)]
    sq2 = [ab.view(ab.take(1024), 0, 1024), ab.view(ab.take(1024), 0, 1024)]
    ZL = dict(stat2=stat2, r_stat2=[Res("stat4a"), Res("stat4b")],
              sq2=sq2, r_sq2=[Res("sq4a"), Res("sq4b")])

    zr = ab.viewf(ab.take_f(1024), 0, 1024)[0:1, :]
    r_zr = Res("zrow4")
    C.op("vector", lambda v: v.memset(zr, 0.0), writes=[r_zr])
    for b_ in range(nb):
        for g_ in range(NG):
            C.dma("sync", lambda e, b_=b_, g_=g_: e.dma_start(
                out=scratch["og"][b_][g_][NQQ:NQQ + 1, :], in_=zr),
                "ogz", nowaw=True, reads=[r_zr],
                writes=[scratch["r_og"][b_][g_]])

    ab.mark()
    o_logall = ab.take_f(2048)
    r_logall = Res("logall")
    logall = ab.viewf(o_logall, 0, 2048)
    o_logb = ab.take_f(2048)
    r_logb = Res("logb")
    logb = ab.viewf(o_logb, 0, 2048)
    o_xin = [ab.take_f(1024), ab.take_f(1024)]
    r_xin = [Res("xin0"), Res("xin1")]
    o_xhi = [ab.take(1024), ab.take(1024)]
    o_xlo = [ab.take(1024), ab.take(1024)]
    o_xlf = [ab.take_f(1024), ab.take_f(1024)]
    r_xhi = [Res("xhi0"), Res("xhi1")]
    r_xlo = [Res("xlo0"), Res("xlo1")]
    r_xlf = [Res("xlf0"), Res("xlf1")]
    o_xfh = ab.take(1024)
    o_xfl = ab.take(1024)
    r_xfh, r_xfl = Res("xfh"), Res("xfl")
    xfh = ab.view(o_xfh, 0, 1024)
    xfl = ab.view(o_xfl, 0, 1024)

    for b in range(nb):
        x_dram = io["x"][b]
        for t in range(NTT):
            par = t % 2
            xin = ab.viewf(o_xin[par], 0, 1024)
            xhi = ab.view(o_xhi[par], 0, 1024)
            xlo = ab.view(o_xlo[par], 0, 1024)
            xlf = ab.viewf(o_xlf[par], 0, 1024)
            C.dma("sync", lambda e, t=t, xin=xin, x_dram=x_dram: e.dma_start(
                out=xin, in_=x_dram[t * 128:(t + 1) * 128, :]), f"x4{par}",
                writes=[r_xin[par]])
            C.op("vector", lambda v, xhi=xhi, xin=xin: v.tensor_copy(xhi, xin),
                 reads=[r_xin[par]], writes=[r_xhi[par]])
            C.op("vector", lambda v, xlf=xlf, xhi=xhi: v.tensor_copy(xlf, xhi),
                 reads=[r_xhi[par]], writes=[r_xlf[par]])
            C.op("vector", lambda v, xlf=xlf, xin=xin: v.tensor_tensor(
                xlf, xin, xlf, ALU.subtract),
                reads=[r_xin[par], r_xlf[par]], writes=[r_xlf[par]])
            C.op("vector", lambda v, xlo=xlo, xlf=xlf: v.tensor_copy(xlo, xlf),
                 reads=[r_xlf[par]], writes=[r_xlo[par]])
            for c in range(NCH):
                C.op("tensor", lambda e, c=c, xhi=xhi: e.transpose(
                    A.pbt[:, c * 128:(c + 1) * 128],
                    xhi[:, c * 128:(c + 1) * 128], A.ident[:]),
                    reads=[r_xhi[par], A.r_ident], writes=[A.rpbt])
            C.op("vector", lambda v: v.tensor_copy(xfh[:], A.pbt[:]),
                 reads=[A.rpbt], writes=[r_xfh])
            for c in range(NCH):
                C.op("tensor", lambda e, c=c, xlo=xlo: e.transpose(
                    A.pbt[:, c * 128:(c + 1) * 128],
                    xlo[:, c * 128:(c + 1) * 128], A.ident[:]),
                    reads=[r_xlo[par], A.r_ident], writes=[A.rpbt])
            C.op("vector", lambda v: v.tensor_copy(xfl[:], A.pbt[:]),
                 reads=[A.rpbt], writes=[r_xfl])
            nmm = 3 * NCH
            i = 0
            for dc in range(NCH):
                for (wv, xv, rw, rx) in ((wrth, xfh, r_wrth, r_xfh),
                                         (wrth, xfl, r_wrth, r_xfl),
                                         (wrtl, xfh, r_wrtl, r_xfh)):
                    C.op("tensor", lambda e, dc=dc, wv=wv, xv=xv, i=i, nmm=nmm:
                         e.matmul(A.pb[4][0:8, 0:128],
                                  wv[:, dc * 8:(dc + 1) * 8],
                                  xv[:, dc * 128:(dc + 1) * 128],
                                  start=(i == 0), stop=(i == nmm - 1)),
                         reads=[rw, rx], writes=[A.rpb[4]])
                    i += 1
            C.op("scalar", lambda e, t=t: e.activation(
                logb[0:8, t * 128:(t + 1) * 128], A.pb[4][0:8, 0:128],
                AF.Identity, scale=1.0),
                reads=[A.rpb[4]], writes=[r_logb])
        C.dma("sync", lambda e, b=b: e.dma_start(
            out=bass_sb_rows(logall, b * 8, 8), in_=logb[0:8, :]),
            "logmv", nowaw=(b > 0), reads=[r_logb], writes=[r_logall])

    tok = ab.viewf(ab.take_f(2048), 0, 2048)
    r_tok = Res("tok")
    C.op("gpsimd", lambda g: g.iota(
        tok[0:RT, :], pattern=[[1, 2048]], base=0, channel_multiplier=0,
        allow_small_or_imprecise_dtypes=True), writes=[r_tok])
    zt = ab.viewf(ab.take_f(2048), 0, 2048)
    r_zt = Res("zt")
    C.op("vector", lambda v: v.memset(zt[0:RT, :], 0.0), writes=[r_zt])
    ge = ab.viewf(ab.take_f(2048), 0, 2048)
    m1 = ab.viewf(ab.take_f(2048), 0, 2048)
    m2 = ab.viewf(ab.take_f(2048), 0, 2048)
    cms = ab.viewf(ab.take_f(2048), 0, 2048)
    msk = ab.viewf(ab.take_f(2048), 0, 2048)
    slt = ab.viewf(ab.take_f(2048), 0, 2048)
    sgv = ab.viewf(ab.take_f(2048), 0, 2048)
    r_ge, r_m1, r_m2 = Res("ge"), Res("m1"), Res("m2")
    r_cms, r_msk, r_slt, r_sgv = Res("cms"), Res("msk"), Res("slt"), Res("sgv")
    o_su = ab.take_f(2048)
    su = ab.viewu(o_su, 0, 2048)
    r_su = Res("su")
    mb = ab.view(ab.take(2048), 0, 2048)
    r_mb = Res("mb")
    cnts4 = ab.viewf(ab.take_f(2048), 0, 2048)
    r_cnts4 = Res("cnts4")
    ts4 = nt(nc, "ts4", 32, 8, F32)
    r_ts4 = Res("ts4")
    predi = nt(nc, "predi4", 32, 1, U8)
    r_predi = Res("predi")
    npredi = nt(nc, "npredi4", 32, 1, U8)
    r_npredi = Res("npredi")

    lg = lambda: logall[0:RT, :]
    C.op("vector", lambda v: v.tensor_reduce(ts4[0:RT, 0:1], lg(), AX.X, ALU.min,
                                             negate=False),
         reads=[r_logall], writes=[r_ts4])
    C.op("vector", lambda v: v.tensor_reduce(ts4[0:RT, 1:2], lg(), AX.X, ALU.max),
         reads=[r_logall], writes=[r_ts4])
    C.op("vector", lambda v: v.tensor_scalar(
        ts4[0:RT, 0:1], ts4[0:RT, 0:1], -1.0, None, ALU.add),
        reads=[r_ts4], writes=[r_ts4])
    C.op("vector", lambda v: v.tensor_scalar(
        ts4[0:RT, 1:2], ts4[0:RT, 1:2], 1.0, None, ALU.add),
        reads=[r_ts4], writes=[r_ts4])
    for it in range(BISECT_ITERS):
        C.op("vector", lambda v: v.tensor_tensor(
            ts4[0:RT, 2:3], ts4[0:RT, 0:1], ts4[0:RT, 1:2], ALU.add),
            reads=[r_ts4], writes=[r_ts4])
        C.op("vector", lambda v: v.tensor_scalar(
            ts4[0:RT, 2:3], ts4[0:RT, 2:3], 0.5, None, ALU.mult),
            reads=[r_ts4], writes=[r_ts4])
        C.op("vector", lambda v: v.tensor_scalar(
            ge[0:RT, :], lg(), ts4[0:RT, 2:3], None, ALU.is_ge,
            ALU.add, accum_out=ts4[0:RT, 3:4]),
            reads=[r_logall, r_ts4], writes=[r_ge, r_ts4])
        C.op("vector", lambda v: v.tensor_tensor(
            ts4[0:RT, 4:5], ts4[0:RT, 3:4], nqv[0:RT, :], ALU.is_ge),
            reads=[r_ts4, r_nqv], writes=[r_ts4])
        C.op("vector", lambda v: v.tensor_copy(predi[0:RT, :], ts4[0:RT, 4:5]),
             reads=[r_ts4], writes=[r_predi])
        C.op("vector", lambda v: v.tensor_scalar(
            ts4[0:RT, 5:6], ts4[0:RT, 4:5], -1.0, 1.0, ALU.mult, ALU.add),
            reads=[r_ts4], writes=[r_ts4])
        C.op("vector", lambda v: v.tensor_copy(npredi[0:RT, :], ts4[0:RT, 5:6]),
             reads=[r_ts4], writes=[r_npredi])
        C.op("vector", lambda v: v.select(
            ts4[0:RT, 0:1], predi[0:RT, :], ts4[0:RT, 2:3], ts4[0:RT, 0:1],
            add_drain=True),
            reads=[r_predi, r_ts4], writes=[r_ts4])
        C.op("vector", lambda v: v.select(
            ts4[0:RT, 1:2], npredi[0:RT, :], ts4[0:RT, 2:3], ts4[0:RT, 1:2],
            add_drain=True),
            reads=[r_npredi, r_ts4], writes=[r_ts4])
    C.op("vector", lambda v: v.tensor_scalar(
        m1[0:RT, :], lg(), ts4[0:RT, 1:2], None, ALU.is_ge, ALU.add,
        accum_out=ts4[0:RT, 6:7]),
        reads=[r_logall, r_ts4], writes=[r_m1, r_ts4])
    C.op("vector", lambda v: v.tensor_scalar(
        m2[0:RT, :], lg(), ts4[0:RT, 0:1], None, ALU.is_equal),
        reads=[r_logall, r_ts4], writes=[r_m2])
    C.op("vector", lambda v: v.tensor_tensor_scan(
        cms[0:RT, :], m2[0:RT, :], zt[0:RT, :], 0.0, ALU.add, ALU.add),
        reads=[r_m2, r_zt], writes=[r_cms])
    C.op("vector", lambda v: v.tensor_tensor(
        ts4[0:RT, 5:6], nqv[0:RT, :], ts4[0:RT, 6:7], ALU.subtract),
        reads=[r_nqv, r_ts4], writes=[r_ts4])
    C.op("vector", lambda v: v.tensor_scalar(
        ge[0:RT, :], cms[0:RT, :], ts4[0:RT, 5:6], None, ALU.is_le),
        reads=[r_cms, r_ts4], writes=[r_ge])
    C.op("vector", lambda v: v.tensor_tensor(
        m2[0:RT, :], m2[0:RT, :], ge[0:RT, :], ALU.mult),
        reads=[r_m2, r_ge], writes=[r_m2])
    C.op("vector", lambda v: v.tensor_tensor(
        msk[0:RT, :], m1[0:RT, :], m2[0:RT, :], ALU.add),
        reads=[r_m1, r_m2], writes=[r_msk])
    if P4_DEBUG:
        for nm, tl in (("dbg0", msk), ("dbg1", m1), ("dbg2", logall),
                       ("dbg3", m2)):
            dt_ = C.nc.dram_tensor(nm, [RT, 2048], F32, kind="Internal").ap()
            C.dma("sync", lambda e, dt_=dt_, tl=tl: e.dma_start(
                out=dt_, in_=tl[0:RT, :]), "dbg", nowaw=True,
                reads=[r_msk, r_m1, r_logall, r_m2])
        dt4 = C.nc.dram_tensor("dbg4", [RT, 2], F32, kind="Internal").ap()
        C.dma("sync", lambda e: e.dma_start(out=dt4, in_=ts4[0:RT, 0:2]),
              "dbg", nowaw=True, reads=[r_ts4])
    C.op("vector", lambda v: v.tensor_copy(mb[0:RT, :], msk[0:RT, :]),
         reads=[r_msk], writes=[r_mb])
    for j in range(4):
        C.op("tensor", lambda e, j=j: e.matmul(
            A.pb[5][0:nb, :], bsel[0:RT, 0:nb],
            mb[0:RT, j * 512:(j + 1) * 512],
            start=True, stop=True),
            reads=[r_bsel, r_mb], writes=[A.rpb[5]])
        C.op("scalar", lambda e, j=j: e.activation(
            cnts4[0:nb, j * 512:(j + 1) * 512], A.pb[5][0:nb, :], AF.Identity,
            scale=1.0), reads=[A.rpb[5]], writes=[r_cnts4])
    C.dma("sync", lambda e: e.dma_start(out=scratch["cntsd"][0:nb, :],
                                        in_=cnts4[0:nb, :]),
          "cnt4", reads=[r_cnts4], writes=[scratch["r_cn"]])
    C.op("vector", lambda v: v.tensor_tensor_scan(
        slt[0:RT, :], msk[0:RT, :], zt[0:RT, :], 0.0, ALU.add, ALU.add),
        reads=[r_msk, r_zt], writes=[r_slt])
    C.op("vector", lambda v: v.tensor_scalar(
        slt[0:RT, :], slt[0:RT, :], -1.0, None, ALU.add),
        reads=[r_slt], writes=[r_slt])
    C.op("scalar", lambda e: e.activation(sgv[0:RT, :], lg(), AF.Exp,
                                          scale=-1.0),
         reads=[r_logall], writes=[r_sgv])
    C.op("vector", lambda v: v.tensor_scalar(
        sgv[0:RT, :], sgv[0:RT, :], 1.0, None, ALU.add),
        reads=[r_sgv], writes=[r_sgv])
    C.op("vector", lambda v: v.reciprocal(sgv[0:RT, :], sgv[0:RT, :]),
         reads=[r_sgv], writes=[r_sgv])
    C.op("vector", lambda v: v.tensor_scalar(
        ts4[0:RT, 7:8], nqv[0:RT, :], 1.0, None, ALU.add),
        reads=[r_nqv], writes=[r_ts4])
    C.op("vector", lambda v: v.tensor_scalar(
        ge[0:RT, :], tok[0:RT, :], ts4[0:RT, 7:8], None, ALU.add),
        reads=[r_tok, r_ts4], writes=[r_ge])
    C.op("vector", lambda v: v.tensor_tensor(
        cms[0:RT, :], slt[0:RT, :], ge[0:RT, :], ALU.subtract),
        reads=[r_slt, r_ge], writes=[r_cms])
    C.op("vector", lambda v: v.tensor_tensor(
        cms[0:RT, :], cms[0:RT, :], msk[0:RT, :], ALU.mult),
        reads=[r_cms, r_msk], writes=[r_cms])
    C.op("vector", lambda v: v.tensor_tensor(
        cms[0:RT, :], cms[0:RT, :], ge[0:RT, :], ALU.add),
        reads=[r_cms, r_ge], writes=[r_cms])
    C.op("vector", lambda v: v.tensor_scalar(
        cms[0:RT, :], cms[0:RT, :], 1024.0, None, ALU.min),
        reads=[r_cms], writes=[r_cms])
    C.op("vector", lambda v: v.tensor_copy(su[0:RT, :], cms[0:RT, :]),
         reads=[r_cms], writes=[r_su])
    C.dma("sync", lambda e: e.dma_start(out=scratch["slotsc"][0:RT, :],
                                        in_=su[0:RT, :]),
          "sl4", reads=[r_su], writes=[scratch["r_sl"]])

    o_R0 = ab.take_f(2048)
    o_R1 = ab.take_f(2048)
    Rb = [ab.viewf(o_R0, 0, 2048), ab.viewf(o_R1, 0, 2048)]
    Vt = [ge, m1]
    Vs = [m2, cms]
    r_Rb = [Res("R0"), Res("R1")]
    r_Vt = [r_ge, r_m1]
    r_Vs = [r_m2, r_cms]
    o_bu = ab.take(1024)
    bu8 = ab.view(o_bu, 0, 1024).bitcast(U8)
    r_bu = Res("bu8")
    bmv = zt
    r_bmv = r_zt
    r2v = slt
    r_r2 = r_slt
    rsv = sgv
    C.op("vector", lambda v: v.tensor_copy(Vt[0][0:RT, :], tok[0:RT, :]),
         reads=[r_tok], writes=[r_Vt[0]])
    C.op("vector", lambda v: v.tensor_copy(Vs[0][0:RT, :], sgv[0:RT, :]),
         reads=[r_sgv], writes=[r_Vs[0]])
    C.op("vector", lambda v: v.tensor_tensor(
        Rb[0][0:RT, :], tok[0:RT, :], slt[0:RT, :], ALU.subtract),
        reads=[r_tok, r_slt], writes=[r_Rb[0]])
    C.op("vector", lambda v: v.tensor_tensor(
        Rb[0][0:RT, :], Rb[0][0:RT, :], msk[0:RT, :], ALU.mult),
        reads=[r_Rb[0], r_msk], writes=[r_Rb[0]])
    rsv_r = r_sgv
    o_Ri = ab.take_f(2048)
    Rint = ab.viewu(o_Ri, 0, 2048).bitcast(mybir.dt.int32)
    r_Rint = Res("Rint")
    for k in range(11):
        s = 1 << k
        cur, nxt = k % 2, (k + 1) % 2
        W = 2048 - s
        C.op("vector", lambda v, cur=cur: v.tensor_copy(
            Rint[0:RT, :], Rb[cur][0:RT, :]),
            reads=[r_Rb[cur]], writes=[r_Rint])
        C.op("vector", lambda v, s=s: v.tensor_scalar(
            Rint[0:RT, :], Rint[0:RT, :], int(s), None, ALU.bitwise_and),
            reads=[r_Rint], writes=[r_Rint])
        C.op("vector", lambda v: v.tensor_scalar(
            bmv[0:RT, :], Rint[0:RT, :], 0.0, None, ALU.is_gt),
            reads=[r_Rint], writes=[r_bmv])
        C.op("vector", lambda v, s=s, W=W: v.tensor_copy(
            bu8[0:RT, 0:W], bmv[0:RT, s:2048]),
            reads=[r_bmv], writes=[r_bu])
        C.op("vector", lambda v, cur=cur: v.scalar_tensor_tensor(
            r2v[0:RT, :], bmv[0:RT, :], -1.0, Rb[cur][0:RT, :],
            ALU.mult, ALU.mult), reads=[r_bmv, r_Rb[cur]], writes=[r_r2])
        C.op("vector", lambda v, cur=cur: v.tensor_tensor(
            r2v[0:RT, :], r2v[0:RT, :], Rb[cur][0:RT, :], ALU.add),
            reads=[r_r2, r_Rb[cur]], writes=[r_r2])
        C.op("vector", lambda v, s=s, cur=cur: v.tensor_scalar(
            rsv[0:RT, 0:2048 - s], Rb[cur][0:RT, s:2048], float(-s), None,
            ALU.add), reads=[r_Rb[cur]], writes=[rsv_r])
        for (dst, dr, src_, sr, keep, kr) in (
                (Rb[nxt], r_Rb[nxt], rsv, rsv_r, r2v, r_r2),
                (Vt[nxt], r_Vt[nxt], Vt[cur], r_Vt[cur], Vt[cur], r_Vt[cur]),
                (Vs[nxt], r_Vs[nxt], Vs[cur], r_Vs[cur], Vs[cur], r_Vs[cur])):
            shift_src = (src_[0:RT, 0:W] if src_ is rsv
                         else src_[0:RT, s:2048])
            C.op("vector", lambda v, dst=dst, keep=keep, s=s, W=W:
                 v.tensor_copy(dst[0:RT, W:2048], keep[0:RT, W:2048]),
                 reads=[kr], writes=[dr])
            C.op("vector", lambda v, dst=dst, shift_src=shift_src, keep=keep,
                 s=s, W=W: v.select(
                     dst[0:RT, 0:W], bu8[0:RT, 0:W], shift_src,
                     keep[0:RT, 0:W], add_drain=True),
                 reads=[r_bu, sr, kr], writes=[dr])
    fin = 11 % 2
    C.dma("sync", lambda e: e.dma_start(out=scratch["tokd"][0:RT, :],
                                        in_=Vt[fin][0:RT, :]),
          "tk4", nowaw=False, reads=[r_Vt[fin]], writes=[scratch["r_lists"]])
    C.dma("sync", lambda e: e.dma_start(out=scratch["sgcd"][0:RT, :],
                                        in_=Vs[fin][0:RT, :]),
          "tk4", nowaw=True, reads=[r_Vs[fin]], writes=[scratch["r_lists"]])

    C.barrier()
    ab.release()

    o_wq4 = ab.take(4096)
    o_wkv4 = ab.take(8192)
    o_wo4 = ab.take(4096)
    r_wq4, r_wkv4, r_wo4 = Res("wq4"), Res("wkv4"), Res("wo4")
    wq4 = lambda a, b_: ab.view(o_wq4, a, b_)
    wkv4 = lambda a, b_: ab.view(o_wkv4, a, b_)
    wo4 = lambda a, b_: ab.view(o_wo4, a, b_)
    o_qgfm = ab.take(8192)
    o_kgfm = ab.take(4096)
    r_qgfm, r_kgfm = Res("qgfm"), Res("kgfm")
    qgfm = lambda c, a, b_: ab.view(o_qgfm, c * 1024 + a, c * 1024 + b_)
    kgfm = lambda c, a, b_: ab.view(o_kgfm, c * 512 + a, c * 512 + b_)
    o_gg = [ab.take(1024) for _ in range(4)]
    r_gg = [Res(f"gg{i}") for i in range(4)]
    o_qpfm = ab.take(4096)
    r_qpfm = Res("qpfm")
    qpfm = lambda c, a, b_: ab.view(o_qpfm, c * 1024 + a, c * 1024 + b_)
    o_kfm4 = ab.take(4 * 520)
    r_kfm4 = Res("kfm4")
    kfm4 = lambda c, a, b_: ab.view(o_kfm4, c * 520 + a, c * 520 + b_)
    o_von4 = ab.take(5 * 520)
    r_von4 = Res("von4")

    def von4_ctx(kt, head):
        base = kt * 520 + head * 65
        return ab.view(o_von4, base, base + 65)

    o_ctx4 = ab.take(4 * 1024)
    r_ctx4 = Res("ctx4")
    ctx4 = lambda c, a, b_: ab.view(o_ctx4, c * 1024 + a, c * 1024 + b_)
    o_qpr = ab.take_f(32)
    qpr = ab.viewf(o_qpr, 0, 32)
    r_qpr = Res("qpr")
    o_qtok = ab.take_f(8)
    qtok = ab.viewu(o_qtok, 0, 8)
    o_qsg = ab.take_f(8)
    qsg = ab.viewf(o_qsg, 0, 8)
    r_qtok, r_qsg = Res("qtok"), Res("qsg")
    o_ktok = ab.take_f(4)
    ktok = ab.viewu(o_ktok, 0, 4)
    o_ksg = ab.take_f(4)
    ksg = ab.viewf(o_ksg, 0, 4)
    r_ktok, r_ksg = Res("ktok"), Res("ksg")
    o_prA = [ab.take(512), ab.take(512)]
    o_prB = [ab.take(512), ab.take(512)]
    r_prA = [Res("prA0"), Res("prA1")]
    r_prB = [Res("prB0"), Res("prB1")]
    o_ot = [ab.take_f(1024), ab.take_f(1024)]
    r_ot = [Res("ot0"), Res("ot1")]
    d2s4 = ab.viewf(ab.take_f(512), 0, 512)
    d2rf4 = ab.viewf(ab.take_f(512), 0, 512)
    d2b4 = ab.view(ab.take(512), 0, 512)
    r_d2s4, r_d2rf4, r_d2b4 = Res("d2s4"), Res("d2rf4"), Res("d2b4")
    C.op("vector", lambda v: v.memset(d2s4[:], 1.0), writes=[r_d2s4])
    nkT = ab.view(ab.take(16), 0, 16)
    r_nkT = Res("nkT")
    C.dma("gpsimd", lambda e: e.dma_start(out=nkT, in_=io["nullkT"]),
          "nk4", writes=[r_nkT])
    nvrow = ab.view(ab.take(512), 0, 512)[0:1, :]
    r_nvrow = Res("nvrow")

    for g in range(NG):
        for c in range(NCH):
            C.dma("gpsimd", lambda e, c=c, g=g: e.dma_start(
                out=wq4(c * 512, (c + 1) * 512),
                in_=io[f"wqT{g}"][c * 128:(c + 1) * 128, :]), "wq4",
                nowaw=(c > 0), writes=[r_wq4])
            C.dma("gpsimd", lambda e, c=c, g=g: e.dma_start(
                out=wkv4(c * 1024, (c + 1) * 1024),
                in_=io[f"wkvT{g}"][c * 128:(c + 1) * 128, :]), "wkv4",
                nowaw=(c > 0), writes=[r_wkv4])
        for c in range(4):
            C.dma("gpsimd", lambda e, c=c, g=g: e.dma_start(
                out=wo4(c * 1024, (c + 1) * 1024),
                in_=io[f"woutT{g}"][c * 128:(c + 1) * 128, :]), "wo4",
                nowaw=(c > 0), writes=[r_wo4])
        for b in range(nb):
            emit_p4_group_batch(C, A, io, scratch, g, b, locals())
        del b

    o_cnt = ab.take_f(16)
    cntt = ab.viewf(o_cnt, 0, 16)
    r_cnt = Res("cntt")
    o_scg = [ab.take_f(16) for _ in range(4)]
    scg = [ab.viewu(o, 0, 16) for o in o_scg]
    r_scg = [Res(f"scg{g}") for g in range(4)]
    o_st4 = [ab.take_f(1024), ab.take_f(1024)]
    r_st4 = [Res("st40"), Res("st41")]
    o_gst = [ab.take_f(1024) for _ in range(4)]
    gst = [ab.viewf(o, 0, 1024) for o in o_gst]
    r_gst = [Res(f"gst{g}") for g in range(4)]
    o_c1s = ab.take_f(2)
    c1s = ab.viewf(o_c1s, 0, 1)
    c2s = ab.viewf(o_c1s, 1, 2)
    r_c1s, r_c2s = Res("c1s"), Res("c2s")
    o_y02 = [ab.take_f(1024), ab.take_f(1024)]
    r_y02 = [Res("y04a"), Res("y04b")]
    o_yout = [ab.take_f(1024), ab.take_f(1024)]
    r_yout = [Res("yo40"), Res("yo41")]
    o_xr4 = [ab.take_f(1024), ab.take_f(1024)]
    r_xr4 = [Res("xr40"), Res("xr41")]

    for b in range(nb):
        x_dram = io["x"][b]
        y_dram = io["y"][b]
        C.dma("sync", lambda e, b=b: e.dma_start(
            out=cntt, in_=scratch["cntsd"][b:b + 1, :].rearrange(
                "x (p t) -> (x p) t", p=128)), "cnt4l",
            reads=[scratch["r_cn"]], writes=[r_cnt])
        for g in range(4):
            C.dma("sync", lambda e, b=b, g=g: e.dma_start(
                out=scg[g], in_=scratch["slotsc"][b * 8 + g:b * 8 + g + 1, :]
                .rearrange("x (p t) -> (x p) t", p=128)), f"scg{g}",
                reads=[scratch["r_sl"]], writes=[r_scg[g]])
        for t in range(16):
            par = t % 2
            st4 = ab.viewf(o_st4[par], 0, 1024)
            for g in range(4):
                C.dma("gpsimd", lambda e, g=g, t=t, b=b:
                      e.indirect_dma_start(
                          out=gst[g], out_offset=None,
                          in_=scratch["og"][b][g],
                          in_offset=IOA(scg[g][:, t:t + 1], 0)),
                      f"cg{g}", reads=[r_scg[g], scratch["r_og"][b][g]],
                      writes=[r_gst[g]])
            C.op("vector", lambda v: v.tensor_tensor(
                gst[0], gst[0], gst[1], ALU.add),
                reads=[r_gst[0], r_gst[1]], writes=[r_gst[0]])
            C.op("vector", lambda v: v.tensor_tensor(
                gst[2], gst[2], gst[3], ALU.add),
                reads=[r_gst[2], r_gst[3]], writes=[r_gst[2]])
            C.op("vector", lambda v, st4=st4: v.tensor_tensor(
                st4, gst[0], gst[2], ALU.add),
                reads=[r_gst[0], r_gst[2]], writes=[r_st4[par]])
            ccol = cntt[:, t:t + 1]
            C.op("vector", lambda v, ccol=ccol: v.tensor_scalar(
                c1s, ccol, 1e-5, None, ALU.max),
                reads=[r_cnt], writes=[r_c1s])
            C.op("vector", lambda v: v.reciprocal(c1s, c1s),
                 reads=[r_c1s], writes=[r_c1s])
            C.op("vector", lambda v, st4=st4: v.tensor_scalar(
                st4, st4, c1s, None, ALU.mult),
                reads=[r_st4[par], r_c1s], writes=[r_st4[par]])
            C.op("vector", lambda v, ccol=ccol: v.tensor_scalar(
                c2s, ccol, 0.0, None, ALU.is_gt),
                reads=[r_cnt], writes=[r_c2s])
            C.op("vector", lambda v, st4=st4: v.tensor_tensor(
                st4, st4, nullbc[:], ALU.subtract),
                reads=[r_st4[par], r_nullbc], writes=[r_st4[par]])
            C.op("vector", lambda v, st4=st4: v.scalar_tensor_tensor(
                st4, st4, c2s, nullbc[:], ALU.mult, ALU.add),
                reads=[r_st4[par], r_c2s, r_nullbc], writes=[r_st4[par]])
            xr = ab.viewf(o_xr4[par], 0, 1024)
            xsl = x_dram.rearrange("(p t) d -> p t d", t=16)[:, t, :]
            ysl = y_dram.rearrange("(p t) d -> p t d", t=16)[:, t, :]
            C.dma("sync", lambda e, xr=xr, xsl=xsl: e.dma_start(
                out=xr, in_=xsl), f"xr4{par}", writes=[r_xr4[par]])
            y0 = ab.viewf(o_y02[par], 0, 1024)
            C.op("vector", lambda v, st4=st4, xr=xr, y0=y0: v.tensor_tensor(
                y0, st4, xr, ALU.add),
                reads=[r_st4[par], r_xr4[par]], writes=[r_y02[par]])
            emit_ln_store(C, A, ZL, y0, r_y02[par],
                          ab.viewf(o_yout[par], 0, 1024), r_yout[par],
                          ysl, f"yst4{par}", par=par)
    C.barrier()
    ab.release()


def emit_ln_store(C, A, Z, y0, r_y0, yout, r_yout, y_dram_slice, st_sem,
                  par=0):
    st = Z["stat2"][par]
    r_st = Z["r_stat2"][par]
    C.op("vector", lambda v: v.tensor_reduce(st[:, 0:1], y0, AX.X, ALU.add),
         reads=[r_y0], writes=[r_st])
    C.op("scalar", lambda e: e.activation(
        Z["sq2"][par][:], y0, AF.Square, accum_out=st[:, 1:2]),
        reads=[r_y0], writes=[Z["r_sq2"][par], r_st])
    C.op("vector", lambda v: v.tensor_scalar(
        st[:, 2:3], st[:, 0:1], 1.0 / D, None, ALU.mult),
        reads=[r_st], writes=[r_st])
    C.op("vector", lambda v: v.tensor_scalar(
        st[:, 3:4], st[:, 2:3], st[:, 2:3], None, ALU.mult),
        reads=[r_st], writes=[r_st])
    C.op("vector", lambda v: v.scalar_tensor_tensor(
        st[:, 4:5], st[:, 1:2], 1.0 / D, st[:, 3:4], ALU.mult, ALU.subtract),
        reads=[r_st], writes=[r_st])
    C.op("scalar", lambda e: e.activation(
        st[:, 5:6], st[:, 4:5], AF.Ln, bias=A.epsb[:], scale=1.0),
        reads=[r_st, A.r_epsb], writes=[r_st])
    C.op("scalar", lambda e: e.activation(
        st[:, 5:6], st[:, 5:6], AF.Exp, scale=-0.5),
        reads=[r_st], writes=[r_st])
    C.op("vector", lambda v: v.tensor_scalar(
        st[:, 6:7], st[:, 2:3], st[:, 5:6], -1.0, ALU.mult, ALU.mult),
        reads=[r_st], writes=[r_st])
    C.op("scalar", lambda e: e.activation(
        yout, y0, AF.Identity, bias=st[:, 6:7], scale=st[:, 5:6]),
        reads=[r_y0, r_st], writes=[r_yout])
    C.dma("sync", lambda e: e.dma_start(out=y_dram_slice, in_=yout),
          st_sem, reads=[r_yout])


def emit_p4_group_batch(C, A, io, scratch, g, b, L):
    ab = C.ab
    x_dram = io["x"][b]
    og = scratch["og"][b][g]
    r_og = scratch["r_og"][b][g]

    qpr, r_qpr = L["qpr"], L["r_qpr"]
    qtok, r_qtok, qsg, r_qsg = L["qtok"], L["r_qtok"], L["qsg"], L["r_qsg"]
    ktok, r_ktok, ksg, r_ksg = L["ktok"], L["r_ktok"], L["ksg"], L["r_ksg"]
    qgfm, r_qgfm = L["qgfm"], L["r_qgfm"]
    kgfm, r_kgfm = L["kgfm"], L["r_kgfm"]
    o_gg, r_gg = L["o_gg"], L["r_gg"]
    wq4, wkv4, wo4 = L["wq4"], L["wkv4"], L["wo4"]
    r_wq4, r_wkv4, r_wo4 = L["r_wq4"], L["r_wkv4"], L["r_wo4"]
    qpfm, r_qpfm = L["qpfm"], L["r_qpfm"]
    kfm4, r_kfm4 = L["kfm4"], L["r_kfm4"]
    o_von4, r_von4 = L["o_von4"], L["r_von4"]
    von4_ctx = L["von4_ctx"]
    ctx4, r_ctx4 = L["ctx4"], L["r_ctx4"]
    o_prA, o_prB = L["o_prA"], L["o_prB"]
    r_prA, r_prB = L["r_prA"], L["r_prB"]
    o_ot, r_ot = L["o_ot"], L["r_ot"]
    d2s4, d2rf4, d2b4 = L["d2s4"], L["d2rf4"], L["d2b4"]
    r_d2s4, r_d2rf4, r_d2b4 = L["r_d2s4"], L["r_d2rf4"], L["r_d2b4"]
    nmsk, r_nmsk = L["nmsk"], L["r_nmsk"]

    rowq = b * 8 + g
    rowk = b * 8 + 4 + g
    C.dma("sync", lambda e, rowq=rowq: e.dma_start(
        out=qpr[:, 0:8],
        in_=scratch["tokd"][rowq:rowq + 1, 0:1024].rearrange(
            "x (p n) -> (x p) n", p=128)), "qpr",
        reads=[scratch["r_lists"]], writes=[r_qpr])
    C.op("vector", lambda v: v.tensor_copy(qtok, qpr[:, 0:8]),
         reads=[r_qpr], writes=[r_qtok])
    C.dma("sync", lambda e, rowq=rowq: e.dma_start(
        out=qsg, in_=scratch["sgcd"][rowq:rowq + 1, 0:1024].rearrange(
            "x (p n) -> (x p) n", p=128)), "qsgl",
        reads=[scratch["r_lists"]], writes=[r_qsg])
    C.dma("sync", lambda e, rowk=rowk: e.dma_start(
        out=qpr[:, 8:12],
        in_=scratch["tokd"][rowk:rowk + 1, 0:512].rearrange(
            "x (p n) -> (x p) n", p=128)), "qpr",
        reads=[scratch["r_lists"], r_qtok], writes=[r_qpr])
    C.op("vector", lambda v: v.tensor_copy(ktok, qpr[:, 8:12]),
         reads=[r_qpr], writes=[r_ktok])
    C.dma("sync", lambda e, rowk=rowk: e.dma_start(
        out=ksg, in_=scratch["sgcd"][rowk:rowk + 1, 0:512].rearrange(
            "x (p n) -> (x p) n", p=128)), "ksgl",
        reads=[scratch["r_lists"]], writes=[r_ksg])

    def gather_tp(n_tiles, tokres, tokap, dstfm, r_dst, dst_elems):
        def dma_tile(nt_):
            par = nt_ % 4
            gg = ab.view(o_gg[par], 0, 1024)
            C.dma("gpsimd", lambda e, nt_=nt_, gg=gg: e.indirect_dma_start(
                out=gg, out_offset=None, in_=x_dram,
                in_offset=IOA(tokap[:, nt_:nt_ + 1], 0)),
                f"gt{par}", reads=[tokres], writes=[r_gg[par]])

        for nt_ in range(min(3, n_tiles)):
            dma_tile(nt_)
        for nt_ in range(n_tiles):
            par = nt_ % 4
            if nt_ + 3 < n_tiles:
                dma_tile(nt_ + 3)
            gg = ab.view(o_gg[par], 0, 1024)
            for c in range(NCH):
                C.op("tensor", lambda e, c=c, gg=gg: e.transpose(
                    A.pbt[:, c * 128:(c + 1) * 128],
                    gg[:, c * 128:(c + 1) * 128], A.ident[:]),
                    reads=[r_gg[par], A.r_ident], writes=[A.rpbt])
            dst = dstfm.rearrange("p (c s) -> p c s", c=NCH)[
                :, :, nt_ * 128:(nt_ + 1) * 128]
            C.op("vector", lambda v, dst=dst: v.tensor_copy(dst, A.pbt[:]),
                 reads=[A.rpbt], writes=[r_dst])
            r_gg[par].readers.append((C.pe, C.pe.n))

    gather_tp(8, r_qtok, qtok, ab.view(L["o_qgfm"], 0, 8192), r_qgfm, 8192)
    gather_tp(4, r_ktok, ktok, ab.view(L["o_kgfm"], 0, 4096), r_kgfm, 4096)

    for qe in range(4):
        for ns in range(2):
            pp = 4 + (qe * 2 + ns) % 2
            for dc in range(NCH):
                C.op("tensor", lambda e, qe=qe, ns=ns, dc=dc, pp=pp:
                     e.matmul(A.pb[pp][:],
                              wq4(dc * 512 + qe * 128, dc * 512 + (qe + 1) * 128),
                              qgfm(dc, ns * 512, (ns + 1) * 512),
                              start=(dc == 0), stop=(dc == NCH - 1)),
                     reads=[r_wq4, r_qgfm], writes=[A.rpb[pp]])
            C.op("scalar", lambda e, qe=qe, ns=ns, pp=pp: e.activation(
                qpfm(qe, ns * 512, (ns + 1) * 512), A.pb[pp][:],
                AF.Identity, scale=1.0),
                reads=[A.rpb[pp]], writes=[r_qpfm])
    for ke in range(4):
        pp = 4 + ke % 2
        for dc in range(NCH):
            C.op("tensor", lambda e, ke=ke, dc=dc, pp=pp: e.matmul(
                A.pb[pp][:],
                wkv4(dc * 1024 + ke * 128, dc * 1024 + (ke + 1) * 128),
                kgfm(dc, 0, 512),
                start=(dc == 0), stop=(dc == NCH - 1)),
                reads=[r_wkv4, r_kgfm], writes=[A.rpb[pp]])
        C.op("scalar", lambda e, ke=ke, pp=pp: e.activation(
            kfm4(ke, 0, 512), A.pb[pp][:], AF.Identity, scale=1.0),
            reads=[A.rpb[pp]], writes=[r_kfm4])
    C.op("vector", lambda v: v.tensor_copy(
        kfm4(0, 0, 4 * 520).rearrange("p (c w) -> p c w", c=4)[
            :, :, 512:513],
        L["nkT"][:, g * 4:(g + 1) * 4].unsqueeze(2)),
        reads=[L["r_nkT"]], writes=[r_kfm4])
    C.op("vector", lambda v: v.memset(
        kfm4(0, 0, 4 * 520).rearrange("p (c w) -> p c w", c=4)[
            :, :, 513:520], 0.0), writes=[r_kfm4])
    for kt in range(4):
        pp = 4 + kt % 2
        for dc in range(NCH):
            C.op("tensor", lambda e, kt=kt, dc=dc, pp=pp: e.matmul(
                A.pb[pp][:], kgfm(dc, kt * 128, (kt + 1) * 128),
                wkv4(dc * 1024 + 512, dc * 1024 + 1024),
                start=(dc == 0), stop=(dc == NCH - 1)),
                reads=[r_wkv4, r_kgfm], writes=[A.rpb[pp]])
        dst = ab.view(o_von4, kt * 520, (kt + 1) * 520).rearrange(
            "p (h w) -> p h w", w=65)[:, :, 0:64]
        src3 = A.pb[pp][:].rearrange("p (h d) -> p h d", d=64)
        C.op("scalar", lambda e, kt=kt, pp=pp, dst=dst, src3=src3:
             e.activation(dst, src3, AF.Identity,
                          scale=ksg[:, kt:kt + 1]),
             reads=[A.rpb[pp], r_ksg], writes=[r_von4])
    C.op("vector", lambda v: v.memset(
        ab.view(o_von4, 0, 4 * 520).rearrange(
            "p (t h w) -> p t h w", h=8, w=65)[:, :, :, 64:65], 1.0),
        writes=[r_von4])
    C.op("vector", lambda v: v.memset(
        ab.view(o_von4, 4 * 520, 5 * 520), 0.0), writes=[r_von4])
    C.dma("gpsimd", lambda e: e.dma_start(
        out=L["nvrow"], in_=io["nullv"][g:g + 1, :]),
        "nv4", reads=[L["r_nvrow"]], writes=[L["r_nvrow"]])
    C.op("vector", lambda v: v.tensor_copy(
        ab.view(o_von4, 4 * 520, 5 * 520)[0:1, :].rearrange(
            "p (h w) -> p h w", w=65)[:, :, 0:64],
        L["nvrow"].rearrange("x (h d) -> x h d", d=64)),
        reads=[L["r_nvrow"]], writes=[r_von4])
    C.op("vector", lambda v: v.memset(
        ab.view(o_von4, 4 * 520, 5 * 520)[0:1, :].rearrange(
            "p (h w) -> p h w", w=65)[:, :, 64:65], 1.0), writes=[r_von4])

    def recip_cast4():
        C.op("vector", lambda v: v.reciprocal(d2rf4[:], d2s4[:]),
             reads=[r_d2s4], writes=[r_d2rf4])
        C.op("vector", lambda v: v.tensor_copy(d2b4[:], d2rf4[:]),
             reads=[r_d2rf4], writes=[r_d2b4])

    def finish4(si):
        hc, q2 = si // 2, si % 2
        sel = A.onecolE if si % 2 == 0 else A.onecolO
        C.op("tensor", lambda e, sel=sel: e.matmul(
            A.pb6[:], sel[:], d2b4[:], start=True, stop=True),
            reads=[A.r_onecolEO, r_d2b4], writes=[A.rpb6])
        C.op("vector", lambda v, hc=hc, q2=q2: v.tensor_tensor(
            ctx4(hc, q2 * 512, (q2 + 1) * 512),
            ctx4(hc, q2 * 512, (q2 + 1) * 512), A.pb6[:], ALU.mult),
            reads=[r_ctx4, A.rpb6], writes=[r_ctx4])

    for hc in range(4):
        for q2 in range(2):
            for kt in range(5):
                sa, sb = (0, 1) if kt % 2 == 0 else (2, 3)
                par = kt % 2
                if kt < 4:
                    lA = kfm4(hc, kt * 128, (kt + 1) * 128)[0:64, :]
                    lB = kfm4(hc, kt * 128, (kt + 1) * 128)[64:128, :]
                    oA, oB = A.pb[sa][:], A.pb[sb][:]
                    npart = 128
                else:
                    lA = kfm4(hc, 512, 520)[0:64, :]
                    lB = kfm4(hc, 512, 520)[64:128, :]
                    oA, oB = A.pb[sa][0:8, :], A.pb[sb][0:8, :]
                    npart = 8
                C.op("tensor", lambda e, lA=lA, oA=oA, hc=hc, q2=q2:
                     e.matmul(oA, lA,
                              qpfm(hc, q2 * 512, (q2 + 1) * 512)[0:64, :],
                              start=True, stop=True, tile_position=(0, 0)),
                     reads=[r_kfm4, r_qpfm], writes=[A.rpb[sa]])
                C.op("tensor", lambda e, lB=lB, oB=oB, hc=hc, q2=q2:
                     e.matmul(oB, lB,
                              qpfm(hc, q2 * 512, (q2 + 1) * 512)[64:128, :],
                              start=True, stop=True, tile_position=(64, 0)),
                     reads=[r_kfm4, r_qpfm], writes=[A.rpb[sb]])
                prA = ab.view(o_prA[par], 0, 512)
                prB = ab.view(o_prB[par], 0, 512)
                if kt < 4:
                    C.op("scalar", lambda e, sa=sa, prA=prA: e.activation(
                        prA, A.pb[sa][:], AF.Exp, scale=0.125),
                        reads=[A.rpb[sa]], writes=[r_prA[par]])
                    C.op("scalar", lambda e, sb=sb, prB=prB: e.activation(
                        prB, A.pb[sb][:], AF.Exp, scale=0.125),
                        reads=[A.rpb[sb]], writes=[r_prB[par]])
                else:
                    C.op("scalar", lambda e, sa=sa, prA=prA: e.activation(
                        prA[0:8, :], A.pb[sa][0:8, :], AF.Exp, scale=0.125,
                        bias=nmsk[0:8, :]),
                        reads=[A.rpb[sa], r_nmsk], writes=[r_prA[par]])
                    C.op("scalar", lambda e, sb=sb, prB=prB: e.activation(
                        prB[0:8, :], A.pb[sb][0:8, :], AF.Exp, scale=0.125,
                        bias=nmsk[0:8, :]),
                        reads=[A.rpb[sb], r_nmsk], writes=[r_prB[par]])
                ctxA = A.pb[4][0:65, :]
                ctxB = A.pb[5][0:65, :]
                if kt < 4:
                    vA = von4_ctx(kt, 2 * hc)
                    vB = von4_ctx(kt, 2 * hc + 1)
                    rhA, rhB = prA, prB
                else:
                    vA = von4_ctx(4, 2 * hc)[0:8, :]
                    vB = von4_ctx(4, 2 * hc + 1)[0:8, :]
                    rhA, rhB = prA[0:8, :], prB[0:8, :]
                C.op("tensor", lambda e, vA=vA, rhA=rhA, kt=kt: e.matmul(
                    A.pb[4][0:65, :], vA, rhA,
                    start=(kt == 0), stop=(kt == 4)),
                    reads=[r_von4, r_prA[par]], writes=[A.rpb[4]])
                C.op("tensor", lambda e, vB=vB, rhB=rhB, kt=kt: e.matmul(
                    A.pb[5][0:65, :], vB, rhB,
                    start=(kt == 0), stop=(kt == 4)),
                    reads=[r_von4, r_prB[par]], writes=[A.rpb[5]])
            C.op("scalar", lambda e, hc=hc, q2=q2: e.activation(
                ctx4(hc, q2 * 512, (q2 + 1) * 512)[0:64, :],
                A.pb[4][0:64, :], AF.Identity, scale=1.0),
                reads=[A.rpb[4]], writes=[r_ctx4])
            C.op("scalar", lambda e, hc=hc, q2=q2: e.activation(
                ctx4(hc, q2 * 512, (q2 + 1) * 512)[64:128, :],
                A.pb[5][0:64, :], AF.Identity, scale=1.0),
                reads=[A.rpb[5]], writes=[r_ctx4])
            si = 2 * hc + q2
            bA = (si % 2) * 64
            C.op("vector", lambda v, bA=bA: v.tensor_copy(
                d2s4[bA:bA + 1, :], A.pb[4][64:65, :]),
                reads=[A.rpb[4]], writes=[r_d2s4])
            C.op("vector", lambda v, bA=bA: v.tensor_copy(
                d2s4[bA + 32:bA + 33, :], A.pb[5][64:65, :]),
                reads=[A.rpb[5]], writes=[r_d2s4])
            if si >= 2:
                finish4(si - 2)
            if si % 2 == 1:
                recip_cast4()
    finish4(6)
    finish4(7)

    for nt_ in range(8):
        par = nt_ % 2
        ot = ab.viewf(o_ot[par], 0, 1024)
        for dd in range(2):
            pp = (4, 0, 5, 2)[(nt_ * 2 + dd) % 4]
            for e4 in range(4):
                C.op("tensor", lambda e, nt_=nt_, dd=dd, e4=e4, pp=pp:
                     e.matmul(A.pb[pp][:],
                              ctx4(e4, nt_ * 128, (nt_ + 1) * 128),
                              wo4(e4 * 1024 + dd * 512,
                                  e4 * 1024 + (dd + 1) * 512),
                              start=(e4 == 0), stop=(e4 == 3)),
                     reads=[r_ctx4, r_wo4], writes=[A.rpb[pp]])
            C.op("scalar", lambda e, nt_=nt_, dd=dd, pp=pp, ot=ot:
                 e.activation(ot[:, dd * 512:(dd + 1) * 512], A.pb[pp][:],
                              AF.Identity, scale=qsg[:, nt_:nt_ + 1]),
                 reads=[A.rpb[pp], r_qsg], writes=[r_ot[par]])
        dst = og[0:1024].rearrange("(p n) d -> p n d", n=8)[:, nt_, :]
        C.dma("sync", lambda e, dst=dst, ot=ot: e.dma_start(out=dst, in_=ot),
              f"og{par}", reads=[r_ot[par]], writes=[r_og])



def new_ctx(nc, ab_elems=84000, af_elems=5200):
    C = Ctx(nc)
    C.ab = Arena(nc, "arena_b", BF16, ab_elems)
    C.af = Arena(nc, "arena_f", F32, af_elems)
    return C


L0_CFG = dict(uniq="", rope=True, causal=True, escale=0.125, xscale=1.0,
              wq=("wq", 0, False), wk=("wk", 0, False), wv=("wv", 0, False),
              wd=("wd", 0, False), bq="bq", bk="bk", bd="bd", bv_row="bv_row")

L1_CFG = dict(uniq="", rope=False, causal=False, escale=1.0 / 32.0, xscale=2.0,
              wq=("wqkv", 0, True), wk=("wqkv", 1024, True),
              wv=("wqkv", 2048, True), wd=("wo", 0, True),
              bq="bq", bk="bk", bd="bd", bv_row="bv_row")


def _decl_stream(nc, name, nb, kind):
    return [nc.dram_tensor(f"{name}{b}", [S, D], F32, kind=kind).ap()
            for b in range(nb)]


def build_phase1_program(nb=BPC):
    nc = bass.Bass("TRN2", target_bir_lowering=False)
    io = {"x": _decl_stream(nc, "x", nb, "ExternalInput"),
          "y": _decl_stream(nc, "y", nb, "ExternalOutput")}
    for w in ("wq", "wk", "wv", "wd"):
        io[w] = nc.dram_tensor(w, [D, D], F32, kind="ExternalInput").ap()
    for bn in ("bq", "bk", "bd"):
        io[bn] = nc.dram_tensor(bn, [128, 8], F32, kind="ExternalInput").ap()
    io["bv_row"] = nc.dram_tensor("bv_row", [1, D], F32,
                                  kind="ExternalInput").ap()
    io["ropec"] = nc.dram_tensor("ropec", [128, S], BF16,
                                 kind="ExternalInput").ap()
    io["ropes"] = nc.dram_tensor("ropes", [128, S], BF16,
                                 kind="ExternalInput").ap()
    C = new_ctx(nc, ab_elems=89600, af_elems=6160)
    A = build_common(C)
    emit_attn_phase(C, A, io, L0_CFG)
    C.replay()
    return nc


def phase1_inputs(xs, Wq, bq, Wk, bk, Wv, bv, Wd, bd):
    c, s = host_rope_tables()
    d = {"wq": np.ascontiguousarray(Wq), "wk": np.ascontiguousarray(Wk),
         "wv": np.ascontiguousarray(Wv), "wd": np.ascontiguousarray(Wd),
         "bq": np.ascontiguousarray(bq.reshape(8, 128).T),
         "bk": np.ascontiguousarray(bk.reshape(8, 128).T),
         "bd": np.ascontiguousarray(bd.reshape(8, 128).T),
         "bv_row": np.ascontiguousarray(bv.reshape(1, D)),
         "ropec": c, "ropes": s}
    for b in range(xs.shape[0]):
        d[f"x{b}"] = np.ascontiguousarray(xs[b])
    return d


def build_phase2_program(nb=BPC):
    nc = bass.Bass("TRN2", target_bir_lowering=False)
    io = {"x": _decl_stream(nc, "x", nb, "ExternalInput"),
          "y": _decl_stream(nc, "y", nb, "ExternalOutput")}
    io["wf"] = nc.dram_tensor("wf", [D, D], F32, kind="ExternalInput").ap()
    io["fscale"] = nc.dram_tensor("fscale", [128, 8], F32,
                                  kind="ExternalInput").ap()
    io["fbias"] = nc.dram_tensor("fbias", [128, 8], F32,
                                 kind="ExternalInput").ap()
    C = new_ctx(nc, ab_elems=76000, af_elems=6160)
    A = build_common(C)
    emit_ffn_phase(C, A, io)
    C.replay()
    return nc


def phase2_inputs(xs, Wf, scale, bias):
    d = {"wf": np.ascontiguousarray(Wf),
         "fscale": np.ascontiguousarray(scale.reshape(8, 128).T),
         "fbias": np.ascontiguousarray(bias.reshape(8, 128).T)}
    for b in range(xs.shape[0]):
        d[f"x{b}"] = np.ascontiguousarray(xs[b])
    return d


def build_phase3_program(nb=BPC):
    nc = bass.Bass("TRN2", target_bir_lowering=False)
    io = {"x": _decl_stream(nc, "x", nb, "ExternalInput"),
          "y": _decl_stream(nc, "y", nb, "ExternalOutput")}
    io["wqkv"] = nc.dram_tensor("wqkv", [3 * D, D], F32,
                                kind="ExternalInput").ap()
    io["wo"] = nc.dram_tensor("wo", [D, D], F32, kind="ExternalInput").ap()
    for bn in ("bq", "bk", "bd"):
        io[bn] = nc.dram_tensor(bn, [128, 8], F32, kind="ExternalInput").ap()
    io["bv_row"] = nc.dram_tensor("bv_row", [1, D], F32,
                                  kind="ExternalInput").ap()
    C = new_ctx(nc, ab_elems=89600, af_elems=6160)
    A = build_common(C)
    emit_attn_phase(C, A, io, L1_CFG)
    C.replay()
    return nc


def phase3_inputs(xs, w_qkv, b_qkv, w_o, b_o):
    d = {"wqkv": np.ascontiguousarray(w_qkv),
         "wo": np.ascontiguousarray(w_o),
         "bq": np.ascontiguousarray(b_qkv[0:1024].reshape(8, 128).T),
         "bk": np.ascontiguousarray(b_qkv[1024:2048].reshape(8, 128).T),
         "bd": np.ascontiguousarray(b_o.reshape(8, 128).T),
         "bv_row": np.ascontiguousarray(b_qkv[2048:3072].reshape(1, D))}
    for b in range(xs.shape[0]):
        d[f"x{b}"] = np.ascontiguousarray(xs[b])
    return d


def build_phase4_program(nb=BPC):
    nc = bass.Bass("TRN2", target_bir_lowering=False)
    io = {"x": _decl_stream(nc, "x", nb, "ExternalInput"),
          "y": _decl_stream(nc, "y", nb, "ExternalOutput")}
    io["wrt"] = nc.dram_tensor("wrt", [128, 64], F32, kind="ExternalInput").ap()
    for g in range(NG):
        io[f"wqT{g}"] = nc.dram_tensor(f"wqT{g}", [D, 512], F32,
                                       kind="ExternalInput").ap()
        io[f"wkvT{g}"] = nc.dram_tensor(f"wkvT{g}", [D, D], F32,
                                        kind="ExternalInput").ap()
        io[f"woutT{g}"] = nc.dram_tensor(f"woutT{g}", [512, D], F32,
                                         kind="ExternalInput").ap()
    io["nullk"] = nc.dram_tensor("nullk", [NG, 512], F32,
                                 kind="ExternalInput").ap()
    io["nullkT"] = nc.dram_tensor("nullkT", [128, 16], F32,
                                  kind="ExternalInput").ap()
    io["nullv"] = nc.dram_tensor("nullv", [NG, 512], F32,
                                 kind="ExternalInput").ap()
    io["ntok_row"] = nc.dram_tensor("ntok_row", [1, D], F32,
                                    kind="ExternalInput").ap()
    io["nqrow"] = nc.dram_tensor("nqrow", [32, 1], F32,
                                 kind="ExternalInput").ap()
    io["bsel"] = nc.dram_tensor("bsel", [32, 4], BF16,
                                kind="ExternalInput").ap()
    scratch = p4_scratch(nc, nb)
    C = new_ctx(nc, ab_elems=90500, af_elems=256)
    A = build_common(C)
    emit_phase4(C, A, io, scratch)
    C.replay()
    return nc


def p4_scratch(nc, nb):
    sc = {}
    sc["og"] = [[nc.dram_tensor(f"og{b}_{g}", [NQQ + 1, D], F32,
                                kind="Internal").ap()
                 for g in range(NG)] for b in range(nb)]
    sc["r_og"] = [[Res(f"og{b}_{g}") for g in range(NG)] for b in range(nb)]
    sc["slotsc"] = nc.dram_tensor("slotsc", [32, S], U32, kind="Internal").ap()
    sc["cntsd"] = nc.dram_tensor("cntsd", [4, S], F32, kind="Internal").ap()
    sc["r_sl"] = Res("slotsc_d")
    sc["r_cn"] = Res("cntsd_d")
    sc["tokd"] = nc.dram_tensor("tokd", [32, S], F32, kind="Internal").ap()
    sc["sgcd"] = nc.dram_tensor("sgcd", [32, S], F32, kind="Internal").ap()
    sc["r_lists"] = Res("lists")
    return sc


def phase4_inputs(xs, w_qr, w_kvr, w_q, w_kv, w_out, null_kv, null_token):
    nb = xs.shape[0]
    wrt = np.zeros((128, 64), np.float32)
    for dc in range(8):
        for gg in range(4):
            wrt[:, dc * 8 + gg] = w_qr[gg, dc * 128:(dc + 1) * 128]
            wrt[:, dc * 8 + 4 + gg] = w_kvr[gg, dc * 128:(dc + 1) * 128]
    nqrow = np.zeros((32, 1), np.float32)
    bsel = np.zeros((32, 4), np.float32)
    for b in range(4):
        for rr in range(8):
            nqrow[b * 8 + rr, 0] = NQQ if rr < 4 else NKK
            if rr < 4:
                bsel[b * 8 + rr, b] = 1.0
    d = {"wrt": wrt, "nqrow": nqrow,
         "bsel": bsel.astype(ml_dtypes.bfloat16),
         "nullk": np.ascontiguousarray(null_kv[0].reshape(NG, 512)),
         "nullkT": np.ascontiguousarray(
             null_kv[0].reshape(NG, 4, 128).transpose(2, 0, 1).reshape(
                 128, 16)),
         "nullv": np.ascontiguousarray(null_kv[1].reshape(NG, 512)),
         "ntok_row": np.ascontiguousarray(null_token.reshape(1, D))}
    for g in range(NG):
        d[f"wqT{g}"] = np.ascontiguousarray(w_q[g].T)
        d[f"wkvT{g}"] = np.ascontiguousarray(w_kv[g].T)
        d[f"woutT{g}"] = np.ascontiguousarray(w_out[g].T)
    for b in range(nb):
        d[f"x{b}"] = np.ascontiguousarray(xs[b])
    return d



_PROG_CACHE = {}


def _programs():
    if "p" not in _PROG_CACHE:
        _PROG_CACHE["p"] = (build_phase1_program(BPC), build_phase2_program(BPC),
                            build_phase3_program(BPC), build_phase4_program(BPC))
    return _PROG_CACHE["p"]


def kernel_multi(L0_attn_hidden_states, L0_attn_Wq, L0_attn_bq, L0_attn_Wk,
           L0_attn_bk, L0_attn_Wv, L0_attn_bv, L0_attn_Wd, L0_attn_bd,
           L0_ffn_weights, L0_ffn_scale, L0_ffn_bias,
           L1_attn_w_qkv, L1_attn_b_qkv, L1_attn_w_o, L1_attn_b_o,
           L1_ffn_w_qr, L1_ffn_w_kvr, L1_ffn_w_q, L1_ffn_w_kv, L1_ffn_w_out,
           L1_ffn_null_kv, L1_ffn_null_token):
    x = np.ascontiguousarray(np.asarray(L0_attn_hidden_states,
                                        dtype=np.float32))
    B = x.shape[0]
    ncores = B // BPC
    p1, p2, p3, p4 = _programs()

    def shard(arr):
        return [arr[c * BPC:(c + 1) * BPC] for c in range(ncores)]

    xs = shard(x)

    def run(prog, mk_inputs, xs_in):
        in_maps = [mk_inputs(xs_in[c]) for c in range(ncores)]
        res = run_bass_kernel_spmd(prog, in_maps, core_ids=list(range(ncores)))
        return [np.stack([np.asarray(res.results[c][f"y{b}"])
                          for b in range(BPC)]) for c in range(ncores)]

    ys = run(p1, lambda xc: phase1_inputs(
        xc, L0_attn_Wq, L0_attn_bq, L0_attn_Wk, L0_attn_bk,
        L0_attn_Wv, L0_attn_bv, L0_attn_Wd, L0_attn_bd), xs)
    ys = run(p2, lambda xc: phase2_inputs(
        xc, L0_ffn_weights, L0_ffn_scale, L0_ffn_bias), ys)
    ys = run(p3, lambda xc: phase3_inputs(
        xc, L1_attn_w_qkv, L1_attn_b_qkv, L1_attn_w_o, L1_attn_b_o), ys)
    ys = run(p4, lambda xc: phase4_inputs(
        xc, L1_ffn_w_qr, L1_ffn_w_kvr, L1_ffn_w_q, L1_ffn_w_kv,
        L1_ffn_w_out, L1_ffn_null_kv, L1_ffn_null_token), ys)
    return np.concatenate(ys, axis=0)



def build_fused_program(nb=BPC):
    nc = bass.Bass("TRN2", target_bir_lowering=False)
    xin = _decl_stream(nc, "x", nb, "ExternalInput")
    s1 = _decl_stream(nc, "s1_", nb, "Internal")
    s2 = _decl_stream(nc, "s2_", nb, "Internal")
    s3 = _decl_stream(nc, "s3_", nb, "Internal")
    yout = _decl_stream(nc, "y", nb, "ExternalOutput")

    io1 = {"x": xin, "y": s1}
    for w in ("wq", "wk", "wv", "wd"):
        io1[w] = nc.dram_tensor(w, [D, D], F32, kind="ExternalInput").ap()
    for bn in ("bq0", "bk0", "bd0"):
        io1[bn] = nc.dram_tensor(bn, [128, 8], F32, kind="ExternalInput").ap()
    io1["bv_row0"] = nc.dram_tensor("bv_row0", [1, D], F32,
                                    kind="ExternalInput").ap()
    io1["ropec"] = nc.dram_tensor("ropec", [128, S], BF16,
                                  kind="ExternalInput").ap()
    io1["ropes"] = nc.dram_tensor("ropes", [128, S], BF16,
                                  kind="ExternalInput").ap()

    io2 = {"x": s1, "y": s2}
    io2["wf"] = nc.dram_tensor("wf", [D, D], F32, kind="ExternalInput").ap()
    io2["fscale"] = nc.dram_tensor("fscale", [128, 8], F32,
                                   kind="ExternalInput").ap()
    io2["fbias"] = nc.dram_tensor("fbias", [128, 8], F32,
                                  kind="ExternalInput").ap()

    io3 = {"x": s2, "y": s3}
    io3["wqkv"] = nc.dram_tensor("wqkv", [3 * D, D], F32,
                                 kind="ExternalInput").ap()
    io3["wo"] = nc.dram_tensor("wo", [D, D], F32, kind="ExternalInput").ap()
    for bn in ("bq1", "bk1", "bd1"):
        io3[bn] = nc.dram_tensor(bn, [128, 8], F32, kind="ExternalInput").ap()
    io3["bv_row1"] = nc.dram_tensor("bv_row1", [1, D], F32,
                                    kind="ExternalInput").ap()

    io4 = {"x": s3, "y": yout}
    io4["wrt"] = nc.dram_tensor("wrt", [128, 64], F32,
                                kind="ExternalInput").ap()
    for g in range(NG):
        io4[f"wqT{g}"] = nc.dram_tensor(f"wqT{g}", [D, 512], F32,
                                        kind="ExternalInput").ap()
        io4[f"wkvT{g}"] = nc.dram_tensor(f"wkvT{g}", [D, D], F32,
                                         kind="ExternalInput").ap()
        io4[f"woutT{g}"] = nc.dram_tensor(f"woutT{g}", [512, D], F32,
                                          kind="ExternalInput").ap()
    io4["nullv"] = nc.dram_tensor("nullv", [NG, 512], F32,
                                  kind="ExternalInput").ap()
    io4["nullkT"] = nc.dram_tensor("nullkT", [128, 16], F32,
                                   kind="ExternalInput").ap()
    io4["ntok_row"] = nc.dram_tensor("ntok_row", [1, D], F32,
                                     kind="ExternalInput").ap()
    io4["nqrow"] = nc.dram_tensor("nqrow", [32, 1], F32,
                                  kind="ExternalInput").ap()
    io4["bsel"] = nc.dram_tensor("bsel", [32, 4], BF16,
                                 kind="ExternalInput").ap()
    scratch = p4_scratch(nc, nb)

    C = new_ctx(nc, ab_elems=89600, af_elems=6160)
    A = build_common(C)
    cfg0 = dict(L0_CFG)
    cfg0.update(uniq="0", bq="bq0", bk="bk0", bd="bd0", bv_row="bv_row0")
    cfg1 = dict(L1_CFG)
    cfg1.update(uniq="1", bq="bq1", bk="bk1", bd="bd1", bv_row="bv_row1")

    for emitfn in (lambda: emit_attn_phase(C, A, io1, cfg0),
                   lambda: emit_ffn_phase(C, A, io2),
                   lambda: emit_attn_phase(C, A, io3, cfg1),
                   lambda: emit_phase4(C, A, io4, scratch)):
        C.ab.mark()
        C.af.mark()
        emitfn()
        C.barrier()
        C.ab.release()
        C.af.release()
    C.replay()
    return nc


def fused_inputs(xs, i):
    d = {}
    p1 = phase1_inputs(xs, i["L0_attn_Wq"], i["L0_attn_bq"], i["L0_attn_Wk"],
                       i["L0_attn_bk"], i["L0_attn_Wv"], i["L0_attn_bv"],
                       i["L0_attn_Wd"], i["L0_attn_bd"])
    for k, v in p1.items():
        d[{"bq": "bq0", "bk": "bk0", "bd": "bd0",
           "bv_row": "bv_row0"}.get(k, k)] = v
    p2 = phase2_inputs(xs, i["L0_ffn_weights"], i["L0_ffn_scale"],
                       i["L0_ffn_bias"])
    for k, v in p2.items():
        if not k.startswith("x"):
            d[k] = v
    p3 = phase3_inputs(xs, i["L1_attn_w_qkv"], i["L1_attn_b_qkv"],
                       i["L1_attn_w_o"], i["L1_attn_b_o"])
    for k, v in p3.items():
        if not k.startswith("x"):
            d[{"bq": "bq1", "bk": "bk1", "bd": "bd1",
               "bv_row": "bv_row1"}.get(k, k)] = v
    p4 = phase4_inputs(xs, i["L1_ffn_w_qr"], i["L1_ffn_w_kvr"],
                       i["L1_ffn_w_q"], i["L1_ffn_w_kv"], i["L1_ffn_w_out"],
                       i["L1_ffn_null_kv"], i["L1_ffn_null_token"])
    for k, v in p4.items():
        if not (k.startswith("x") and k[1:].isdigit()) and k != "nullk":
            d[k] = v
    return d


def kernel(**inputs):
    x = np.ascontiguousarray(np.asarray(inputs["L0_attn_hidden_states"],
                                        dtype=np.float32))
    B = x.shape[0]
    ncores = B // BPC
    if "f" not in _PROG_CACHE:
        _PROG_CACHE["f"] = build_fused_program(BPC)
    prog = _PROG_CACHE["f"]
    in_maps = [fused_inputs(x[c * BPC:(c + 1) * BPC], inputs)
               for c in range(ncores)]
    res = run_bass_kernel_spmd(prog, in_maps, core_ids=list(range(ncores)))
    return np.concatenate(
        [np.stack([np.asarray(res.results[c][f"y{b}"]) for b in range(BPC)])
         for c in range(ncores)], axis=0)


# version 14
# speedup vs baseline: 1.0532x; 1.0532x over previous
import numpy as np
import ml_dtypes
import concourse.bass as bass
import concourse.mybir as mybir
from concourse.bass_utils import run_bass_kernel_spmd

F32 = mybir.dt.float32
BF16 = mybir.dt.bfloat16
U32 = mybir.dt.uint32
U8 = mybir.dt.uint8
AF = mybir.ActivationFunctionType
ALU = mybir.AluOpType
AX = mybir.AxisListType
IOA = bass.IndirectOffsetOnAxis

BPC = 4
S = 2048
D = 1024
NCH = 8
NTT = S // 128
NQS = S // 512
H = 16
DH = 64
NEG = -1.0e30
LN_EPS = 1e-5

P4_DEBUG = False

ENGINES = ("tensor", "vector", "scalar", "gpsimd", "sync")


class Sem:
    def __init__(self, nc, name):
        self.h = nc.alloc_semaphore(name)
        self.n = 0
        self.name = name


class Ctx:
    def __init__(self, nc):
        self.nc = nc
        self.q = {e: [] for e in ENGINES}
        self.pe = Sem(nc, "s_pe")
        self.act = Sem(nc, "s_act")
        self.dve = Sem(nc, "s_dve")
        self.pool = Sem(nc, "s_pool")
        self.ld = Sem(nc, "s_ld")
        self.ld2 = Sem(nc, "s_ld2")
        self.sw = Sem(nc, "s_sw")
        self.esem = {"tensor": self.pe, "vector": self.dve,
                     "scalar": self.act, "gpsimd": self.pool}
        self.dsem = {"sync": self.ld, "scalar": self.ld2, "gpsimd": self.sw}
        self.dma_sems = {}
        self.satisfied = {e: {} for e in ENGINES}
        self.n_ops = 0

    def _wait(self, eng_name, sem, cnt):
        if eng_name == "tensor" and sem is self.pe:
            return
        if cnt is None:
            cnt = sem.n
        if cnt <= 0:
            return
        sat = self.satisfied[eng_name]
        if sat.get(sem.name, 0) >= cnt:
            return
        sat[sem.name] = cnt
        h = sem.h
        self.q[eng_name].append(lambda e: e.wait_ge(h, cnt))
        self.n_ops += 1

    def op(self, eng_name, build, reads=(), writes=(), inc=1):
        sem = self.esem[eng_name]
        for res in reads:
            if res.w is not None:
                self._wait(eng_name, *res.w)
        for res in writes:
            if res.w is not None:
                self._wait(eng_name, *res.w)
            for s, n in res.readers:
                self._wait(eng_name, s, n)
        sem.n += inc
        cnt, k = sem.n, inc

        def f(e, build=build, k=k, h=sem.h):
            build(e).then_inc(h, k)
        self.q[eng_name].append(f)
        self.n_ops += 1
        for res in writes:
            res.w = (sem, cnt)
            res.readers = []
        for res in reads:
            res.readers.append((sem, cnt))
        return cnt

    def dma(self, queue, build, sem_name="d", reads=(), writes=(),
            nowaw=False):
        sem = self.dma_sems.get(sem_name)
        if sem is None:
            sem = Sem(self.nc, "d_" + sem_name)
            self.dma_sems[sem_name] = sem
        for res in reads:
            if res.w is not None:
                self._wait(queue, *res.w)
        for res in writes:
            if not nowaw and res.w is not None:
                self._wait(queue, *res.w)
            for s, n in res.readers:
                self._wait(queue, s, n)
        sem.n += 16
        cnt = sem.n

        def f(e, build=build, h=sem.h):
            build(e).then_inc(h, 16)
        self.q[queue].append(f)
        self.n_ops += 1
        for res in writes:
            res.w = (sem, cnt)
            res.readers = []
        for res in reads:
            res.readers.append((sem, cnt))
        return cnt

    def raw(self, eng_name, fn):
        self.q[eng_name].append(fn)

    def barrier(self):
        sems = [self.pe, self.act, self.dve, self.pool] + \
            list(self.dma_sems.values())
        for e in ENGINES:
            for s in sems:
                self._wait(e, s, s.n)

    def replay(self):
        nc = self.nc
        with nc.Block() as block:
            qs = self.q

            @block.sync
            def _(e):
                for f in qs["sync"]:
                    f(e)

            @block.tensor
            def _(e):
                for f in qs["tensor"]:
                    f(e)

            @block.vector
            def _(e):
                for f in qs["vector"]:
                    f(e)

            @block.scalar
            def _(e):
                for f in qs["scalar"]:
                    f(e)

            @block.gpsimd
            def _(e):
                for f in qs["gpsimd"]:
                    f(e)


class Res:

    __slots__ = ("w", "readers", "name")

    def __init__(self, name=""):
        self.w = None
        self.readers = []
        self.name = name


class Arena:
    def __init__(self, nc, name, dtype, free_elems):
        self.t = nc.alloc_sbuf_tensor(name, [128, free_elems], dtype)
        self.cap = free_elems
        self.off = 0
        self.marks = []

    def take(self, n):
        assert self.off + n <= self.cap, \
            f"arena overflow {self.off}+{n}>{self.cap}"
        o = self.off
        self.off += n
        return o

    def view(self, off, a, b):
        return self.t[:, off + a: off + b]

    def take_f(self, n_f32):
        o = self.take(2 * n_f32)
        assert o % 2 == 0
        return o

    def viewf(self, off, a, b):
        return self.t[:, off + 2 * a: off + 2 * b].bitcast(F32)

    def viewu(self, off, a, b):
        return self.t[:, off + 2 * a: off + 2 * b].bitcast(U32)

    def mark(self):
        self.marks.append(self.off)

    def release(self):
        self.off = self.marks.pop()



def host_rope_tables():
    inv = 1.0 / (10000.0 ** (np.arange(0, DH, 2, dtype=np.float64) / DH))
    pos = np.arange(S, dtype=np.float64)
    fr = np.outer(inv, pos)
    c = np.ones((128, S), np.float32)
    s = np.zeros((128, S), np.float32)
    for base in (0, 64):
        c[base:base + 32] = np.cos(fr)
        s[base:base + 32] = np.sin(fr)
    return c.astype(ml_dtypes.bfloat16), s.astype(ml_dtypes.bfloat16)




def bass_sb_rows(ap, base, count):
    return ap[base:base + count, :]


class Assets:
    pass


def nt(nc, name, p, f, dt):
    return nc.alloc_sbuf_tensor(name, [p, f], dt)


def build_common(C, uniq=""):
    nc = C.nc
    A = Assets()
    A.pbs = [nc.alloc_psum_tensor(f"pbs{i}{uniq}", [128, 1024], F32)
             for i in range(2)]
    A.rpbs = [Res("pbs0"), Res("pbs1")]
    _p4 = nc.alloc_psum_tensor(f"pb4{uniq}", [128, 512], F32)
    _p5 = nc.alloc_psum_tensor(f"pb5{uniq}", [128, 512], F32)
    A.pb = [A.pbs[0][:, 0:512], A.pbs[0][:, 512:1024],
            A.pbs[1][:, 0:512], A.pbs[1][:, 512:1024], _p4, _p5]
    A.rpb = [A.rpbs[0], A.rpbs[0], A.rpbs[1], A.rpbs[1], Res("pb4"),
             Res("pb5")]
    A.pbt = nc.alloc_psum_tensor(f"pbt{uniq}", [128, 1024], BF16)
    A.pb6 = nc.alloc_psum_tensor(f"pb6{uniq}", [128, 512], F32)
    A.rpbt = Res("pbt")
    A.rpb6 = Res("pb6")

    A.ident = nt(nc, "ident" + uniq, 128, 128, BF16)
    A.identf = nt(nc, "identf" + uniq, 128, 128, F32)
    A.masks2 = nt(nc, "masks2" + uniq, 128, 4 * 512, BF16)
    A.onecolA = nt(nc, "onecolA" + uniq, 1, 128, BF16)
    A.onecolB = nt(nc, "onecolB" + uniq, 1, 128, BF16)
    C.ab.mark()
    A.iosc = C.ab.viewf(C.ab.take_f(512), 0, 512)
    A.epsb = nt(nc, "epsb" + uniq, 128, 1, F32)
    A.r_ident = Res("ident")
    A.r_identf = Res("identf")
    A.r_masks2 = Res("masks2")
    A.r_onecol = Res("onecol")
    A.r_iosc = Res("iosc")
    A.r_epsb = Res("epsb")
    C.op("vector", lambda v: v.memset(A.epsb[:], LN_EPS), writes=[A.r_epsb])

    C.op("gpsimd", lambda g: g.iota(
        A.iosc[:], pattern=[[1, 512]], base=0, channel_multiplier=-1,
        allow_small_or_imprecise_dtypes=True), writes=[A.r_iosc])
    C.op("vector", lambda v: v.tensor_scalar(
        A.ident[:], A.iosc[:, 0:128], 0.0, None, ALU.is_equal),
        reads=[A.r_iosc], writes=[A.r_ident])
    C.op("vector", lambda v: v.tensor_scalar(
        A.identf[:], A.iosc[:, 0:128], 0.0, None, ALU.is_equal),
        reads=[A.r_iosc], writes=[A.r_identf])
    for j in range(4):
        C.op("vector", lambda v, j=j: v.tensor_scalar(
            A.masks2[:, j * 512:(j + 1) * 512],
            A.iosc[:], float(j * 128), None,
            ALU.is_ge), reads=[A.r_iosc], writes=[A.r_masks2])
    C.op("gpsimd", lambda g: g.iota(
        A.iosc[0:1, 0:128], pattern=[[1, 128]], base=0, channel_multiplier=0,
        allow_small_or_imprecise_dtypes=True),
        reads=[A.r_masks2, A.r_ident, A.r_identf], writes=[A.r_iosc])
    C.op("vector", lambda v: v.tensor_scalar(
        A.onecolA[:], A.iosc[0:1, 0:128], 64.0, None, ALU.is_lt),
        reads=[A.r_iosc], writes=[A.r_onecol])
    C.op("vector", lambda v: v.tensor_scalar(
        A.onecolB[:], A.iosc[0:1, 0:128], 64.0, None, ALU.is_ge),
        reads=[A.r_iosc], writes=[A.r_onecol])
    A.onecolE = nt(nc, "onecolE" + uniq, 128, 128, BF16)
    A.onecolO = nt(nc, "onecolO" + uniq, 128, 128, BF16)
    A.r_onecolEO = Res("onecolEO")
    for t_ in (A.onecolE, A.onecolO):
        C.op("vector", lambda v, t_=t_: v.memset(t_[:], 0.0),
             reads=[A.r_onecolEO], writes=[A.r_onecolEO])
    for (t_, b_) in ((A.onecolE, 0), (A.onecolO, 64)):
        C.op("vector", lambda v, t_=t_, b_=b_: v.tensor_scalar(
            t_[b_:b_ + 1, :], A.iosc[0:1, 0:128], 64.0, None, ALU.is_lt),
            reads=[A.r_iosc, A.r_onecolEO], writes=[A.r_onecolEO])
        C.op("vector", lambda v, t_=t_, b_=b_: v.tensor_scalar(
            t_[b_ + 32:b_ + 33, :], A.iosc[0:1, 0:128], 64.0, None,
            ALU.is_ge),
            reads=[A.r_iosc, A.r_onecolEO], writes=[A.r_onecolEO])
    C.barrier()
    C.ab.release()
    return A


def bcast_row_from_dram(C, dst_ap, dst_res, row_ap, parts=128, sem="bc"):
    n = row_ap.shape[-1]
    bsrc = row_ap.broadcast_to([parts, n])
    C.dma("sync", lambda e: e.dma_start(out=dst_ap, in_=bsrc),
          sem, writes=[dst_res])



def emit_cast_transpose_in(C, A, x_dram, xfm_dst3, r_xfm, o_xc, r_xc, ab,
                           sem_base):
    nb_ = len(o_xc)

    def dma_tile(t):
        par = t % nb_
        xc = ab.view(o_xc[par], 0, 1024)
        C.dma("gpsimd", lambda e, t=t, xc=xc: e.dma_start(
            out=xc, in_=x_dram[t * 128:(t + 1) * 128, :]),
            f"{sem_base}{par}", writes=[r_xc[par]])

    for t in range(nb_ - 1):
        dma_tile(t)
    for t in range(NTT):
        par = t % nb_
        if t + nb_ - 1 < NTT:
            dma_tile(t + nb_ - 1)
        xc = ab.view(o_xc[par], 0, 1024)
        for c in range(NCH):
            C.op("tensor", lambda e, c=c, xc=xc: e.transpose(
                A.pbt[:, c * 128:(c + 1) * 128], xc[:, c * 128:(c + 1) * 128],
                A.ident[:]), reads=[r_xc[par], A.r_ident], writes=[A.rpbt])
        C.op("vector", lambda v, t=t: v.tensor_copy(xfm_dst3(t), A.pbt[:]),
             reads=[A.rpbt], writes=[r_xfm])
        r_xc[par].readers.append((C.pe, C.pe.n))


def emit_load_weight_direct(C, Z, io, wname, row_off=0):
    for c in range(NCH):
        C.dma("gpsimd", lambda e, c=c: e.dma_start(
            out=Z["wb"](c, 0, 1024),
            in_=io[wname][row_off + c * 128:row_off + (c + 1) * 128, :]),
            "wb", nowaw=(c > 0), writes=[Z["r_wb"]])


def emit_load_weight_T(C, A, Z, io, wname, row_off, ab, sign=False):
    o_ws = Z["o_wscr"]
    r_ws = Z["r_wscr"]
    for rc in range(NCH):
        par = rc % 2
        ws = ab.view(o_ws[par], 0, 1024)
        if sign:
            o_wf = Z["o_wf"][par]
            wfv = ab.viewf(o_wf, 0, 1024)
            C.dma("gpsimd", lambda e, rc=rc, wfv=wfv: e.dma_start(
                out=wfv, in_=io[wname][row_off + rc * 128:
                                       row_off + (rc + 1) * 128, :]),
                f"wf{par}", writes=[Z["r_wf"][par]])
            C.op("vector", lambda v, ws=ws, wfv=wfv: v.tensor_scalar(
                ws, wfv, 0.0, None, ALU.is_ge),
                reads=[Z["r_wf"][par]], writes=[r_ws[par]])
            C.op("vector", lambda v, ws=ws: v.tensor_scalar(
                ws, ws, 2.0, -1.0, ALU.mult, ALU.add),
                reads=[r_ws[par]], writes=[r_ws[par]])
        else:
            C.dma("gpsimd", lambda e, rc=rc, ws=ws: e.dma_start(
                out=ws, in_=io[wname][row_off + rc * 128:
                                      row_off + (rc + 1) * 128, :]),
                f"wscr{par}", writes=[r_ws[par]])
        for ic in range(NCH):
            C.op("tensor", lambda e, ic=ic, ws=ws: e.transpose(
                A.pbt[:, ic * 128:(ic + 1) * 128],
                ws[:, ic * 128:(ic + 1) * 128], A.ident[:]),
                reads=[r_ws[par], A.r_ident], writes=[A.rpbt])
        dst = Z["wb_all"]().rearrange("p (c n) -> p c n", c=NCH)[
            :, :, rc * 128:(rc + 1) * 128]
        C.op("vector", lambda v, dst=dst: v.tensor_copy(dst, A.pbt[:]),
             reads=[A.rpbt], writes=[Z["r_wb"]])



def emit_attn_phase(C, A, io, cfg):
    nc = C.nc
    ab, af = C.ab, C.af
    ab.mark()
    af.mark()

    o_qfm = ab.take(NCH * S)
    o_kfm = ab.take(NCH * S)
    o_von = ab.take(NTT * 1040)
    o_wb = ab.take(NCH * 1024)
    r_qfm, r_kfm, r_von, r_wb = Res("qfm"), Res("kfm"), Res("von"), Res("wb")

    qfm = lambda c, a, b_: ab.view(o_qfm, c * S + a, c * S + b_)
    kfm = lambda c, a, b_: ab.view(o_kfm, c * S + a, c * S + b_)
    wb = lambda c, a, b_: ab.view(o_wb, c * 1024 + a, c * 1024 + b_)

    def von_ctx(kt, head):
        base = kt * 1040 + head * 65
        return ab.view(o_von, base, base + 65)

    def von_dst(kt, vs):
        base = kt * 1040 + vs * 8 * 65
        v = ab.view(o_von, base, base + 8 * 65)
        return v.rearrange("p (h d) -> p h d", d=65)[:, :, 0:64]

    biasq = nt(nc, "biasq" + cfg["uniq"], 128, 8, F32)
    biask = nt(nc, "biask" + cfg["uniq"], 128, 8, F32)
    biasd = nt(nc, "biasd" + cfg["uniq"], 128, 8, F32)
    vbias = ab.viewf(ab.take_f(1024), 0, 1024)
    d2s = ab.viewf(ab.take_f(512), 0, 512)
    d2rf = ab.viewf(ab.take_f(512), 0, 512)
    d2ball = ab.view(ab.take(512), 0, 512)
    stat2 = [af.view(af.take(8), 0, 8), af.view(af.take(8), 0, 8)]
    sq2 = [ab.view(ab.take(1024), 0, 1024), ab.view(ab.take(1024), 0, 1024)]
    r_biasq, r_biask, r_biasd = Res("bq"), Res("bk"), Res("bd")
    r_vbias = Res("vb")
    r_d2s, r_d2rf, r_d2ball = Res("d2s"), Res("d2rf"), Res("d2ball")
    r_stat2 = [Res("stat0"), Res("stat1")]
    r_sq2 = [Res("sq0"), Res("sq1")]
    C.op("vector", lambda v: v.memset(d2s[:], 1.0), writes=[r_d2s])

    C.dma("sync", lambda e: e.dma_start(out=biasq[:], in_=io[cfg["bq"]]),
          "cbq", writes=[r_biasq])
    C.dma("sync", lambda e: e.dma_start(out=biask[:], in_=io[cfg["bk"]]),
          "cbk", writes=[r_biask])
    C.dma("sync", lambda e: e.dma_start(out=biasd[:], in_=io[cfg["bd"]]),
          "cbd", writes=[r_biasd])
    bcast_row_from_dram(C, vbias, r_vbias, io[cfg["bv_row"]], sem="cvb")

    if cfg["rope"]:
        ropeC = ab.view(ab.take(S), 0, S)
        ropeS = ab.view(ab.take(S), 0, S)
        r_ropeC, r_ropeS = Res("rC"), Res("rS")
        C.dma("sync", lambda e: e.dma_start(out=ropeC, in_=io["ropec"]),
              "crc", writes=[r_ropeC])
        C.dma("sync", lambda e: e.dma_start(out=ropeS, in_=io["ropes"]),
              "crs", writes=[r_ropeS])
    else:
        ropeC = ropeS = r_ropeC = r_ropeS = None

    vall = ab.view(o_von, 0, NTT * 1040)
    ones_dst = vall.rearrange("p (t h d) -> p t h d", h=16, d=65)[:, :, :, 64:65]
    C.op("vector", lambda v: v.memset(ones_dst, 1.0), writes=[r_von])

    r_pr = [Res("pr0"), Res("pr1")]
    r_ctx = Res("ctx")
    r_att = Res("att")
    r_aout = [Res("aout0"), Res("aout1")]
    r_y02 = [Res("y0a"), Res("y0b")]
    r_yout = [Res("yout0"), Res("yout1")]
    r_xres = [Res("xres0"), Res("xres1")]
    o_y02 = [af.take(1024), af.take(1024)]
    o_yout = [af.take(1024), af.take(1024)]
    o_xres = [af.take(1024), af.take(1024)]

    Z = dict(qfm=qfm, kfm=kfm, wb=wb, von_ctx=von_ctx, von_dst=von_dst,
             o_von=o_von, r_qfm=r_qfm, r_kfm=r_kfm, r_von=r_von, r_wb=r_wb,
             wb_all=lambda: ab.view(o_wb, 0, NCH * 1024),
             biasq=biasq, biask=biask, biasd=biasd, vbias=vbias,
             r_biasq=r_biasq, r_biask=r_biask, r_biasd=r_biasd,
             r_vbias=r_vbias,
             ropeC=ropeC, ropeS=ropeS, r_ropeC=r_ropeC, r_ropeS=r_ropeS,
             r_pr=r_pr, r_ctx=r_ctx,
             r_att=r_att, r_aout=r_aout,
             d2s=d2s, d2rf=d2rf, d2ball=d2ball,
             r_d2s=r_d2s, r_d2rf=r_d2rf, r_d2ball=r_d2ball,
             stat2=stat2, r_stat2=r_stat2, sq2=sq2, r_sq2=r_sq2,
             o_y02=o_y02, r_y02=r_y02, o_yout=o_yout, r_yout=r_yout,
             o_xres=o_xres, r_xres=r_xres)

    for b in range(len(io["x"])):
        emit_attn_batch(C, A, io, cfg, b, Z)
    C.barrier()
    ab.release()
    af.release()


def emit_attn_batch(C, A, io, cfg, b, Z):
    ab, af = C.ab, C.af
    x_dram = io["x"][b]
    y_dram = io["y"][b]

    ab.mark()
    o_xfm = ab.take(NCH * S)
    o_xc = [ab.take(1024) for _ in range(3)]
    r_xfm = Res("xfm")
    r_xc = [Res(f"xc{i}") for i in range(3)]
    xfm = lambda c, a, b_: ab.view(o_xfm, c * S + a, c * S + b_)
    af.mark()
    any_t = any(cfg[k][2] for k in ("wq", "wk", "wv", "wd"))
    o_wscr = [ab.take(1024), ab.take(1024)] if any_t else None
    r_wscr = [Res("ws0"), Res("ws1")]
    Zb = dict(Z)
    Zb["o_wscr"], Zb["r_wscr"] = o_wscr, r_wscr

    def xfm_dst3(t):
        return ab.view(o_xfm, 0, NCH * S).rearrange(
            "p (c s) -> p c s", c=NCH)[:, :, t * 128:(t + 1) * 128]

    emit_cast_transpose_in(C, A, x_dram, xfm_dst3, r_xfm, o_xc, r_xc, ab, "xc")

    def load_w(key):
        wname, row_off, transposed = cfg[key]
        if transposed:
            emit_load_weight_T(C, A, Zb, io, wname, row_off, ab)
        else:
            emit_load_weight_direct(C, Zb, io, wname, row_off)

    def proj_fm(dstv, r_dst, bias, r_bias):
        for fc in range(NCH):
            for ts in range(NQS):
                pp = (4, 0, 5, 2)[(fc * NQS + ts) % 4]
                for ec in range(NCH):
                    C.op("tensor", lambda e, fc=fc, ts=ts, ec=ec, pp=pp:
                         e.matmul(A.pb[pp][:],
                                  Zb["wb"](ec, fc * 128, (fc + 1) * 128),
                                  xfm(ec, ts * 512, (ts + 1) * 512),
                                  start=(ec == 0), stop=(ec == NCH - 1)),
                         reads=[Zb["r_wb"], r_xfm], writes=[A.rpb[pp]])
                C.op("scalar", lambda e, fc=fc, ts=ts, pp=pp, dstv=dstv,
                     bias=bias: e.activation(
                         dstv(fc, ts * 512, (ts + 1) * 512), A.pb[pp][:],
                         AF.Identity, bias=bias[:, fc:fc + 1], scale=1.0),
                     reads=[A.rpb[pp], r_bias], writes=[r_dst])

    load_w("wq")
    proj_fm(Zb["qfm"], Zb["r_qfm"], Zb["biasq"], Zb["r_biasq"])
    load_w("wk")
    proj_fm(Zb["kfm"], Zb["r_kfm"], Zb["biask"], Zb["r_biask"])
    load_w("wv")
    for kt in range(NTT):
        for vs in range(2):
            pp = (4, 0, 5, 2)[(kt * 2 + vs) % 4]
            for ec in range(NCH):
                C.op("tensor", lambda e, kt=kt, vs=vs, ec=ec, pp=pp:
                     e.matmul(A.pb[pp][:],
                              xfm(ec, kt * 128, (kt + 1) * 128),
                              Zb["wb"](ec, vs * 512, (vs + 1) * 512),
                              start=(ec == 0), stop=(ec == NCH - 1)),
                     reads=[Zb["r_wb"], r_xfm], writes=[A.rpb[pp]])
            src3 = A.pb[pp][:].rearrange("p (h d) -> p h d", d=64)
            C.op("vector", lambda v, kt=kt, vs=vs, src3=src3: v.tensor_tensor(
                Zb["von_dst"](kt, vs), src3,
                Zb["vbias"][:, vs * 512:(vs + 1) * 512].rearrange(
                    "p (h d) -> p h d", d=64), ALU.add),
                reads=[A.rpb[pp], Zb["r_vbias"]], writes=[Zb["r_von"]])

    C.barrier()
    ab.release()
    af.release()
    ab.mark()
    o_sw = ab.take(S)
    o_rt = ab.take(S)
    r_sw, r_rt = Res("sw"), Res("rt")
    o_pr = [ab.take(1024), ab.take(1024)]
    o_ctx = ab.take(NCH * 512)
    o_att = ab.take(4 * 1024)
    o_aout = [ab.take(512), ab.take(512)]
    Zb["o_pr"] = o_pr
    Zb["o_ctx"], Zb["o_att"], Zb["o_aout"] = o_ctx, o_att, o_aout
    Zb["ctxv"] = lambda c: ab.view(o_ctx, c * 512, (c + 1) * 512)

    if cfg["rope"]:
        def rope(fmv, r_fm):
            for c in range(NCH):
                sw = ab.view(o_sw, 0, S)
                rt = ab.view(o_rt, 0, S)
                fv = fmv(c, 0, S)
                for (da, sa) in ((0, 16), (16, 0), (64, 80), (80, 64)):
                    C.dma("sync", lambda e, fv=fv, da=da, sa=sa, sw=sw:
                          e.dma_start(out=sw[da:da + 16, :],
                                      in_=fv[sa:sa + 16, :]), "ropesw",
                          nowaw=(da != 0), reads=[r_fm], writes=[r_sw])
                C.op("vector", lambda v, sw=sw, rt=rt: v.tensor_tensor(
                    rt, sw, Zb["ropeS"], ALU.mult),
                    reads=[r_sw, Zb["r_ropeS"]], writes=[r_rt])
                C.op("vector", lambda v, c=c, sw=sw, fv=fv: v.tensor_tensor(
                    sw, fv, Zb["ropeC"], ALU.mult),
                    reads=[r_fm, Zb["r_ropeC"]], writes=[r_sw])
                C.op("vector", lambda v, c=c, sw=sw, rt=rt, fv=fv:
                     v.tensor_tensor(fv, sw, rt, ALU.subtract),
                     reads=[r_sw, r_rt], writes=[r_fm])
        rope(Zb["qfm"], Zb["r_qfm"])
        rope(Zb["kfm"], Zb["r_kfm"])

    load_w("wd")

    for qs in range(NQS):
        emit_attn_qs(C, A, cfg, b, qs, Zb, x_dram, y_dram)
    C.barrier()
    ab.release()


def emit_attn_qs(C, A, cfg, b, qs, Z, x_dram, y_dram):
    ab, af = C.ab, C.af
    causal = cfg["causal"]
    kmax = 4 * qs + 4 if causal else NTT
    esc = cfg["escale"]

    def emit_scores(hc, kt):
        par = kt % 2
        C.op("tensor", lambda e, hc=hc, qs=qs, kt=kt, par=par:
             e.matmul(A.pbs[par][:, 0:512],
                      Z["kfm"](hc, kt * 128, (kt + 1) * 128)[0:64, :],
                      Z["qfm"](hc, qs * 512, (qs + 1) * 512)[0:64, :],
                      start=True, stop=True, tile_position=(0, 0)),
             reads=[Z["r_kfm"], Z["r_qfm"]], writes=[A.rpbs[par]])
        C.op("tensor", lambda e, hc=hc, qs=qs, kt=kt, par=par:
             e.matmul(A.pbs[par][:, 512:1024],
                      Z["kfm"](hc, kt * 128, (kt + 1) * 128)[64:128, :],
                      Z["qfm"](hc, qs * 512, (qs + 1) * 512)[64:128, :],
                      start=True, stop=True, tile_position=(64, 0)),
             reads=[Z["r_kfm"], Z["r_qfm"]], writes=[A.rpbs[par]])
        pr = ab.view(Z["o_pr"][par], 0, 1024)
        C.op("scalar", lambda e, par=par, pr=pr: e.activation(
            pr, A.pbs[par][:], AF.Exp, scale=esc),
            reads=[A.rpbs[par]], writes=[Z["r_pr"][par]])
        if causal and kt >= 4 * qs:
            j = kt - 4 * qs
            mj = A.masks2[:, j * 512:(j + 1) * 512]
            for h in range(2):
                C.op("vector", lambda v, pr=pr, mj=mj, h=h: v.tensor_tensor(
                    pr[:, h * 512:(h + 1) * 512], pr[:, h * 512:(h + 1) * 512],
                    mj, ALU.mult),
                    reads=[Z["r_pr"][par], A.r_masks2],
                    writes=[Z["r_pr"][par]])

    def emit_ctx(hc, kt):
        par = kt % 2
        pr = ab.view(Z["o_pr"][par], 0, 1024)
        C.op("tensor", lambda e, hc=hc, kt=kt, pr=pr, km=kmax:
             e.matmul(A.pb[4][0:65, :], Z["von_ctx"](kt, 2 * hc),
                      pr[:, 0:512], start=(kt == 0), stop=(kt == km - 1)),
             reads=[Z["r_von"], Z["r_pr"][par]], writes=[A.rpb[4]])
        C.op("tensor", lambda e, hc=hc, kt=kt, pr=pr, km=kmax:
             e.matmul(A.pb[5][0:65, :], Z["von_ctx"](kt, 2 * hc + 1),
                      pr[:, 512:1024], start=(kt == 0), stop=(kt == km - 1)),
             reads=[Z["r_von"], Z["r_pr"][par]], writes=[A.rpb[5]])

    def evac_ctx(hc):
        bA = (hc % 2) * 64
        ctxc = Z["ctxv"](hc)
        C.op("scalar", lambda e, ctxc=ctxc: e.activation(
            ctxc[0:64, :], A.pb[4][0:64, :], AF.Identity, scale=1.0),
            reads=[A.rpb[4]], writes=[Z["r_ctx"]])
        C.op("scalar", lambda e, ctxc=ctxc: e.activation(
            ctxc[64:128, :], A.pb[5][0:64, :], AF.Identity, scale=1.0),
            reads=[A.rpb[5]], writes=[Z["r_ctx"]])
        C.op("vector", lambda v, bA=bA: v.tensor_copy(
            Z["d2s"][bA:bA + 1, :], A.pb[4][64:65, :]),
            reads=[A.rpb[4]], writes=[Z["r_d2s"]])
        C.op("vector", lambda v, bA=bA: v.tensor_copy(
            Z["d2s"][bA + 32:bA + 33, :], A.pb[5][64:65, :]),
            reads=[A.rpb[5]], writes=[Z["r_d2s"]])

    def recip_cast():
        C.op("vector", lambda v: v.reciprocal(Z["d2rf"][:], Z["d2s"][:]),
             reads=[Z["r_d2s"]], writes=[Z["r_d2rf"]])
        C.op("vector", lambda v: v.tensor_copy(Z["d2ball"][:], Z["d2rf"][:]),
             reads=[Z["r_d2rf"]], writes=[Z["r_d2ball"]])

    def finish_norm(hc):
        sel = A.onecolE if hc % 2 == 0 else A.onecolO
        ctxc = Z["ctxv"](hc)
        C.op("tensor", lambda e, sel=sel: e.matmul(
            A.pb6[:], sel[:], Z["d2ball"][:], start=True, stop=True),
            reads=[A.r_onecolEO, Z["r_d2ball"]], writes=[A.rpb6])
        C.op("vector", lambda v, ctxc=ctxc: v.tensor_tensor(
            ctxc, ctxc, A.pb6[:], ALU.mult),
            reads=[Z["r_ctx"], A.rpb6], writes=[Z["r_ctx"]])

    for hc in range(NCH):
        emit_scores(hc, 0)
        for kt in range(1, kmax):
            emit_scores(hc, kt)
            emit_ctx(hc, kt - 1)
        emit_ctx(hc, kmax - 1)
        evac_ctx(hc)
        if hc >= 2:
            finish_norm(hc - 2)
        if hc % 2 == 1:
            recip_cast()
    finish_norm(NCH - 2)
    finish_norm(NCH - 1)

    def tp_out(dc):
        bi = dc % 2
        aout = ab.view(Z["o_aout"][bi], 0, 512)
        for j in range(4):
            C.op("tensor", lambda e, j=j, aout=aout: e.transpose(
                A.pbt[:, j * 128:(j + 1) * 128],
                aout[:, j * 128:(j + 1) * 128], A.ident[:]),
                reads=[Z["r_aout"][bi], A.r_ident], writes=[A.rpbt])
        dst = ab.view(Z["o_att"], 0, 4 * 1024).rearrange(
            "p (j d) -> p j d", j=4)[:, :, dc * 128:(dc + 1) * 128]
        src = A.pbt[:, 0:512].rearrange("p (j t) -> p j t", j=4)
        C.op("vector", lambda v, dst=dst, src=src: v.tensor_copy(dst, src),
             reads=[A.rpbt], writes=[Z["r_att"]])

    for dc in range(NCH):
        pp = (4, 0, 5, 2)[dc % 4]
        bi = dc % 2
        for ec in range(NCH):
            C.op("tensor", lambda e, dc=dc, ec=ec, pp=pp: e.matmul(
                A.pb[pp][:], Z["wb"](ec, dc * 128, (dc + 1) * 128),
                Z["ctxv"](ec), start=(ec == 0), stop=(ec == NCH - 1)),
                reads=[Z["r_wb"], Z["r_ctx"]], writes=[A.rpb[pp]])
        if dc > 0:
            tp_out(dc - 1)
        aout = ab.view(Z["o_aout"][bi], 0, 512)
        C.op("scalar", lambda e, dc=dc, pp=pp, bi=bi, aout=aout: e.activation(
            aout, A.pb[pp][:], AF.Identity,
            bias=Z["biasd"][:, dc:dc + 1], scale=1.0),
            reads=[A.rpb[pp], Z["r_biasd"]], writes=[Z["r_aout"][bi]])
    tp_out(NCH - 1)

    for j in range(4):
        t = qs * 4 + j
        par = t % 2
        xres = af.view(Z["o_xres"][par], 0, 1024)
        C.dma("sync", lambda e, t=t, xres=xres: e.dma_start(
            out=xres, in_=x_dram[t * 128:(t + 1) * 128, :]), f"xr{par}",
            writes=[Z["r_xres"][par]])
        att_t = ab.view(Z["o_att"], j * 1024, (j + 1) * 1024)
        y0 = af.view(Z["o_y02"][par], 0, 1024)
        emit_res_ln(C, A, Z, y0, Z["r_y02"][par],
                    [(att_t, Z["r_att"], 1.0)], xres, Z["r_xres"][par],
                    cfg["xscale"], Z["o_yout"][par], Z["r_yout"][par],
                    y_dram[t * 128:(t + 1) * 128, :], f"yst{par}", par=par)



def emit_ffn_phase(C, A, io):
    nc = C.nc
    ab, af = C.ab, C.af

    o_wb = ab.take(NCH * 1024)
    r_wb = Res("wb2")
    wb = lambda c, a, b_: ab.view(o_wb, c * 1024 + a, c * 1024 + b_)
    fscale = nt(nc, "fscale_sb", 128, 8, F32)
    fbias = nt(nc, "fbias_sb", 128, 8, F32)
    stat2 = [af.view(af.take(8), 0, 8), af.view(af.take(8), 0, 8)]
    sq2 = [ab.view(ab.take(1024), 0, 1024), ab.view(ab.take(1024), 0, 1024)]
    r_fscale, r_fbias = Res("fscale"), Res("fbias")
    r_stat2 = [Res("f2st0"), Res("f2st1")]
    r_sq2 = [Res("f2sq0"), Res("f2sq1")]
    C.dma("sync", lambda e: e.dma_start(out=fscale[:], in_=io["fscale"]),
          "cfs", writes=[r_fscale])
    C.dma("sync", lambda e: e.dma_start(out=fbias[:], in_=io["fbias"]),
          "cfb", writes=[r_fbias])

    ab.mark()
    af.mark()
    o_wscr = [ab.take(1024), ab.take(1024)]
    o_wf = [ab.take_f(1024), ab.take_f(1024)]
    Z = dict(o_wscr=o_wscr, r_wscr=[Res("ws0"), Res("ws1")],
             o_wf=o_wf, r_wf=[Res("wf0"), Res("wf1")],
             wb=wb, r_wb=r_wb, wb_all=lambda: ab.view(o_wb, 0, NCH * 1024),
             stat2=stat2, r_stat2=r_stat2, sq2=sq2, r_sq2=r_sq2)
    emit_load_weight_T(C, A, Z, io, "wf", 0, ab, sign=True)
    C.barrier()
    ab.release()
    af.release()

    ab.mark()
    af.mark()
    o_xfm = ab.take(NCH * S)
    o_xfl = ab.take(NCH * S)
    o_xc = [ab.take(1024), ab.take(1024)]
    o_att = ab.take_f(4 * 1024)
    o_aout = [ab.take_f(512), ab.take_f(512)]
    o_y02 = [af.take(1024), af.take(1024)]
    o_yout = [af.take(1024), af.take(1024)]
    o_xres = [af.take(1024), af.take(1024)]
    r_xfm = Res("xfm2")
    r_xfl = Res("xfl2")
    r_xc = [Res("xc0"), Res("xc1")]
    r_att = Res("att2")
    r_aout = [Res("ao0"), Res("ao1")]
    r_y02 = [Res("y02a"), Res("y02b")]
    r_yout = [Res("yo0"), Res("yo1")]
    r_xres = [Res("xr0"), Res("xr1")]
    xfm = lambda c, a, b_: ab.view(o_xfm, c * S + a, c * S + b_)
    xfl = lambda c, a, b_: ab.view(o_xfl, c * S + a, c * S + b_)
    o_xif = ab.take_f(1024)
    o_xlf = ab.take_f(1024)
    r_xif, r_xlf = Res("xif"), Res("xlf")
    o_xcl = [ab.take(1024), ab.take(1024)]
    r_xcl = [Res("xcl0"), Res("xcl1")]

    for b in range(len(io["x"])):
        x_dram = io["x"][b]
        y_dram = io["y"][b]
        for t in range(NTT):
            par = t % 2
            xc = ab.view(o_xc[par], 0, 1024)
            xcl = ab.view(o_xcl[par], 0, 1024)
            xif = ab.viewf(o_xif, 0, 1024)
            xlf = ab.viewf(o_xlf, 0, 1024)
            C.dma("sync", lambda e, t=t, x_dram=x_dram: e.dma_start(
                out=xif, in_=x_dram[t * 128:(t + 1) * 128, :]), "x2in",
                writes=[r_xif])
            C.op("scalar", lambda e, xc=xc: e.activation(
                xc, xif, AF.Identity, scale=1.0),
                reads=[r_xif], writes=[r_xc[par]])
            C.op("scalar", lambda e, xc=xc: e.activation(
                xlf, xc, AF.Identity, scale=1.0),
                reads=[r_xc[par]], writes=[r_xlf])
            C.op("vector", lambda v: v.tensor_tensor(
                xlf, xif, xlf, ALU.subtract),
                reads=[r_xif, r_xlf], writes=[r_xlf])
            C.op("scalar", lambda e, xcl=xcl: e.activation(
                xcl, xlf, AF.Identity, scale=1.0),
                reads=[r_xlf], writes=[r_xcl[par]])
            for (src_t, rs, dsto, rd) in ((xc, r_xc[par], o_xfm, r_xfm),
                                          (xcl, r_xcl[par], o_xfl, r_xfl)):
                for c in range(NCH):
                    C.op("tensor", lambda e, c=c, src_t=src_t: e.transpose(
                        A.pbt[:, c * 128:(c + 1) * 128],
                        src_t[:, c * 128:(c + 1) * 128], A.ident[:]),
                        reads=[rs, A.r_ident], writes=[A.rpbt])
                dst = ab.view(dsto, 0, NCH * S).rearrange(
                    "p (c s) -> p c s", c=NCH)[:, :, t * 128:(t + 1) * 128]
                C.op("vector", lambda v, dst=dst: v.tensor_copy(dst, A.pbt[:]),
                     reads=[A.rpbt], writes=[rd])
        for qs in range(NQS):
            for oc in range(NCH):
                pp = (4, 0, 5, 2)[oc % 4]

                def tp_oc(dc):
                    aoutp = ab.viewf(o_aout[dc % 2], 0, 512)
                    for j in range(4):
                        C.op("tensor", lambda e, j=j, aoutp=aoutp:
                             e.transpose(
                                 A.pb6[:, j * 128:(j + 1) * 128],
                                 aoutp[:, j * 128:(j + 1) * 128],
                                 A.identf[:]),
                             reads=[r_aout[dc % 2], A.r_identf],
                             writes=[A.rpb6])
                    dst = ab.viewf(o_att, 0, 4 * 1024).rearrange(
                        "p (j d) -> p j d", j=4)[:, :, dc * 128:(dc + 1) * 128]
                    src = A.pb6[:].rearrange("p (j t) -> p j t", j=4)
                    C.op("vector", lambda v, dst=dst, src=src:
                         v.tensor_copy(dst, src), reads=[A.rpb6],
                         writes=[r_att])

                for ic in range(NCH):
                    C.op("tensor", lambda e, oc=oc, ic=ic, pp=pp, qs=qs:
                         e.matmul(A.pb[pp][:],
                                  wb(ic, oc * 128, (oc + 1) * 128),
                                  xfm(ic, qs * 512, (qs + 1) * 512),
                                  start=(ic == 0), stop=False),
                         reads=[r_wb, r_xfm], writes=[A.rpb[pp]])
                for ic in range(NCH):
                    C.op("tensor", lambda e, oc=oc, ic=ic, pp=pp, qs=qs:
                         e.matmul(A.pb[pp][:],
                                  wb(ic, oc * 128, (oc + 1) * 128),
                                  xfl(ic, qs * 512, (qs + 1) * 512),
                                  start=False, stop=(ic == NCH - 1)),
                         reads=[r_wb, r_xfl], writes=[A.rpb[pp]])
                if oc > 0:
                    tp_oc(oc - 1)
                aout = ab.viewf(o_aout[oc % 2], 0, 512)
                C.op("scalar", lambda e, oc=oc, pp=pp, aout=aout:
                     e.activation(aout, A.pb[pp][:], AF.Identity,
                                  bias=fbias[:, oc:oc + 1],
                                  scale=fscale[:, oc:oc + 1]),
                     reads=[A.rpb[pp], r_fscale, r_fbias],
                     writes=[r_aout[oc % 2]])
                C.op("vector", lambda v, aout=aout: v.tensor_scalar(
                    aout, aout, 100.0, -100.0, ALU.min, ALU.max),
                    reads=[r_aout[oc % 2]], writes=[r_aout[oc % 2]])
                if oc == NCH - 1:
                    tp_oc(oc)
            for j in range(4):
                t = qs * 4 + j
                par = t % 2
                xres = af.view(o_xres[par], 0, 1024)
                C.dma("sync", lambda e, t=t, xres=xres, x_dram=x_dram:
                      e.dma_start(
                          out=xres, in_=x_dram[t * 128:(t + 1) * 128, :]),
                      f"xr{par}", writes=[r_xres[par]])
                att_t = ab.viewf(o_att, j * 1024, (j + 1) * 1024)
                y0 = af.view(o_y02[par], 0, 1024)
                emit_res_ln(C, A, Z, y0, r_y02[par], [(att_t, r_att, 1.0)],
                            xres, r_xres[par], 1.0, o_yout[par], r_yout[par],
                            y_dram[t * 128:(t + 1) * 128, :], f"yst{par}",
                            par=par)
                if P4_DEBUG and t < 2:
                    dy0 = C.nc.dram_tensor(f"dy0_{b}_{t}", [128, 1024], F32,
                                           kind="Internal").ap()
                    C.dma("sync", lambda e, dy0=dy0: e.dma_start(
                        out=dy0, in_=y0), "dbg4", nowaw=True, reads=[r_y0])
        C.barrier()
    ab.release()
    af.release()


def emit_res_ln(C, A, Z, y0, r_y0, terms, xres, r_xres, xscale, o_yout,
                r_yout, y_dram_slice, st_sem="yst", par=0):
    af = C.af
    st_, r_st = Z["stat2"][par], Z["r_stat2"][par]
    sq_, r_sq_ = Z["sq2"][par], Z["r_sq2"][par]
    (t_ap, t_res, t_scale) = terms[0]
    assert t_scale == 1.0
    if xscale == 1.0:
        C.op("vector", lambda v: v.tensor_tensor(y0, t_ap, xres, ALU.add),
             reads=[t_res, r_xres], writes=[r_y0])
    else:
        C.op("vector", lambda v: v.scalar_tensor_tensor(
            y0, xres, float(xscale), t_ap, ALU.mult, ALU.add),
            reads=[t_res, r_xres], writes=[r_y0])
    st = st_
    C.op("vector", lambda v: v.tensor_reduce(st[:, 0:1], y0, AX.X, ALU.add),
         reads=[r_y0], writes=[r_st])
    C.op("scalar", lambda e: e.activation(
        sq_, y0, AF.Square, accum_out=st[:, 1:2]),
        reads=[r_y0], writes=[r_sq_, r_st])
    C.op("vector", lambda v: v.tensor_scalar(
        st[:, 2:3], st[:, 0:1], 1.0 / D, None, ALU.mult),
        reads=[r_st], writes=[r_st])
    C.op("vector", lambda v: v.tensor_scalar(
        st[:, 3:4], st[:, 2:3], st[:, 2:3], None, ALU.mult),
        reads=[r_st], writes=[r_st])
    C.op("vector", lambda v: v.scalar_tensor_tensor(
        st[:, 4:5], st[:, 1:2], 1.0 / D, st[:, 3:4], ALU.mult, ALU.subtract),
        reads=[r_st], writes=[r_st])
    C.op("scalar", lambda e: e.activation(
        st[:, 5:6], st[:, 4:5], AF.Ln, bias=A.epsb[:], scale=1.0),
        reads=[r_st, A.r_epsb], writes=[r_st])
    C.op("scalar", lambda e: e.activation(
        st[:, 5:6], st[:, 5:6], AF.Exp, scale=-0.5),
        reads=[r_st], writes=[r_st])
    C.op("vector", lambda v: v.tensor_scalar(
        st[:, 6:7], st[:, 2:3], st[:, 5:6], -1.0, ALU.mult, ALU.mult),
        reads=[r_st], writes=[r_st])
    yout = af.view(o_yout, 0, 1024)
    C.op("scalar", lambda e: e.activation(
        yout, y0, AF.Identity, bias=st[:, 6:7], scale=st[:, 5:6]),
        reads=[r_y0, r_st], writes=[r_yout])
    C.dma("sync", lambda e: e.dma_start(out=y_dram_slice, in_=yout),
          st_sem, reads=[r_yout])



NQQ = 1024
NKK = 512
NG = 4
NHT = 8
BISECT_ITERS = 34


def emit_phase4(C, A, io, scratch):
    nc = C.nc
    ab = C.ab
    nb = len(io["x"])
    RT = nb * 8
    ab.mark()

    o_wrtf = ab.take_f(64)
    o_wrth = ab.take(64)
    o_wrtl = ab.take(64)
    r_wrtf, r_wrth, r_wrtl = Res("wrtf"), Res("wrth"), Res("wrtl")
    wrtf = ab.viewf(o_wrtf, 0, 64)
    wrth = ab.view(o_wrth, 0, 64)
    wrtl = ab.view(o_wrtl, 0, 64)
    o_tmpf = ab.take_f(64)
    tmpf = ab.viewf(o_tmpf, 0, 64)
    r_tmpf = Res("tmpf")
    C.dma("sync", lambda e: e.dma_start(out=wrtf, in_=io["wrt"]), "cwrt",
          writes=[r_wrtf])
    C.op("vector", lambda v: v.tensor_copy(wrth, wrtf),
         reads=[r_wrtf], writes=[r_wrth])
    C.op("vector", lambda v: v.tensor_copy(tmpf, wrth),
         reads=[r_wrth], writes=[r_tmpf])
    C.op("vector", lambda v: v.tensor_tensor(tmpf, wrtf, tmpf, ALU.subtract),
         reads=[r_wrtf, r_tmpf], writes=[r_tmpf])
    C.op("vector", lambda v: v.tensor_copy(wrtl, tmpf),
         reads=[r_tmpf], writes=[r_wrtl])

    nqv = nt(nc, "nqv4", 32, 1, F32)
    r_nqv = Res("nqv")
    C.dma("sync", lambda e: e.dma_start(out=nqv[0:RT, :], in_=io["nqrow"][0:RT, :]),
          "cnqv", writes=[r_nqv])
    bsel = nt(nc, "bsel4", 32, 4, BF16)
    r_bsel = Res("bsel")
    C.dma("sync", lambda e: e.dma_start(out=bsel[:], in_=io["bsel"]),
          "cbsel", writes=[r_bsel])
    nmsk = nt(nc, "nmsk4", 128, 1, F32)
    r_nmsk = Res("nmsk")
    C.op("vector", lambda v: v.memset(nmsk[0:8, :], NEG), writes=[r_nmsk])
    C.op("vector", lambda v: v.memset(nmsk[0:1, :], 0.0), writes=[r_nmsk])
    nullbc = ab.viewf(ab.take_f(1024), 0, 1024)
    r_nullbc = Res("nullbc")
    bcast_row_from_dram(C, nullbc, r_nullbc, io["ntok_row"], sem="cnbc")
    stat2 = [ab.viewf(ab.take_f(8), 0, 8), ab.viewf(ab.take_f(8), 0, 8)]
    sq2 = [ab.view(ab.take(1024), 0, 1024), ab.view(ab.take(1024), 0, 1024)]
    ZL = dict(stat2=stat2, r_stat2=[Res("stat4a"), Res("stat4b")],
              sq2=sq2, r_sq2=[Res("sq4a"), Res("sq4b")])

    zr = ab.viewf(ab.take_f(1024), 0, 1024)[0:1, :]
    r_zr = Res("zrow4")
    C.op("vector", lambda v: v.memset(zr, 0.0), writes=[r_zr])
    for b_ in range(nb):
        for g_ in range(NG):
            C.dma("sync", lambda e, b_=b_, g_=g_: e.dma_start(
                out=scratch["og"][b_][g_][NQQ:NQQ + 1, :], in_=zr),
                "ogz", nowaw=True, reads=[r_zr],
                writes=[scratch["r_og"][b_][g_]])

    ab.mark()
    o_logall = ab.take_f(2048)
    r_logall = Res("logall")
    logall = ab.viewf(o_logall, 0, 2048)
    o_logb = ab.take_f(2048)
    r_logb = Res("logb")
    logb = ab.viewf(o_logb, 0, 2048)
    o_xin = [ab.take_f(1024), ab.take_f(1024)]
    r_xin = [Res("xin0"), Res("xin1")]
    o_xhi = [ab.take(1024), ab.take(1024)]
    o_xlo = [ab.take(1024), ab.take(1024)]
    o_xlf = [ab.take_f(1024), ab.take_f(1024)]
    r_xhi = [Res("xhi0"), Res("xhi1")]
    r_xlo = [Res("xlo0"), Res("xlo1")]
    r_xlf = [Res("xlf0"), Res("xlf1")]
    o_xfh = ab.take(1024)
    o_xfl = ab.take(1024)
    r_xfh, r_xfl = Res("xfh"), Res("xfl")
    xfh = ab.view(o_xfh, 0, 1024)
    xfl = ab.view(o_xfl, 0, 1024)

    for b in range(nb):
        x_dram = io["x"][b]
        for t in range(NTT):
            par = t % 2
            xin = ab.viewf(o_xin[par], 0, 1024)
            xhi = ab.view(o_xhi[par], 0, 1024)
            xlo = ab.view(o_xlo[par], 0, 1024)
            xlf = ab.viewf(o_xlf[par], 0, 1024)
            C.dma("sync", lambda e, t=t, xin=xin, x_dram=x_dram: e.dma_start(
                out=xin, in_=x_dram[t * 128:(t + 1) * 128, :]), f"x4{par}",
                writes=[r_xin[par]])
            C.op("vector", lambda v, xhi=xhi, xin=xin: v.tensor_copy(xhi, xin),
                 reads=[r_xin[par]], writes=[r_xhi[par]])
            C.op("vector", lambda v, xlf=xlf, xhi=xhi: v.tensor_copy(xlf, xhi),
                 reads=[r_xhi[par]], writes=[r_xlf[par]])
            C.op("vector", lambda v, xlf=xlf, xin=xin: v.tensor_tensor(
                xlf, xin, xlf, ALU.subtract),
                reads=[r_xin[par], r_xlf[par]], writes=[r_xlf[par]])
            C.op("vector", lambda v, xlo=xlo, xlf=xlf: v.tensor_copy(xlo, xlf),
                 reads=[r_xlf[par]], writes=[r_xlo[par]])
            for c in range(NCH):
                C.op("tensor", lambda e, c=c, xhi=xhi: e.transpose(
                    A.pbt[:, c * 128:(c + 1) * 128],
                    xhi[:, c * 128:(c + 1) * 128], A.ident[:]),
                    reads=[r_xhi[par], A.r_ident], writes=[A.rpbt])
            C.op("vector", lambda v: v.tensor_copy(xfh[:], A.pbt[:]),
                 reads=[A.rpbt], writes=[r_xfh])
            for c in range(NCH):
                C.op("tensor", lambda e, c=c, xlo=xlo: e.transpose(
                    A.pbt[:, c * 128:(c + 1) * 128],
                    xlo[:, c * 128:(c + 1) * 128], A.ident[:]),
                    reads=[r_xlo[par], A.r_ident], writes=[A.rpbt])
            C.op("vector", lambda v: v.tensor_copy(xfl[:], A.pbt[:]),
                 reads=[A.rpbt], writes=[r_xfl])
            nmm = 3 * NCH
            i = 0
            for dc in range(NCH):
                for (wv, xv, rw, rx) in ((wrth, xfh, r_wrth, r_xfh),
                                         (wrth, xfl, r_wrth, r_xfl),
                                         (wrtl, xfh, r_wrtl, r_xfh)):
                    C.op("tensor", lambda e, dc=dc, wv=wv, xv=xv, i=i, nmm=nmm:
                         e.matmul(A.pb[4][0:8, 0:128],
                                  wv[:, dc * 8:(dc + 1) * 8],
                                  xv[:, dc * 128:(dc + 1) * 128],
                                  start=(i == 0), stop=(i == nmm - 1)),
                         reads=[rw, rx], writes=[A.rpb[4]])
                    i += 1
            C.op("scalar", lambda e, t=t: e.activation(
                logb[0:8, t * 128:(t + 1) * 128], A.pb[4][0:8, 0:128],
                AF.Identity, scale=1.0),
                reads=[A.rpb[4]], writes=[r_logb])
        C.dma("sync", lambda e, b=b: e.dma_start(
            out=bass_sb_rows(logall, b * 8, 8), in_=logb[0:8, :]),
            "logmv", nowaw=(b > 0), reads=[r_logb], writes=[r_logall])

    tok = ab.viewf(ab.take_f(2048), 0, 2048)
    r_tok = Res("tok")
    C.op("gpsimd", lambda g: g.iota(
        tok[0:RT, :], pattern=[[1, 2048]], base=0, channel_multiplier=0,
        allow_small_or_imprecise_dtypes=True), writes=[r_tok])
    zt = ab.viewf(ab.take_f(2048), 0, 2048)
    r_zt = Res("zt")
    C.op("vector", lambda v: v.memset(zt[0:RT, :], 0.0), writes=[r_zt])
    ge = ab.viewf(ab.take_f(2048), 0, 2048)
    m1 = ab.viewf(ab.take_f(2048), 0, 2048)
    m2 = ab.viewf(ab.take_f(2048), 0, 2048)
    cms = ab.viewf(ab.take_f(2048), 0, 2048)
    msk = ab.viewf(ab.take_f(2048), 0, 2048)
    slt = ab.viewf(ab.take_f(2048), 0, 2048)
    sgv = ab.viewf(ab.take_f(2048), 0, 2048)
    r_ge, r_m1, r_m2 = Res("ge"), Res("m1"), Res("m2")
    r_cms, r_msk, r_slt, r_sgv = Res("cms"), Res("msk"), Res("slt"), Res("sgv")
    o_su = ab.take_f(2048)
    su = ab.viewu(o_su, 0, 2048)
    r_su = Res("su")
    mb = ab.view(ab.take(2048), 0, 2048)
    r_mb = Res("mb")
    cnts4 = ab.viewf(ab.take_f(2048), 0, 2048)
    r_cnts4 = Res("cnts4")
    ts4 = nt(nc, "ts4", 32, 8, F32)
    r_ts4 = Res("ts4")
    predi = nt(nc, "predi4", 32, 1, U8)
    r_predi = Res("predi")
    npredi = nt(nc, "npredi4", 32, 1, U8)
    r_npredi = Res("npredi")

    lg = lambda: logall[0:RT, :]
    C.op("vector", lambda v: v.tensor_reduce(ts4[0:RT, 0:1], lg(), AX.X, ALU.min,
                                             negate=False),
         reads=[r_logall], writes=[r_ts4])
    C.op("vector", lambda v: v.tensor_reduce(ts4[0:RT, 1:2], lg(), AX.X, ALU.max),
         reads=[r_logall], writes=[r_ts4])
    C.op("vector", lambda v: v.tensor_scalar(
        ts4[0:RT, 0:1], ts4[0:RT, 0:1], -1.0, None, ALU.add),
        reads=[r_ts4], writes=[r_ts4])
    C.op("vector", lambda v: v.tensor_scalar(
        ts4[0:RT, 1:2], ts4[0:RT, 1:2], 1.0, None, ALU.add),
        reads=[r_ts4], writes=[r_ts4])
    for it in range(BISECT_ITERS):
        C.op("vector", lambda v: v.tensor_tensor(
            ts4[0:RT, 2:3], ts4[0:RT, 0:1], ts4[0:RT, 1:2], ALU.add),
            reads=[r_ts4], writes=[r_ts4])
        C.op("vector", lambda v: v.tensor_scalar(
            ts4[0:RT, 2:3], ts4[0:RT, 2:3], 0.5, None, ALU.mult),
            reads=[r_ts4], writes=[r_ts4])
        C.op("vector", lambda v: v.tensor_scalar(
            ge[0:RT, :], lg(), ts4[0:RT, 2:3], None, ALU.is_ge,
            ALU.add, accum_out=ts4[0:RT, 3:4]),
            reads=[r_logall, r_ts4], writes=[r_ge, r_ts4])
        C.op("vector", lambda v: v.tensor_tensor(
            ts4[0:RT, 4:5], ts4[0:RT, 3:4], nqv[0:RT, :], ALU.is_ge),
            reads=[r_ts4, r_nqv], writes=[r_ts4])
        C.op("vector", lambda v: v.tensor_copy(predi[0:RT, :], ts4[0:RT, 4:5]),
             reads=[r_ts4], writes=[r_predi])
        C.op("vector", lambda v: v.tensor_scalar(
            ts4[0:RT, 5:6], ts4[0:RT, 4:5], -1.0, 1.0, ALU.mult, ALU.add),
            reads=[r_ts4], writes=[r_ts4])
        C.op("vector", lambda v: v.tensor_copy(npredi[0:RT, :], ts4[0:RT, 5:6]),
             reads=[r_ts4], writes=[r_npredi])
        C.op("vector", lambda v: v.select(
            ts4[0:RT, 0:1], predi[0:RT, :], ts4[0:RT, 2:3], ts4[0:RT, 0:1],
            add_drain=True),
            reads=[r_predi, r_ts4], writes=[r_ts4])
        C.op("vector", lambda v: v.select(
            ts4[0:RT, 1:2], npredi[0:RT, :], ts4[0:RT, 2:3], ts4[0:RT, 1:2],
            add_drain=True),
            reads=[r_npredi, r_ts4], writes=[r_ts4])
    C.op("vector", lambda v: v.tensor_scalar(
        m1[0:RT, :], lg(), ts4[0:RT, 1:2], None, ALU.is_ge, ALU.add,
        accum_out=ts4[0:RT, 6:7]),
        reads=[r_logall, r_ts4], writes=[r_m1, r_ts4])
    C.op("vector", lambda v: v.tensor_scalar(
        m2[0:RT, :], lg(), ts4[0:RT, 0:1], None, ALU.is_equal),
        reads=[r_logall, r_ts4], writes=[r_m2])
    C.op("vector", lambda v: v.tensor_tensor_scan(
        cms[0:RT, :], m2[0:RT, :], zt[0:RT, :], 0.0, ALU.add, ALU.add),
        reads=[r_m2, r_zt], writes=[r_cms])
    C.op("vector", lambda v: v.tensor_tensor(
        ts4[0:RT, 5:6], nqv[0:RT, :], ts4[0:RT, 6:7], ALU.subtract),
        reads=[r_nqv, r_ts4], writes=[r_ts4])
    C.op("vector", lambda v: v.tensor_scalar(
        ge[0:RT, :], cms[0:RT, :], ts4[0:RT, 5:6], None, ALU.is_le),
        reads=[r_cms, r_ts4], writes=[r_ge])
    C.op("vector", lambda v: v.tensor_tensor(
        m2[0:RT, :], m2[0:RT, :], ge[0:RT, :], ALU.mult),
        reads=[r_m2, r_ge], writes=[r_m2])
    C.op("vector", lambda v: v.tensor_tensor(
        msk[0:RT, :], m1[0:RT, :], m2[0:RT, :], ALU.add),
        reads=[r_m1, r_m2], writes=[r_msk])
    if P4_DEBUG:
        for nm, tl in (("dbg0", msk), ("dbg1", m1), ("dbg2", logall),
                       ("dbg3", m2)):
            dt_ = C.nc.dram_tensor(nm, [RT, 2048], F32, kind="Internal").ap()
            C.dma("sync", lambda e, dt_=dt_, tl=tl: e.dma_start(
                out=dt_, in_=tl[0:RT, :]), "dbg", nowaw=True,
                reads=[r_msk, r_m1, r_logall, r_m2])
        dt4 = C.nc.dram_tensor("dbg4", [RT, 2], F32, kind="Internal").ap()
        C.dma("sync", lambda e: e.dma_start(out=dt4, in_=ts4[0:RT, 0:2]),
              "dbg", nowaw=True, reads=[r_ts4])
    C.op("vector", lambda v: v.tensor_copy(mb[0:RT, :], msk[0:RT, :]),
         reads=[r_msk], writes=[r_mb])
    for j in range(4):
        C.op("tensor", lambda e, j=j: e.matmul(
            A.pb[5][0:nb, :], bsel[0:RT, 0:nb],
            mb[0:RT, j * 512:(j + 1) * 512],
            start=True, stop=True),
            reads=[r_bsel, r_mb], writes=[A.rpb[5]])
        C.op("scalar", lambda e, j=j: e.activation(
            cnts4[0:nb, j * 512:(j + 1) * 512], A.pb[5][0:nb, :], AF.Identity,
            scale=1.0), reads=[A.rpb[5]], writes=[r_cnts4])
    C.dma("sync", lambda e: e.dma_start(out=scratch["cntsd"][0:nb, :],
                                        in_=cnts4[0:nb, :]),
          "cnt4", reads=[r_cnts4], writes=[scratch["r_cn"]])
    C.op("vector", lambda v: v.tensor_tensor_scan(
        slt[0:RT, :], msk[0:RT, :], zt[0:RT, :], 0.0, ALU.add, ALU.add),
        reads=[r_msk, r_zt], writes=[r_slt])
    C.op("vector", lambda v: v.tensor_scalar(
        slt[0:RT, :], slt[0:RT, :], -1.0, None, ALU.add),
        reads=[r_slt], writes=[r_slt])
    C.op("scalar", lambda e: e.activation(sgv[0:RT, :], lg(), AF.Exp,
                                          scale=-1.0),
         reads=[r_logall], writes=[r_sgv])
    C.op("vector", lambda v: v.tensor_scalar(
        sgv[0:RT, :], sgv[0:RT, :], 1.0, None, ALU.add),
        reads=[r_sgv], writes=[r_sgv])
    C.op("vector", lambda v: v.reciprocal(sgv[0:RT, :], sgv[0:RT, :]),
         reads=[r_sgv], writes=[r_sgv])
    C.op("vector", lambda v: v.tensor_scalar(
        ts4[0:RT, 7:8], nqv[0:RT, :], 1.0, None, ALU.add),
        reads=[r_nqv], writes=[r_ts4])
    C.op("vector", lambda v: v.tensor_scalar(
        ge[0:RT, :], tok[0:RT, :], ts4[0:RT, 7:8], None, ALU.add),
        reads=[r_tok, r_ts4], writes=[r_ge])
    C.op("vector", lambda v: v.tensor_tensor(
        cms[0:RT, :], slt[0:RT, :], ge[0:RT, :], ALU.subtract),
        reads=[r_slt, r_ge], writes=[r_cms])
    C.op("vector", lambda v: v.tensor_tensor(
        cms[0:RT, :], cms[0:RT, :], msk[0:RT, :], ALU.mult),
        reads=[r_cms, r_msk], writes=[r_cms])
    C.op("vector", lambda v: v.tensor_tensor(
        cms[0:RT, :], cms[0:RT, :], ge[0:RT, :], ALU.add),
        reads=[r_cms, r_ge], writes=[r_cms])
    C.op("vector", lambda v: v.tensor_scalar(
        cms[0:RT, :], cms[0:RT, :], 1024.0, None, ALU.min),
        reads=[r_cms], writes=[r_cms])
    C.op("vector", lambda v: v.tensor_copy(su[0:RT, :], cms[0:RT, :]),
         reads=[r_cms], writes=[r_su])
    C.dma("sync", lambda e: e.dma_start(out=scratch["slotsc"][0:RT, :],
                                        in_=su[0:RT, :]),
          "sl4", reads=[r_su], writes=[scratch["r_sl"]])

    o_R0 = ab.take_f(2048)
    o_R1 = ab.take_f(2048)
    Rb = [ab.viewf(o_R0, 0, 2048), ab.viewf(o_R1, 0, 2048)]
    Vt = [ge, m1]
    Vs = [m2, cms]
    r_Rb = [Res("R0"), Res("R1")]
    r_Vt = [r_ge, r_m1]
    r_Vs = [r_m2, r_cms]
    o_bu = ab.take(1024)
    bu8 = ab.view(o_bu, 0, 1024).bitcast(U8)
    r_bu = Res("bu8")
    bmv = zt
    r_bmv = r_zt
    r2v = slt
    r_r2 = r_slt
    rsv = sgv
    C.op("vector", lambda v: v.tensor_copy(Vt[0][0:RT, :], tok[0:RT, :]),
         reads=[r_tok], writes=[r_Vt[0]])
    C.op("vector", lambda v: v.tensor_copy(Vs[0][0:RT, :], sgv[0:RT, :]),
         reads=[r_sgv], writes=[r_Vs[0]])
    C.op("vector", lambda v: v.tensor_tensor(
        Rb[0][0:RT, :], tok[0:RT, :], slt[0:RT, :], ALU.subtract),
        reads=[r_tok, r_slt], writes=[r_Rb[0]])
    C.op("vector", lambda v: v.tensor_tensor(
        Rb[0][0:RT, :], Rb[0][0:RT, :], msk[0:RT, :], ALU.mult),
        reads=[r_Rb[0], r_msk], writes=[r_Rb[0]])
    rsv_r = r_sgv
    o_Ri = ab.take_f(2048)
    Rint = ab.viewu(o_Ri, 0, 2048).bitcast(mybir.dt.int32)
    r_Rint = Res("Rint")
    for k in range(11):
        s = 1 << k
        cur, nxt = k % 2, (k + 1) % 2
        W = 2048 - s
        C.op("vector", lambda v, cur=cur: v.tensor_copy(
            Rint[0:RT, :], Rb[cur][0:RT, :]),
            reads=[r_Rb[cur]], writes=[r_Rint])
        C.op("vector", lambda v, s=s: v.tensor_scalar(
            Rint[0:RT, :], Rint[0:RT, :], int(s), None, ALU.bitwise_and),
            reads=[r_Rint], writes=[r_Rint])
        C.op("vector", lambda v: v.tensor_scalar(
            bmv[0:RT, :], Rint[0:RT, :], 0.0, None, ALU.is_gt),
            reads=[r_Rint], writes=[r_bmv])
        C.op("vector", lambda v, s=s, W=W: v.tensor_copy(
            bu8[0:RT, 0:W], bmv[0:RT, s:2048]),
            reads=[r_bmv], writes=[r_bu])
        C.op("vector", lambda v, cur=cur: v.scalar_tensor_tensor(
            r2v[0:RT, :], bmv[0:RT, :], -1.0, Rb[cur][0:RT, :],
            ALU.mult, ALU.mult), reads=[r_bmv, r_Rb[cur]], writes=[r_r2])
        C.op("vector", lambda v, cur=cur: v.tensor_tensor(
            r2v[0:RT, :], r2v[0:RT, :], Rb[cur][0:RT, :], ALU.add),
            reads=[r_r2, r_Rb[cur]], writes=[r_r2])
        C.op("vector", lambda v, s=s, cur=cur: v.tensor_scalar(
            rsv[0:RT, 0:2048 - s], Rb[cur][0:RT, s:2048], float(-s), None,
            ALU.add), reads=[r_Rb[cur]], writes=[rsv_r])
        for (dst, dr, src_, sr, keep, kr) in (
                (Rb[nxt], r_Rb[nxt], rsv, rsv_r, r2v, r_r2),
                (Vt[nxt], r_Vt[nxt], Vt[cur], r_Vt[cur], Vt[cur], r_Vt[cur]),
                (Vs[nxt], r_Vs[nxt], Vs[cur], r_Vs[cur], Vs[cur], r_Vs[cur])):
            shift_src = (src_[0:RT, 0:W] if src_ is rsv
                         else src_[0:RT, s:2048])
            C.op("vector", lambda v, dst=dst, keep=keep, s=s, W=W:
                 v.tensor_copy(dst[0:RT, W:2048], keep[0:RT, W:2048]),
                 reads=[kr], writes=[dr])
            C.op("vector", lambda v, dst=dst, shift_src=shift_src, keep=keep,
                 s=s, W=W: v.select(
                     dst[0:RT, 0:W], bu8[0:RT, 0:W], shift_src,
                     keep[0:RT, 0:W], add_drain=True),
                 reads=[r_bu, sr, kr], writes=[dr])
    fin = 11 % 2
    C.dma("sync", lambda e: e.dma_start(out=scratch["tokd"][0:RT, :],
                                        in_=Vt[fin][0:RT, :]),
          "tk4", nowaw=False, reads=[r_Vt[fin]], writes=[scratch["r_lists"]])
    C.dma("sync", lambda e: e.dma_start(out=scratch["sgcd"][0:RT, :],
                                        in_=Vs[fin][0:RT, :]),
          "tk4", nowaw=True, reads=[r_Vs[fin]], writes=[scratch["r_lists"]])

    C.barrier()
    ab.release()

    o_wq4 = ab.take(4096)
    o_wkv4 = ab.take(8192)
    o_wo4 = ab.take(4096)
    r_wq4, r_wkv4, r_wo4 = Res("wq4"), Res("wkv4"), Res("wo4")
    wq4 = lambda a, b_: ab.view(o_wq4, a, b_)
    wkv4 = lambda a, b_: ab.view(o_wkv4, a, b_)
    wo4 = lambda a, b_: ab.view(o_wo4, a, b_)
    o_qgfm = ab.take(8192)
    o_kgfm = ab.take(4096)
    r_qgfm, r_kgfm = Res("qgfm"), Res("kgfm")
    qgfm = lambda c, a, b_: ab.view(o_qgfm, c * 1024 + a, c * 1024 + b_)
    kgfm = lambda c, a, b_: ab.view(o_kgfm, c * 512 + a, c * 512 + b_)
    o_gg = [ab.take(1024) for _ in range(4)]
    r_gg = [Res(f"gg{i}") for i in range(4)]
    o_qpfm = ab.take(4096)
    r_qpfm = Res("qpfm")
    qpfm = lambda c, a, b_: ab.view(o_qpfm, c * 1024 + a, c * 1024 + b_)
    o_kfm4 = ab.take(4 * 520)
    r_kfm4 = Res("kfm4")
    kfm4 = lambda c, a, b_: ab.view(o_kfm4, c * 520 + a, c * 520 + b_)
    o_von4 = ab.take(5 * 520)
    r_von4 = Res("von4")

    def von4_ctx(kt, head):
        base = kt * 520 + head * 65
        return ab.view(o_von4, base, base + 65)

    o_ctx4 = ab.take(4 * 1024)
    r_ctx4 = Res("ctx4")
    ctx4 = lambda c, a, b_: ab.view(o_ctx4, c * 1024 + a, c * 1024 + b_)
    o_qpr = ab.take_f(32)
    qpr = ab.viewf(o_qpr, 0, 32)
    r_qpr = Res("qpr")
    o_qtok = ab.take_f(8)
    qtok = ab.viewu(o_qtok, 0, 8)
    o_qsg = ab.take_f(8)
    qsg = ab.viewf(o_qsg, 0, 8)
    r_qtok, r_qsg = Res("qtok"), Res("qsg")
    o_ktok = ab.take_f(4)
    ktok = ab.viewu(o_ktok, 0, 4)
    o_ksg = ab.take_f(4)
    ksg = ab.viewf(o_ksg, 0, 4)
    r_ktok, r_ksg = Res("ktok"), Res("ksg")
    o_prA = [ab.take(512), ab.take(512)]
    o_prB = [ab.take(512), ab.take(512)]
    r_prA = [Res("prA0"), Res("prA1")]
    r_prB = [Res("prB0"), Res("prB1")]
    o_ot = [ab.take_f(1024), ab.take_f(1024)]
    r_ot = [Res("ot0"), Res("ot1")]
    d2s4 = ab.viewf(ab.take_f(512), 0, 512)
    d2rf4 = ab.viewf(ab.take_f(512), 0, 512)
    d2b4 = ab.view(ab.take(512), 0, 512)
    r_d2s4, r_d2rf4, r_d2b4 = Res("d2s4"), Res("d2rf4"), Res("d2b4")
    C.op("vector", lambda v: v.memset(d2s4[:], 1.0), writes=[r_d2s4])
    nkT = ab.view(ab.take(16), 0, 16)
    r_nkT = Res("nkT")
    C.dma("gpsimd", lambda e: e.dma_start(out=nkT, in_=io["nullkT"]),
          "nk4", writes=[r_nkT])
    nvrow = ab.view(ab.take(512), 0, 512)[0:1, :]
    r_nvrow = Res("nvrow")

    for g in range(NG):
        for c in range(NCH):
            C.dma("gpsimd", lambda e, c=c, g=g: e.dma_start(
                out=wq4(c * 512, (c + 1) * 512),
                in_=io[f"wqT{g}"][c * 128:(c + 1) * 128, :]), "wq4",
                nowaw=(c > 0), writes=[r_wq4])
            C.dma("gpsimd", lambda e, c=c, g=g: e.dma_start(
                out=wkv4(c * 1024, (c + 1) * 1024),
                in_=io[f"wkvT{g}"][c * 128:(c + 1) * 128, :]), "wkv4",
                nowaw=(c > 0), writes=[r_wkv4])
        for c in range(4):
            C.dma("gpsimd", lambda e, c=c, g=g: e.dma_start(
                out=wo4(c * 1024, (c + 1) * 1024),
                in_=io[f"woutT{g}"][c * 128:(c + 1) * 128, :]), "wo4",
                nowaw=(c > 0), writes=[r_wo4])
        for b in range(nb):
            emit_p4_group_batch(C, A, io, scratch, g, b, locals())
        del b

    o_cnt = ab.take_f(16)
    cntt = ab.viewf(o_cnt, 0, 16)
    r_cnt = Res("cntt")
    o_scg = [ab.take_f(16) for _ in range(4)]
    scg = [ab.viewu(o, 0, 16) for o in o_scg]
    r_scg = [Res(f"scg{g}") for g in range(4)]
    o_st4 = [ab.take_f(1024), ab.take_f(1024)]
    r_st4 = [Res("st40"), Res("st41")]
    o_gst = [ab.take_f(1024) for _ in range(4)]
    gst = [ab.viewf(o, 0, 1024) for o in o_gst]
    r_gst = [Res(f"gst{g}") for g in range(4)]
    o_c1s = ab.take_f(2)
    c1s = ab.viewf(o_c1s, 0, 1)
    c2s = ab.viewf(o_c1s, 1, 2)
    r_c1s, r_c2s = Res("c1s"), Res("c2s")
    o_y02 = [ab.take_f(1024), ab.take_f(1024)]
    r_y02 = [Res("y04a"), Res("y04b")]
    o_yout = [ab.take_f(1024), ab.take_f(1024)]
    r_yout = [Res("yo40"), Res("yo41")]
    o_xr4 = [ab.take_f(1024), ab.take_f(1024)]
    r_xr4 = [Res("xr40"), Res("xr41")]

    for b in range(nb):
        x_dram = io["x"][b]
        y_dram = io["y"][b]
        C.dma("sync", lambda e, b=b: e.dma_start(
            out=cntt, in_=scratch["cntsd"][b:b + 1, :].rearrange(
                "x (p t) -> (x p) t", p=128)), "cnt4l",
            reads=[scratch["r_cn"]], writes=[r_cnt])
        for g in range(4):
            C.dma("sync", lambda e, b=b, g=g: e.dma_start(
                out=scg[g], in_=scratch["slotsc"][b * 8 + g:b * 8 + g + 1, :]
                .rearrange("x (p t) -> (x p) t", p=128)), f"scg{g}",
                reads=[scratch["r_sl"]], writes=[r_scg[g]])
        for t in range(16):
            par = t % 2
            st4 = ab.viewf(o_st4[par], 0, 1024)
            for g in range(4):
                C.dma("gpsimd", lambda e, g=g, t=t, b=b:
                      e.indirect_dma_start(
                          out=gst[g], out_offset=None,
                          in_=scratch["og"][b][g],
                          in_offset=IOA(scg[g][:, t:t + 1], 0)),
                      f"cg{g}", reads=[r_scg[g], scratch["r_og"][b][g]],
                      writes=[r_gst[g]])
            C.op("vector", lambda v: v.tensor_tensor(
                gst[0], gst[0], gst[1], ALU.add),
                reads=[r_gst[0], r_gst[1]], writes=[r_gst[0]])
            C.op("vector", lambda v: v.tensor_tensor(
                gst[2], gst[2], gst[3], ALU.add),
                reads=[r_gst[2], r_gst[3]], writes=[r_gst[2]])
            C.op("vector", lambda v, st4=st4: v.tensor_tensor(
                st4, gst[0], gst[2], ALU.add),
                reads=[r_gst[0], r_gst[2]], writes=[r_st4[par]])
            ccol = cntt[:, t:t + 1]
            C.op("vector", lambda v, ccol=ccol: v.tensor_scalar(
                c1s, ccol, 1e-5, None, ALU.max),
                reads=[r_cnt], writes=[r_c1s])
            C.op("vector", lambda v: v.reciprocal(c1s, c1s),
                 reads=[r_c1s], writes=[r_c1s])
            C.op("vector", lambda v, st4=st4: v.tensor_scalar(
                st4, st4, c1s, None, ALU.mult),
                reads=[r_st4[par], r_c1s], writes=[r_st4[par]])
            C.op("vector", lambda v, ccol=ccol: v.tensor_scalar(
                c2s, ccol, 0.0, None, ALU.is_gt),
                reads=[r_cnt], writes=[r_c2s])
            C.op("vector", lambda v, st4=st4: v.tensor_tensor(
                st4, st4, nullbc[:], ALU.subtract),
                reads=[r_st4[par], r_nullbc], writes=[r_st4[par]])
            C.op("vector", lambda v, st4=st4: v.scalar_tensor_tensor(
                st4, st4, c2s, nullbc[:], ALU.mult, ALU.add),
                reads=[r_st4[par], r_c2s, r_nullbc], writes=[r_st4[par]])
            xr = ab.viewf(o_xr4[par], 0, 1024)
            xsl = x_dram.rearrange("(p t) d -> p t d", t=16)[:, t, :]
            ysl = y_dram.rearrange("(p t) d -> p t d", t=16)[:, t, :]
            C.dma("sync", lambda e, xr=xr, xsl=xsl: e.dma_start(
                out=xr, in_=xsl), f"xr4{par}", writes=[r_xr4[par]])
            y0 = ab.viewf(o_y02[par], 0, 1024)
            C.op("vector", lambda v, st4=st4, xr=xr, y0=y0: v.tensor_tensor(
                y0, st4, xr, ALU.add),
                reads=[r_st4[par], r_xr4[par]], writes=[r_y02[par]])
            emit_ln_store(C, A, ZL, y0, r_y02[par],
                          ab.viewf(o_yout[par], 0, 1024), r_yout[par],
                          ysl, f"yst4{par}", par=par)
    C.barrier()
    ab.release()


def emit_ln_store(C, A, Z, y0, r_y0, yout, r_yout, y_dram_slice, st_sem,
                  par=0):
    st = Z["stat2"][par]
    r_st = Z["r_stat2"][par]
    C.op("vector", lambda v: v.tensor_reduce(st[:, 0:1], y0, AX.X, ALU.add),
         reads=[r_y0], writes=[r_st])
    C.op("scalar", lambda e: e.activation(
        Z["sq2"][par][:], y0, AF.Square, accum_out=st[:, 1:2]),
        reads=[r_y0], writes=[Z["r_sq2"][par], r_st])
    C.op("vector", lambda v: v.tensor_scalar(
        st[:, 2:3], st[:, 0:1], 1.0 / D, None, ALU.mult),
        reads=[r_st], writes=[r_st])
    C.op("vector", lambda v: v.tensor_scalar(
        st[:, 3:4], st[:, 2:3], st[:, 2:3], None, ALU.mult),
        reads=[r_st], writes=[r_st])
    C.op("vector", lambda v: v.scalar_tensor_tensor(
        st[:, 4:5], st[:, 1:2], 1.0 / D, st[:, 3:4], ALU.mult, ALU.subtract),
        reads=[r_st], writes=[r_st])
    C.op("scalar", lambda e: e.activation(
        st[:, 5:6], st[:, 4:5], AF.Ln, bias=A.epsb[:], scale=1.0),
        reads=[r_st, A.r_epsb], writes=[r_st])
    C.op("scalar", lambda e: e.activation(
        st[:, 5:6], st[:, 5:6], AF.Exp, scale=-0.5),
        reads=[r_st], writes=[r_st])
    C.op("vector", lambda v: v.tensor_scalar(
        st[:, 6:7], st[:, 2:3], st[:, 5:6], -1.0, ALU.mult, ALU.mult),
        reads=[r_st], writes=[r_st])
    C.op("scalar", lambda e: e.activation(
        yout, y0, AF.Identity, bias=st[:, 6:7], scale=st[:, 5:6]),
        reads=[r_y0, r_st], writes=[r_yout])
    C.dma("sync", lambda e: e.dma_start(out=y_dram_slice, in_=yout),
          st_sem, reads=[r_yout])


def emit_p4_group_batch(C, A, io, scratch, g, b, L):
    ab = C.ab
    x_dram = io["x"][b]
    og = scratch["og"][b][g]
    r_og = scratch["r_og"][b][g]

    qpr, r_qpr = L["qpr"], L["r_qpr"]
    qtok, r_qtok, qsg, r_qsg = L["qtok"], L["r_qtok"], L["qsg"], L["r_qsg"]
    ktok, r_ktok, ksg, r_ksg = L["ktok"], L["r_ktok"], L["ksg"], L["r_ksg"]
    qgfm, r_qgfm = L["qgfm"], L["r_qgfm"]
    kgfm, r_kgfm = L["kgfm"], L["r_kgfm"]
    o_gg, r_gg = L["o_gg"], L["r_gg"]
    wq4, wkv4, wo4 = L["wq4"], L["wkv4"], L["wo4"]
    r_wq4, r_wkv4, r_wo4 = L["r_wq4"], L["r_wkv4"], L["r_wo4"]
    qpfm, r_qpfm = L["qpfm"], L["r_qpfm"]
    kfm4, r_kfm4 = L["kfm4"], L["r_kfm4"]
    o_von4, r_von4 = L["o_von4"], L["r_von4"]
    von4_ctx = L["von4_ctx"]
    ctx4, r_ctx4 = L["ctx4"], L["r_ctx4"]
    o_prA, o_prB = L["o_prA"], L["o_prB"]
    r_prA, r_prB = L["r_prA"], L["r_prB"]
    o_ot, r_ot = L["o_ot"], L["r_ot"]
    d2s4, d2rf4, d2b4 = L["d2s4"], L["d2rf4"], L["d2b4"]
    r_d2s4, r_d2rf4, r_d2b4 = L["r_d2s4"], L["r_d2rf4"], L["r_d2b4"]
    nmsk, r_nmsk = L["nmsk"], L["r_nmsk"]

    rowq = b * 8 + g
    rowk = b * 8 + 4 + g
    C.dma("sync", lambda e, rowq=rowq: e.dma_start(
        out=qpr[:, 0:8],
        in_=scratch["tokd"][rowq:rowq + 1, 0:1024].rearrange(
            "x (p n) -> (x p) n", p=128)), "qpr",
        reads=[scratch["r_lists"]], writes=[r_qpr])
    C.op("vector", lambda v: v.tensor_copy(qtok, qpr[:, 0:8]),
         reads=[r_qpr], writes=[r_qtok])
    C.dma("sync", lambda e, rowq=rowq: e.dma_start(
        out=qsg, in_=scratch["sgcd"][rowq:rowq + 1, 0:1024].rearrange(
            "x (p n) -> (x p) n", p=128)), "qsgl",
        reads=[scratch["r_lists"]], writes=[r_qsg])
    C.dma("sync", lambda e, rowk=rowk: e.dma_start(
        out=qpr[:, 8:12],
        in_=scratch["tokd"][rowk:rowk + 1, 0:512].rearrange(
            "x (p n) -> (x p) n", p=128)), "qpr",
        reads=[scratch["r_lists"], r_qtok], writes=[r_qpr])
    C.op("vector", lambda v: v.tensor_copy(ktok, qpr[:, 8:12]),
         reads=[r_qpr], writes=[r_ktok])
    C.dma("sync", lambda e, rowk=rowk: e.dma_start(
        out=ksg, in_=scratch["sgcd"][rowk:rowk + 1, 0:512].rearrange(
            "x (p n) -> (x p) n", p=128)), "ksgl",
        reads=[scratch["r_lists"]], writes=[r_ksg])

    def gather_tp(n_tiles, tokres, tokap, dstfm, r_dst, dst_elems):
        def dma_tile(nt_):
            par = nt_ % 4
            gg = ab.view(o_gg[par], 0, 1024)
            C.dma("gpsimd", lambda e, nt_=nt_, gg=gg: e.indirect_dma_start(
                out=gg, out_offset=None, in_=x_dram,
                in_offset=IOA(tokap[:, nt_:nt_ + 1], 0)),
                f"gt{par}", reads=[tokres], writes=[r_gg[par]])

        for nt_ in range(min(3, n_tiles)):
            dma_tile(nt_)
        for nt_ in range(n_tiles):
            par = nt_ % 4
            if nt_ + 3 < n_tiles:
                dma_tile(nt_ + 3)
            gg = ab.view(o_gg[par], 0, 1024)
            for c in range(NCH):
                C.op("tensor", lambda e, c=c, gg=gg: e.transpose(
                    A.pbt[:, c * 128:(c + 1) * 128],
                    gg[:, c * 128:(c + 1) * 128], A.ident[:]),
                    reads=[r_gg[par], A.r_ident], writes=[A.rpbt])
            dst = dstfm.rearrange("p (c s) -> p c s", c=NCH)[
                :, :, nt_ * 128:(nt_ + 1) * 128]
            C.op("vector", lambda v, dst=dst: v.tensor_copy(dst, A.pbt[:]),
                 reads=[A.rpbt], writes=[r_dst])
            r_gg[par].readers.append((C.pe, C.pe.n))

    gather_tp(8, r_qtok, qtok, ab.view(L["o_qgfm"], 0, 8192), r_qgfm, 8192)
    gather_tp(4, r_ktok, ktok, ab.view(L["o_kgfm"], 0, 4096), r_kgfm, 4096)

    for qe in range(4):
        for ns in range(2):
            pp = 4 + (qe * 2 + ns) % 2
            for dc in range(NCH):
                C.op("tensor", lambda e, qe=qe, ns=ns, dc=dc, pp=pp:
                     e.matmul(A.pb[pp][:],
                              wq4(dc * 512 + qe * 128, dc * 512 + (qe + 1) * 128),
                              qgfm(dc, ns * 512, (ns + 1) * 512),
                              start=(dc == 0), stop=(dc == NCH - 1)),
                     reads=[r_wq4, r_qgfm], writes=[A.rpb[pp]])
            C.op("scalar", lambda e, qe=qe, ns=ns, pp=pp: e.activation(
                qpfm(qe, ns * 512, (ns + 1) * 512), A.pb[pp][:],
                AF.Identity, scale=1.0),
                reads=[A.rpb[pp]], writes=[r_qpfm])
    for ke in range(4):
        pp = 4 + ke % 2
        for dc in range(NCH):
            C.op("tensor", lambda e, ke=ke, dc=dc, pp=pp: e.matmul(
                A.pb[pp][:],
                wkv4(dc * 1024 + ke * 128, dc * 1024 + (ke + 1) * 128),
                kgfm(dc, 0, 512),
                start=(dc == 0), stop=(dc == NCH - 1)),
                reads=[r_wkv4, r_kgfm], writes=[A.rpb[pp]])
        C.op("scalar", lambda e, ke=ke, pp=pp: e.activation(
            kfm4(ke, 0, 512), A.pb[pp][:], AF.Identity, scale=1.0),
            reads=[A.rpb[pp]], writes=[r_kfm4])
    C.op("vector", lambda v: v.tensor_copy(
        kfm4(0, 0, 4 * 520).rearrange("p (c w) -> p c w", c=4)[
            :, :, 512:513],
        L["nkT"][:, g * 4:(g + 1) * 4].unsqueeze(2)),
        reads=[L["r_nkT"]], writes=[r_kfm4])
    C.op("vector", lambda v: v.memset(
        kfm4(0, 0, 4 * 520).rearrange("p (c w) -> p c w", c=4)[
            :, :, 513:520], 0.0), writes=[r_kfm4])
    for kt in range(4):
        pp = 4 + kt % 2
        for dc in range(NCH):
            C.op("tensor", lambda e, kt=kt, dc=dc, pp=pp: e.matmul(
                A.pb[pp][:], kgfm(dc, kt * 128, (kt + 1) * 128),
                wkv4(dc * 1024 + 512, dc * 1024 + 1024),
                start=(dc == 0), stop=(dc == NCH - 1)),
                reads=[r_wkv4, r_kgfm], writes=[A.rpb[pp]])
        dst = ab.view(o_von4, kt * 520, (kt + 1) * 520).rearrange(
            "p (h w) -> p h w", w=65)[:, :, 0:64]
        src3 = A.pb[pp][:].rearrange("p (h d) -> p h d", d=64)
        C.op("scalar", lambda e, kt=kt, pp=pp, dst=dst, src3=src3:
             e.activation(dst, src3, AF.Identity,
                          scale=ksg[:, kt:kt + 1]),
             reads=[A.rpb[pp], r_ksg], writes=[r_von4])
    C.op("vector", lambda v: v.memset(
        ab.view(o_von4, 0, 4 * 520).rearrange(
            "p (t h w) -> p t h w", h=8, w=65)[:, :, :, 64:65], 1.0),
        writes=[r_von4])
    C.op("vector", lambda v: v.memset(
        ab.view(o_von4, 4 * 520, 5 * 520), 0.0), writes=[r_von4])
    C.dma("gpsimd", lambda e: e.dma_start(
        out=L["nvrow"], in_=io["nullv"][g:g + 1, :]),
        "nv4", reads=[L["r_nvrow"]], writes=[L["r_nvrow"]])
    C.op("vector", lambda v: v.tensor_copy(
        ab.view(o_von4, 4 * 520, 5 * 520)[0:1, :].rearrange(
            "p (h w) -> p h w", w=65)[:, :, 0:64],
        L["nvrow"].rearrange("x (h d) -> x h d", d=64)),
        reads=[L["r_nvrow"]], writes=[r_von4])
    C.op("vector", lambda v: v.memset(
        ab.view(o_von4, 4 * 520, 5 * 520)[0:1, :].rearrange(
            "p (h w) -> p h w", w=65)[:, :, 64:65], 1.0), writes=[r_von4])

    def recip_cast4():
        C.op("vector", lambda v: v.reciprocal(d2rf4[:], d2s4[:]),
             reads=[r_d2s4], writes=[r_d2rf4])
        C.op("vector", lambda v: v.tensor_copy(d2b4[:], d2rf4[:]),
             reads=[r_d2rf4], writes=[r_d2b4])

    def finish4(si):
        hc, q2 = si // 2, si % 2
        sel = A.onecolE if si % 2 == 0 else A.onecolO
        C.op("tensor", lambda e, sel=sel: e.matmul(
            A.pb6[:], sel[:], d2b4[:], start=True, stop=True),
            reads=[A.r_onecolEO, r_d2b4], writes=[A.rpb6])
        C.op("vector", lambda v, hc=hc, q2=q2: v.tensor_tensor(
            ctx4(hc, q2 * 512, (q2 + 1) * 512),
            ctx4(hc, q2 * 512, (q2 + 1) * 512), A.pb6[:], ALU.mult),
            reads=[r_ctx4, A.rpb6], writes=[r_ctx4])

    for hc in range(4):
        for q2 in range(2):
            def sc4(kt):
                sa, sb = (0, 1) if kt % 2 == 0 else (2, 3)
                par = kt % 2
                if kt < 4:
                    lA = kfm4(hc, kt * 128, (kt + 1) * 128)[0:64, :]
                    lB = kfm4(hc, kt * 128, (kt + 1) * 128)[64:128, :]
                    oA, oB = A.pb[sa][:], A.pb[sb][:]
                else:
                    lA = kfm4(hc, 512, 520)[0:64, :]
                    lB = kfm4(hc, 512, 520)[64:128, :]
                    oA, oB = A.pb[sa][0:8, :], A.pb[sb][0:8, :]
                C.op("tensor", lambda e, lA=lA, oA=oA, hc=hc, q2=q2:
                     e.matmul(oA, lA,
                              qpfm(hc, q2 * 512, (q2 + 1) * 512)[0:64, :],
                              start=True, stop=True, tile_position=(0, 0)),
                     reads=[r_kfm4, r_qpfm], writes=[A.rpb[sa]])
                C.op("tensor", lambda e, lB=lB, oB=oB, hc=hc, q2=q2:
                     e.matmul(oB, lB,
                              qpfm(hc, q2 * 512, (q2 + 1) * 512)[64:128, :],
                              start=True, stop=True, tile_position=(64, 0)),
                     reads=[r_kfm4, r_qpfm], writes=[A.rpb[sb]])
                prA = ab.view(o_prA[par], 0, 512)
                prB = ab.view(o_prB[par], 0, 512)
                if kt < 4:
                    C.op("scalar", lambda e, sa=sa, prA=prA: e.activation(
                        prA, A.pb[sa][:], AF.Exp, scale=0.125),
                        reads=[A.rpb[sa]], writes=[r_prA[par]])
                    C.op("scalar", lambda e, sb=sb, prB=prB: e.activation(
                        prB, A.pb[sb][:], AF.Exp, scale=0.125),
                        reads=[A.rpb[sb]], writes=[r_prB[par]])
                else:
                    C.op("scalar", lambda e, sa=sa, prA=prA: e.activation(
                        prA[0:8, :], A.pb[sa][0:8, :], AF.Exp, scale=0.125,
                        bias=nmsk[0:8, :]),
                        reads=[A.rpb[sa], r_nmsk], writes=[r_prA[par]])
                    C.op("scalar", lambda e, sb=sb, prB=prB: e.activation(
                        prB[0:8, :], A.pb[sb][0:8, :], AF.Exp, scale=0.125,
                        bias=nmsk[0:8, :]),
                        reads=[A.rpb[sb], r_nmsk], writes=[r_prB[par]])

            def cx4(kt):
                par = kt % 2
                prA = ab.view(o_prA[par], 0, 512)
                prB = ab.view(o_prB[par], 0, 512)
                if kt < 4:
                    vA = von4_ctx(kt, 2 * hc)
                    vB = von4_ctx(kt, 2 * hc + 1)
                    rhA, rhB = prA, prB
                else:
                    vA = von4_ctx(4, 2 * hc)[0:8, :]
                    vB = von4_ctx(4, 2 * hc + 1)[0:8, :]
                    rhA, rhB = prA[0:8, :], prB[0:8, :]
                C.op("tensor", lambda e, vA=vA, rhA=rhA, kt=kt: e.matmul(
                    A.pb[4][0:65, :], vA, rhA,
                    start=(kt == 0), stop=(kt == 4)),
                    reads=[r_von4, r_prA[par]], writes=[A.rpb[4]])
                C.op("tensor", lambda e, vB=vB, rhB=rhB, kt=kt: e.matmul(
                    A.pb[5][0:65, :], vB, rhB,
                    start=(kt == 0), stop=(kt == 4)),
                    reads=[r_von4, r_prB[par]], writes=[A.rpb[5]])

            sc4(0)
            for kt in range(1, 5):
                sc4(kt)
                cx4(kt - 1)
            cx4(4)
            C.op("scalar", lambda e, hc=hc, q2=q2: e.activation(
                ctx4(hc, q2 * 512, (q2 + 1) * 512)[0:64, :],
                A.pb[4][0:64, :], AF.Identity, scale=1.0),
                reads=[A.rpb[4]], writes=[r_ctx4])
            C.op("scalar", lambda e, hc=hc, q2=q2: e.activation(
                ctx4(hc, q2 * 512, (q2 + 1) * 512)[64:128, :],
                A.pb[5][0:64, :], AF.Identity, scale=1.0),
                reads=[A.rpb[5]], writes=[r_ctx4])
            si = 2 * hc + q2
            bA = (si % 2) * 64
            C.op("vector", lambda v, bA=bA: v.tensor_copy(
                d2s4[bA:bA + 1, :], A.pb[4][64:65, :]),
                reads=[A.rpb[4]], writes=[r_d2s4])
            C.op("vector", lambda v, bA=bA: v.tensor_copy(
                d2s4[bA + 32:bA + 33, :], A.pb[5][64:65, :]),
                reads=[A.rpb[5]], writes=[r_d2s4])
            if si >= 2:
                finish4(si - 2)
            if si % 2 == 1:
                recip_cast4()
    finish4(6)
    finish4(7)

    for nt_ in range(8):
        par = nt_ % 2
        ot = ab.viewf(o_ot[par], 0, 1024)
        for dd in range(2):
            pp = (4, 0, 5, 2)[(nt_ * 2 + dd) % 4]
            for e4 in range(4):
                C.op("tensor", lambda e, nt_=nt_, dd=dd, e4=e4, pp=pp:
                     e.matmul(A.pb[pp][:],
                              ctx4(e4, nt_ * 128, (nt_ + 1) * 128),
                              wo4(e4 * 1024 + dd * 512,
                                  e4 * 1024 + (dd + 1) * 512),
                              start=(e4 == 0), stop=(e4 == 3)),
                     reads=[r_ctx4, r_wo4], writes=[A.rpb[pp]])
            C.op("scalar", lambda e, nt_=nt_, dd=dd, pp=pp, ot=ot:
                 e.activation(ot[:, dd * 512:(dd + 1) * 512], A.pb[pp][:],
                              AF.Identity, scale=qsg[:, nt_:nt_ + 1]),
                 reads=[A.rpb[pp], r_qsg], writes=[r_ot[par]])
        dst = og[0:1024].rearrange("(p n) d -> p n d", n=8)[:, nt_, :]
        C.dma("sync", lambda e, dst=dst, ot=ot: e.dma_start(out=dst, in_=ot),
              f"og{par}", reads=[r_ot[par]], writes=[r_og])



def new_ctx(nc, ab_elems=84000, af_elems=5200):
    C = Ctx(nc)
    C.ab = Arena(nc, "arena_b", BF16, ab_elems)
    C.af = Arena(nc, "arena_f", F32, af_elems)
    return C


L0_CFG = dict(uniq="", rope=True, causal=True, escale=0.125, xscale=1.0,
              wq=("wq", 0, False), wk=("wk", 0, False), wv=("wv", 0, False),
              wd=("wd", 0, False), bq="bq", bk="bk", bd="bd", bv_row="bv_row")

L1_CFG = dict(uniq="", rope=False, causal=False, escale=1.0 / 32.0, xscale=2.0,
              wq=("wqkv", 0, True), wk=("wqkv", 1024, True),
              wv=("wqkv", 2048, True), wd=("wo", 0, True),
              bq="bq", bk="bk", bd="bd", bv_row="bv_row")


def _decl_stream(nc, name, nb, kind):
    return [nc.dram_tensor(f"{name}{b}", [S, D], F32, kind=kind).ap()
            for b in range(nb)]


def build_phase1_program(nb=BPC):
    nc = bass.Bass("TRN2", target_bir_lowering=False)
    io = {"x": _decl_stream(nc, "x", nb, "ExternalInput"),
          "y": _decl_stream(nc, "y", nb, "ExternalOutput")}
    for w in ("wq", "wk", "wv", "wd"):
        io[w] = nc.dram_tensor(w, [D, D], F32, kind="ExternalInput").ap()
    for bn in ("bq", "bk", "bd"):
        io[bn] = nc.dram_tensor(bn, [128, 8], F32, kind="ExternalInput").ap()
    io["bv_row"] = nc.dram_tensor("bv_row", [1, D], F32,
                                  kind="ExternalInput").ap()
    io["ropec"] = nc.dram_tensor("ropec", [128, S], BF16,
                                 kind="ExternalInput").ap()
    io["ropes"] = nc.dram_tensor("ropes", [128, S], BF16,
                                 kind="ExternalInput").ap()
    C = new_ctx(nc, ab_elems=89600, af_elems=6160)
    A = build_common(C)
    emit_attn_phase(C, A, io, L0_CFG)
    C.replay()
    return nc


def phase1_inputs(xs, Wq, bq, Wk, bk, Wv, bv, Wd, bd):
    c, s = host_rope_tables()
    d = {"wq": np.ascontiguousarray(Wq), "wk": np.ascontiguousarray(Wk),
         "wv": np.ascontiguousarray(Wv), "wd": np.ascontiguousarray(Wd),
         "bq": np.ascontiguousarray(bq.reshape(8, 128).T),
         "bk": np.ascontiguousarray(bk.reshape(8, 128).T),
         "bd": np.ascontiguousarray(bd.reshape(8, 128).T),
         "bv_row": np.ascontiguousarray(bv.reshape(1, D)),
         "ropec": c, "ropes": s}
    for b in range(xs.shape[0]):
        d[f"x{b}"] = np.ascontiguousarray(xs[b])
    return d


def build_phase2_program(nb=BPC):
    nc = bass.Bass("TRN2", target_bir_lowering=False)
    io = {"x": _decl_stream(nc, "x", nb, "ExternalInput"),
          "y": _decl_stream(nc, "y", nb, "ExternalOutput")}
    io["wf"] = nc.dram_tensor("wf", [D, D], F32, kind="ExternalInput").ap()
    io["fscale"] = nc.dram_tensor("fscale", [128, 8], F32,
                                  kind="ExternalInput").ap()
    io["fbias"] = nc.dram_tensor("fbias", [128, 8], F32,
                                 kind="ExternalInput").ap()
    C = new_ctx(nc, ab_elems=76000, af_elems=6160)
    A = build_common(C)
    emit_ffn_phase(C, A, io)
    C.replay()
    return nc


def phase2_inputs(xs, Wf, scale, bias):
    d = {"wf": np.ascontiguousarray(Wf),
         "fscale": np.ascontiguousarray(scale.reshape(8, 128).T),
         "fbias": np.ascontiguousarray(bias.reshape(8, 128).T)}
    for b in range(xs.shape[0]):
        d[f"x{b}"] = np.ascontiguousarray(xs[b])
    return d


def build_phase3_program(nb=BPC):
    nc = bass.Bass("TRN2", target_bir_lowering=False)
    io = {"x": _decl_stream(nc, "x", nb, "ExternalInput"),
          "y": _decl_stream(nc, "y", nb, "ExternalOutput")}
    io["wqkv"] = nc.dram_tensor("wqkv", [3 * D, D], F32,
                                kind="ExternalInput").ap()
    io["wo"] = nc.dram_tensor("wo", [D, D], F32, kind="ExternalInput").ap()
    for bn in ("bq", "bk", "bd"):
        io[bn] = nc.dram_tensor(bn, [128, 8], F32, kind="ExternalInput").ap()
    io["bv_row"] = nc.dram_tensor("bv_row", [1, D], F32,
                                  kind="ExternalInput").ap()
    C = new_ctx(nc, ab_elems=89600, af_elems=6160)
    A = build_common(C)
    emit_attn_phase(C, A, io, L1_CFG)
    C.replay()
    return nc


def phase3_inputs(xs, w_qkv, b_qkv, w_o, b_o):
    d = {"wqkv": np.ascontiguousarray(w_qkv),
         "wo": np.ascontiguousarray(w_o),
         "bq": np.ascontiguousarray(b_qkv[0:1024].reshape(8, 128).T),
         "bk": np.ascontiguousarray(b_qkv[1024:2048].reshape(8, 128).T),
         "bd": np.ascontiguousarray(b_o.reshape(8, 128).T),
         "bv_row": np.ascontiguousarray(b_qkv[2048:3072].reshape(1, D))}
    for b in range(xs.shape[0]):
        d[f"x{b}"] = np.ascontiguousarray(xs[b])
    return d


def build_phase4_program(nb=BPC):
    nc = bass.Bass("TRN2", target_bir_lowering=False)
    io = {"x": _decl_stream(nc, "x", nb, "ExternalInput"),
          "y": _decl_stream(nc, "y", nb, "ExternalOutput")}
    io["wrt"] = nc.dram_tensor("wrt", [128, 64], F32, kind="ExternalInput").ap()
    for g in range(NG):
        io[f"wqT{g}"] = nc.dram_tensor(f"wqT{g}", [D, 512], F32,
                                       kind="ExternalInput").ap()
        io[f"wkvT{g}"] = nc.dram_tensor(f"wkvT{g}", [D, D], F32,
                                        kind="ExternalInput").ap()
        io[f"woutT{g}"] = nc.dram_tensor(f"woutT{g}", [512, D], F32,
                                         kind="ExternalInput").ap()
    io["nullk"] = nc.dram_tensor("nullk", [NG, 512], F32,
                                 kind="ExternalInput").ap()
    io["nullkT"] = nc.dram_tensor("nullkT", [128, 16], F32,
                                  kind="ExternalInput").ap()
    io["nullv"] = nc.dram_tensor("nullv", [NG, 512], F32,
                                 kind="ExternalInput").ap()
    io["ntok_row"] = nc.dram_tensor("ntok_row", [1, D], F32,
                                    kind="ExternalInput").ap()
    io["nqrow"] = nc.dram_tensor("nqrow", [32, 1], F32,
                                 kind="ExternalInput").ap()
    io["bsel"] = nc.dram_tensor("bsel", [32, 4], BF16,
                                kind="ExternalInput").ap()
    scratch = p4_scratch(nc, nb)
    C = new_ctx(nc, ab_elems=90500, af_elems=256)
    A = build_common(C)
    emit_phase4(C, A, io, scratch)
    C.replay()
    return nc


def p4_scratch(nc, nb):
    sc = {}
    sc["og"] = [[nc.dram_tensor(f"og{b}_{g}", [NQQ + 1, D], F32,
                                kind="Internal").ap()
                 for g in range(NG)] for b in range(nb)]
    sc["r_og"] = [[Res(f"og{b}_{g}") for g in range(NG)] for b in range(nb)]
    sc["slotsc"] = nc.dram_tensor("slotsc", [32, S], U32, kind="Internal").ap()
    sc["cntsd"] = nc.dram_tensor("cntsd", [4, S], F32, kind="Internal").ap()
    sc["r_sl"] = Res("slotsc_d")
    sc["r_cn"] = Res("cntsd_d")
    sc["tokd"] = nc.dram_tensor("tokd", [32, S], F32, kind="Internal").ap()
    sc["sgcd"] = nc.dram_tensor("sgcd", [32, S], F32, kind="Internal").ap()
    sc["r_lists"] = Res("lists")
    return sc


def phase4_inputs(xs, w_qr, w_kvr, w_q, w_kv, w_out, null_kv, null_token):
    nb = xs.shape[0]
    wrt = np.zeros((128, 64), np.float32)
    for dc in range(8):
        for gg in range(4):
            wrt[:, dc * 8 + gg] = w_qr[gg, dc * 128:(dc + 1) * 128]
            wrt[:, dc * 8 + 4 + gg] = w_kvr[gg, dc * 128:(dc + 1) * 128]
    nqrow = np.zeros((32, 1), np.float32)
    bsel = np.zeros((32, 4), np.float32)
    for b in range(4):
        for rr in range(8):
            nqrow[b * 8 + rr, 0] = NQQ if rr < 4 else NKK
            if rr < 4:
                bsel[b * 8 + rr, b] = 1.0
    d = {"wrt": wrt, "nqrow": nqrow,
         "bsel": bsel.astype(ml_dtypes.bfloat16),
         "nullk": np.ascontiguousarray(null_kv[0].reshape(NG, 512)),
         "nullkT": np.ascontiguousarray(
             null_kv[0].reshape(NG, 4, 128).transpose(2, 0, 1).reshape(
                 128, 16)),
         "nullv": np.ascontiguousarray(null_kv[1].reshape(NG, 512)),
         "ntok_row": np.ascontiguousarray(null_token.reshape(1, D))}
    for g in range(NG):
        d[f"wqT{g}"] = np.ascontiguousarray(w_q[g].T)
        d[f"wkvT{g}"] = np.ascontiguousarray(w_kv[g].T)
        d[f"woutT{g}"] = np.ascontiguousarray(w_out[g].T)
    for b in range(nb):
        d[f"x{b}"] = np.ascontiguousarray(xs[b])
    return d



_PROG_CACHE = {}


def _programs():
    if "p" not in _PROG_CACHE:
        _PROG_CACHE["p"] = (build_phase1_program(BPC), build_phase2_program(BPC),
                            build_phase3_program(BPC), build_phase4_program(BPC))
    return _PROG_CACHE["p"]


def kernel_multi(L0_attn_hidden_states, L0_attn_Wq, L0_attn_bq, L0_attn_Wk,
           L0_attn_bk, L0_attn_Wv, L0_attn_bv, L0_attn_Wd, L0_attn_bd,
           L0_ffn_weights, L0_ffn_scale, L0_ffn_bias,
           L1_attn_w_qkv, L1_attn_b_qkv, L1_attn_w_o, L1_attn_b_o,
           L1_ffn_w_qr, L1_ffn_w_kvr, L1_ffn_w_q, L1_ffn_w_kv, L1_ffn_w_out,
           L1_ffn_null_kv, L1_ffn_null_token):
    x = np.ascontiguousarray(np.asarray(L0_attn_hidden_states,
                                        dtype=np.float32))
    B = x.shape[0]
    ncores = B // BPC
    p1, p2, p3, p4 = _programs()

    def shard(arr):
        return [arr[c * BPC:(c + 1) * BPC] for c in range(ncores)]

    xs = shard(x)

    def run(prog, mk_inputs, xs_in):
        in_maps = [mk_inputs(xs_in[c]) for c in range(ncores)]
        res = run_bass_kernel_spmd(prog, in_maps, core_ids=list(range(ncores)))
        return [np.stack([np.asarray(res.results[c][f"y{b}"])
                          for b in range(BPC)]) for c in range(ncores)]

    ys = run(p1, lambda xc: phase1_inputs(
        xc, L0_attn_Wq, L0_attn_bq, L0_attn_Wk, L0_attn_bk,
        L0_attn_Wv, L0_attn_bv, L0_attn_Wd, L0_attn_bd), xs)
    ys = run(p2, lambda xc: phase2_inputs(
        xc, L0_ffn_weights, L0_ffn_scale, L0_ffn_bias), ys)
    ys = run(p3, lambda xc: phase3_inputs(
        xc, L1_attn_w_qkv, L1_attn_b_qkv, L1_attn_w_o, L1_attn_b_o), ys)
    ys = run(p4, lambda xc: phase4_inputs(
        xc, L1_ffn_w_qr, L1_ffn_w_kvr, L1_ffn_w_q, L1_ffn_w_kv,
        L1_ffn_w_out, L1_ffn_null_kv, L1_ffn_null_token), ys)
    return np.concatenate(ys, axis=0)



def build_fused_program(nb=BPC):
    nc = bass.Bass("TRN2", target_bir_lowering=False)
    xin = _decl_stream(nc, "x", nb, "ExternalInput")
    s1 = _decl_stream(nc, "s1_", nb, "Internal")
    s2 = _decl_stream(nc, "s2_", nb, "Internal")
    s3 = _decl_stream(nc, "s3_", nb, "Internal")
    yout = _decl_stream(nc, "y", nb, "ExternalOutput")

    io1 = {"x": xin, "y": s1}
    for w in ("wq", "wk", "wv", "wd"):
        io1[w] = nc.dram_tensor(w, [D, D], F32, kind="ExternalInput").ap()
    for bn in ("bq0", "bk0", "bd0"):
        io1[bn] = nc.dram_tensor(bn, [128, 8], F32, kind="ExternalInput").ap()
    io1["bv_row0"] = nc.dram_tensor("bv_row0", [1, D], F32,
                                    kind="ExternalInput").ap()
    io1["ropec"] = nc.dram_tensor("ropec", [128, S], BF16,
                                  kind="ExternalInput").ap()
    io1["ropes"] = nc.dram_tensor("ropes", [128, S], BF16,
                                  kind="ExternalInput").ap()

    io2 = {"x": s1, "y": s2}
    io2["wf"] = nc.dram_tensor("wf", [D, D], F32, kind="ExternalInput").ap()
    io2["fscale"] = nc.dram_tensor("fscale", [128, 8], F32,
                                   kind="ExternalInput").ap()
    io2["fbias"] = nc.dram_tensor("fbias", [128, 8], F32,
                                  kind="ExternalInput").ap()

    io3 = {"x": s2, "y": s3}
    io3["wqkv"] = nc.dram_tensor("wqkv", [3 * D, D], F32,
                                 kind="ExternalInput").ap()
    io3["wo"] = nc.dram_tensor("wo", [D, D], F32, kind="ExternalInput").ap()
    for bn in ("bq1", "bk1", "bd1"):
        io3[bn] = nc.dram_tensor(bn, [128, 8], F32, kind="ExternalInput").ap()
    io3["bv_row1"] = nc.dram_tensor("bv_row1", [1, D], F32,
                                    kind="ExternalInput").ap()

    io4 = {"x": s3, "y": yout}
    io4["wrt"] = nc.dram_tensor("wrt", [128, 64], F32,
                                kind="ExternalInput").ap()
    for g in range(NG):
        io4[f"wqT{g}"] = nc.dram_tensor(f"wqT{g}", [D, 512], F32,
                                        kind="ExternalInput").ap()
        io4[f"wkvT{g}"] = nc.dram_tensor(f"wkvT{g}", [D, D], F32,
                                         kind="ExternalInput").ap()
        io4[f"woutT{g}"] = nc.dram_tensor(f"woutT{g}", [512, D], F32,
                                          kind="ExternalInput").ap()
    io4["nullv"] = nc.dram_tensor("nullv", [NG, 512], F32,
                                  kind="ExternalInput").ap()
    io4["nullkT"] = nc.dram_tensor("nullkT", [128, 16], F32,
                                   kind="ExternalInput").ap()
    io4["ntok_row"] = nc.dram_tensor("ntok_row", [1, D], F32,
                                     kind="ExternalInput").ap()
    io4["nqrow"] = nc.dram_tensor("nqrow", [32, 1], F32,
                                  kind="ExternalInput").ap()
    io4["bsel"] = nc.dram_tensor("bsel", [32, 4], BF16,
                                 kind="ExternalInput").ap()
    scratch = p4_scratch(nc, nb)

    C = new_ctx(nc, ab_elems=89600, af_elems=6160)
    A = build_common(C)
    cfg0 = dict(L0_CFG)
    cfg0.update(uniq="0", bq="bq0", bk="bk0", bd="bd0", bv_row="bv_row0")
    cfg1 = dict(L1_CFG)
    cfg1.update(uniq="1", bq="bq1", bk="bk1", bd="bd1", bv_row="bv_row1")

    for emitfn in (lambda: emit_attn_phase(C, A, io1, cfg0),
                   lambda: emit_ffn_phase(C, A, io2),
                   lambda: emit_attn_phase(C, A, io3, cfg1),
                   lambda: emit_phase4(C, A, io4, scratch)):
        C.ab.mark()
        C.af.mark()
        emitfn()
        C.barrier()
        C.ab.release()
        C.af.release()
    C.replay()
    return nc


def fused_inputs(xs, i):
    d = {}
    p1 = phase1_inputs(xs, i["L0_attn_Wq"], i["L0_attn_bq"], i["L0_attn_Wk"],
                       i["L0_attn_bk"], i["L0_attn_Wv"], i["L0_attn_bv"],
                       i["L0_attn_Wd"], i["L0_attn_bd"])
    for k, v in p1.items():
        d[{"bq": "bq0", "bk": "bk0", "bd": "bd0",
           "bv_row": "bv_row0"}.get(k, k)] = v
    p2 = phase2_inputs(xs, i["L0_ffn_weights"], i["L0_ffn_scale"],
                       i["L0_ffn_bias"])
    for k, v in p2.items():
        if not k.startswith("x"):
            d[k] = v
    p3 = phase3_inputs(xs, i["L1_attn_w_qkv"], i["L1_attn_b_qkv"],
                       i["L1_attn_w_o"], i["L1_attn_b_o"])
    for k, v in p3.items():
        if not k.startswith("x"):
            d[{"bq": "bq1", "bk": "bk1", "bd": "bd1",
               "bv_row": "bv_row1"}.get(k, k)] = v
    p4 = phase4_inputs(xs, i["L1_ffn_w_qr"], i["L1_ffn_w_kvr"],
                       i["L1_ffn_w_q"], i["L1_ffn_w_kv"], i["L1_ffn_w_out"],
                       i["L1_ffn_null_kv"], i["L1_ffn_null_token"])
    for k, v in p4.items():
        if not (k.startswith("x") and k[1:].isdigit()) and k != "nullk":
            d[k] = v
    return d


def kernel(**inputs):
    x = np.ascontiguousarray(np.asarray(inputs["L0_attn_hidden_states"],
                                        dtype=np.float32))
    B = x.shape[0]
    ncores = B // BPC
    if "f" not in _PROG_CACHE:
        _PROG_CACHE["f"] = build_fused_program(BPC)
    prog = _PROG_CACHE["f"]
    in_maps = [fused_inputs(x[c * BPC:(c + 1) * BPC], inputs)
               for c in range(ncores)]
    res = run_bass_kernel_spmd(prog, in_maps, core_ids=list(range(ncores)))
    return np.concatenate(
        [np.stack([np.asarray(res.results[c][f"y{b}"]) for b in range(BPC)])
         for c in range(ncores)], axis=0)
